# version 09 (final)
# baseline (speedup 1.0000x reference)
import contextlib
import numpy as np
import sys
for _p in ("/opt/trn_rl_repo",):
    if _p not in sys.path:
        sys.path.append(_p)
import concourse.bass as bass
import concourse.mybir as mybir
from concourse.bass_utils import run_bass_kernel_spmd
from concourse.aot_env import allow_ici_all_to_all

F32 = mybir.dt.float32
BF16 = mybir.dt.bfloat16
ALU = mybir.AluOpType
AF = mybir.ActivationFunctionType
AX = mybir.AxisListType.X
RG = [[0, 1, 2, 3, 4, 5, 6, 7]]

D = 1024
KT = 8
CT = 256
FT = 512
DFF = 4096
NEXP = 8
NB = 8


def build_program(N=4096, dbg=False):
    CH = N // CT
    NG = N // FT
    nc = bass.Bass("TRN2", target_bir_lowering=False)

    x_in = nc.dram_tensor("x", [CH * 128, 2 * D], F32, kind="ExternalInput").ap()
    xlast_in = nc.dram_tensor("xlast", [1, D], F32, kind="ExternalInput").ap()
    W_in, b_in, m_in = {}, {}, {}
    for wi, nm in enumerate("grkvo"):
        W_in[wi] = nc.dram_tensor(f"W{nm}", [D, D], F32, kind="ExternalInput").ap()
        b_in[wi] = nc.dram_tensor(f"b{nm}", [128, KT], F32, kind="ExternalInput").ap()
    for wi, nm in enumerate("grkv"):
        m_in[wi] = nc.dram_tensor(f"m{nm}", [128, KT], F32, kind="ExternalInput").ap()
    wdec_in = nc.dram_tensor("wdec", [128, KT], F32, kind="ExternalInput").ap()
    u_in = nc.dram_tensor("uvec", [128, KT], F32, kind="ExternalInput").ap()
    gate_in = nc.dram_tensor("gate", [128, KT, NEXP], F32, kind="ExternalInput").ap()
    w1_in = nc.dram_tensor("w1e", [32 * 128, D], F32, kind="ExternalInput").ap()
    b1_in = nc.dram_tensor("b1e", [128, 32], F32, kind="ExternalInput").ap()
    w2_in = nc.dram_tensor("w2e", [DFF, D], F32, kind="ExternalInput").ap()
    b2_in = nc.dram_tensor("b2e", [D], F32, kind="ExternalInput").ap()
    out_t = nc.dram_tensor("out", [N, D], F32, kind="ExternalOutput").ap()
    if dbg:
        dxmid = nc.dram_tensor("dxmid", [N, D], F32, kind="ExternalOutput").ap()
        dh2 = nc.dram_tensor("dh2", [D, N], F32, kind="ExternalOutput").ap()
        dscale = nc.dram_tensor("dscale", [NEXP, N], F32, kind="ExternalOutput").ap()
        dyloc = nc.dram_tensor("dyloc", [NB * N, D], F32, kind="ExternalOutput").ap()
        dyred = nc.dram_tensor("dyred", [N, D], F32, kind="ExternalOutput").ap()
        dw1b = nc.dram_tensor("dw1b", [128, KT * DFF], BF16, kind="ExternalOutput").ap()
        dw2b = nc.dram_tensor("dw2b", [128, 32 * D], BF16, kind="ExternalOutput").ap()
        db1 = nc.dram_tensor("db1", [128, 32], F32, kind="ExternalOutput").ap()
        db2 = nc.dram_tensor("db2", [1, D], BF16, kind="ExternalOutput").ap()
        dhsub = nc.dram_tensor("dhsub", [128, 32 * FT], BF16, kind="ExternalOutput").ap()

    x_mid = nc.dram_tensor("x_mid", [N, D], F32, kind="Internal").ap()
    h2_loc = nc.dram_tensor("h2_loc", [NG * D, FT], BF16, kind="Internal").ap()
    h2_all = nc.dram_tensor("h2_all", [NB * NG * D, FT], BF16, kind="Internal",
                            addr_space="Local").ap()
    sc_loc = nc.dram_tensor("sc_loc", [NEXP, N], F32, kind="Internal").ap()
    sc_c = nc.dram_tensor("sc_c", [NB, N], F32, kind="Internal",
                          addr_space="Local").ap()
    y_loc = nc.dram_tensor("y_loc", [NB * N, D], F32, kind="Internal").ap()
    y_red = nc.dram_tensor("y_red", [N, D], F32, kind="Internal",
                           addr_space="Local").ap()

    C = {}

    def mk(instr, sem, key, n=None):
        if n is None:
            n = 16 if key in ("qs", "qg", "so", "nr") else 1
        instr.then_inc(sem, n)
        C[key] = C.get(key, 0) + n
        return C[key]

    es = contextlib.ExitStack()
    with es:
        pb = [es.enter_context(nc.psum_tensor(f"pb{i}", [128, 512], F32)) for i in range(8)]
        epsv = es.enter_context(nc.sbuf_tensor("epsv", [128, 1], F32))
        ident = es.enter_context(nc.sbuf_tensor("ident", [128, 128], F32))
        sQS = es.enter_context(nc.semaphore("qs"))
        sQG = es.enter_context(nc.semaphore("qg"))
        sT = es.enter_context(nc.semaphore("t"))
        sV = es.enter_context(nc.semaphore("v"))
        sS = es.enter_context(nc.semaphore("s"))
        sG = es.enter_context(nc.semaphore("g"))
        sCC = es.enter_context(nc.semaphore("cc"))
        sSO = es.enter_context(nc.semaphore("so"))
        sNR = es.enter_context(nc.semaphore("nr"))
        SEM = {"v": sV, "s": sS, "t": sT, "g": sG, "qs": sQS, "qg": sQG,
               "cc": sCC, "so": sSO, "nr": sNR}

        aes = contextlib.ExitStack()
        TA = lambda name, shape, dt: aes.enter_context(nc.sbuf_tensor(name, shape, dt))
        WT = TA("WT", [128, 5, KT, KT, 128], BF16)
        bias_sb = TA("bias_sb", [128, 5, KT], F32)
        mvec = TA("mvec", [128, 4, KT], F32)
        uvec = TA("uvec_sb", [128, KT], F32)
        e1v = TA("e1v", [128, KT], F32)
        ngv = TA("ngv", [128, KT], F32)
        decv = TA("decv", [128, KT], F32)
        Gmat = TA("Gmat", [128, KT, 16], F32)
        Emat = TA("Emat", [16, KT, 128], F32)
        gate_sb = TA("gate_sb", [128, KT, NEXP], F32)
        h_bnd = TA("h_bnd", [128, KT, 1], F32)
        carry_s = TA("carry_s", [128, KT, 1], F32)
        ses = contextlib.ExitStack()
        wstg = ses.enter_context(nc.sbuf_tensor("wstg", [128, 2, D], F32))
        xrow = ses.enter_context(nc.sbuf_tensor("xrow", [1, D], F32))
        rsm = ses.enter_context(nc.sbuf_tensor("rsm", [1, 12], F32))
        Gp = ses.enter_context(nc.sbuf_tensor("Gp", [128, KT, 16], F32))
        rdmp = ses.enter_context(nc.sbuf_tensor("rdmp", [1, D], F32))

        block = es.enter_context(nc.Block())

        bank_mark = {i: ("v", 0) for i in range(8)}

        def bank_wait(eng, i):
            nm, val = bank_mark[i]
            if val > 0:
                eng.wait_ge(SEM[nm], val)

        @block.gpsimd
        def _(g):
            mk(g.memset(epsv[:], 1e-5), sG, "g")
            mk(g.memset(ident[:], 0.0), sG, "g")
            g.wait_ge(sG, C["g"])
            mk(g.affine_select(out=ident[:], in_=ident[:], compare_op=ALU.not_equal,
                               fill=1.0, base=0, pattern=[[-1, 128]],
                               channel_multiplier=1), sG, "g")
            mk(g.memset(Gmat[:], 0.0), sG, "g")
            mk(g.memset(Gp[:], 0.0), sG, "g")
            g.wait_ge(sG, C["g"])
            for kt in range(KT):
                mk(g.memset(Gmat[0:64, kt, 2 * kt:2 * kt + 1], 1.0 / 64), sG, "g")
                mk(g.memset(Gmat[64:128, kt, 2 * kt + 1:2 * kt + 2], 1.0 / 64),
                   sG, "g")
                mk(g.memset(Gp[0:64, kt, 2 * kt:2 * kt + 1], 1.0), sG, "g")
            for kt in range(KT):
                mk(g.memset(Gp[64:128, kt, 2 * kt + 1:2 * kt + 2], 1.0), sG, "g")
        m_setup_g = C["g"]

        @block.sync
        def _(q):
            for wi in range(5):
                mk(q.dma_start(out=bias_sb[:, wi, :], in_=b_in[wi]), sQS, "qs")
            for wi in range(4):
                mk(q.dma_start(out=mvec[:, wi, :], in_=m_in[wi]), sQS, "qs")
            mk(q.dma_start(out=uvec[:], in_=u_in), sQS, "qs")
            mk(q.dma_start(out=e1v[:], in_=wdec_in), sQS, "qs")
            mk(q.dma_start(out=gate_sb[:], in_=gate_in), sQS, "qs")
            mk(q.dma_start(out=xrow[:], in_=xlast_in), sNR, "nr")
        m_vecs = C["qs"]
        m_xrow = C["nr"]

        @block.scalar
        def _(sc):
            sc.wait_ge(sQS, m_vecs)
            mk(sc.activation(out=decv[:], in_=e1v[:], func=AF.Exp), sS, "s")

        @block.vector
        def _(v):
            v.wait_ge(sS, C["s"])
            mk(v.tensor_scalar(out=ngv[:], in0=decv[:], scalar1=-1.0, scalar2=None,
                               op0=ALU.mult), sV, "v")

        @block.scalar
        def _(sc):
            sc.wait_ge(sV, C["v"])
            mk(sc.activation(out=decv[:], in_=ngv[:], func=AF.Exp), sS, "s")
        m_dec = C["s"]

        m_decrep = m_dec

        for kt in range(KT):
            bk = kt % 2
            @block.tensor
            def _(t, kt=kt, bk=bk):
                t.wait_ge(sG, m_setup_g)
                bank_wait(t, bk)
                mk(t.transpose(out=pb[bk][0:16, 0:128], in_=Gp[:, kt, :],
                               identity=ident[:]), sT, "t")

            @block.vector
            def _(v, kt=kt, bk=bk):
                v.wait_ge(sT, C["t"])
                m = mk(v.tensor_copy(out=Emat[:, kt, :], in_=pb[bk][0:16, 0:128]),
                       sV, "v")
                bank_mark[bk] = ("v", m)

        wstg_guard = {}

        @block.sync
        def _(q):
            q.wait_ge(sQS, m_vecs)
            mk(q.dma_start(out=wstg[:, 0, :], in_=W_in[0][0:128, :]), sQS, "qs")
        m_wstg0 = C["qs"]

        for idx in range(5 * KT):
            wi, mt = divmod(idx, KT)
            @block.sync
            def _(q, idx=idx):
                if idx + 1 < 5 * KT:
                    if idx >= 1:
                        q.wait_ge(sT, wstg_guard[idx - 1])
                    else:
                        q.wait_ge(sQS, m_wstg0)
                    wi2, mt2 = divmod(idx + 1, KT)
                    mk(q.dma_start(out=wstg[:, (idx + 1) % 2, :],
                                   in_=W_in[wi2][mt2 * 128:(mt2 + 1) * 128, :]),
                       sQS, "qs")
                else:
                    q.wait_ge(sT, wstg_guard[idx - 1])
                    mk(q.dma_start(out=wstg[:, (idx + 1) % 2, :],
                                   in_=W_in[0][0:128, :]), sQS, "qs")
            m_wdma = C["qs"]
            for kt2 in range(KT):
                bk = 2 + kt2 % 2
                @block.tensor
                def _(t, wi=wi, mt=mt, idx=idx, kt2=kt2, bk=bk, m_wdma=m_wdma):
                    if kt2 == 0:
                        t.wait_ge(sQS, m_wdma)
                    bank_wait(t, bk)
                    m = mk(t.transpose(out=pb[bk][0:128, 0:128],
                                       in_=wstg[:, idx % 2, kt2 * 128:(kt2 + 1) * 128],
                                       identity=ident[:]), sT, "t")
                    if kt2 == KT - 1:
                        wstg_guard[idx] = m

                @block.vector
                def _(v, wi=wi, mt=mt, kt2=kt2, bk=bk):
                    v.wait_ge(sT, C["t"])
                    m = mk(v.tensor_copy(out=WT[:, wi, kt2, mt, :],
                                         in_=pb[bk][0:128, 0:128]), sV, "v")
                    bank_mark[bk] = ("v", m)
        m_wt = C["v"]

        @block.vector
        def _(v):
            v.wait_ge(sQS, m_vecs)
            v.wait_ge(sNR, m_xrow)
            mk(v.tensor_reduce(out=rsm[:, 0:1], in_=xrow[:], axis=AX, op=ALU.add),
               sV, "v")

        @block.scalar
        def _(sc):
            sc.wait_ge(sV, C["v"])
            mk(sc.mul(rsm[:, 1:2], rsm[:, 0:1], 1.0 / D), sS, "s")
            mk(sc.activation(out=rdmp[:], in_=xrow[:], func=AF.Square,
                             accum_out=rsm[:, 2:3]), sS, "s")

        @block.vector
        def _(v):
            v.wait_ge(sS, C["s"])
            v.tensor_tensor(out=rsm[:, 3:4], in0=rsm[:, 1:2], in1=rsm[:, 1:2], op=ALU.mult)
            mk(v.tensor_scalar(out=rsm[:, 4:5], in0=rsm[:, 2:3], scalar1=1.0 / D,
                               scalar2=None, op0=ALU.mult), sV, "v")

        @block.gpsimd
        def _(g):
            g.wait_ge(sV, C["v"])
            mk(g.tensor_tensor(out=rsm[:, 5:6], in0=rsm[:, 4:5], in1=rsm[:, 3:4],
                               op=ALU.subtract), sG, "g")

        @block.scalar
        def _(sc):
            sc.wait_ge(sG, C["g"])
            mk(sc.activation(out=rsm[:, 6:7], in_=rsm[:, 5:6], func=AF.Ln,
                             bias=epsv[0:1, 0:1]), sS, "s")

        @block.vector
        def _(v):
            v.wait_ge(sS, C["s"])
            mk(v.tensor_scalar(out=rsm[:, 7:8], in0=rsm[:, 6:7], scalar1=-0.5,
                               scalar2=None, op0=ALU.mult), sV, "v")

        @block.scalar
        def _(sc):
            sc.wait_ge(sV, C["v"])
            mk(sc.activation(out=rsm[:, 8:9], in_=rsm[:, 7:8], func=AF.Exp), sS, "s")

        @block.vector
        def _(v):
            v.wait_ge(sS, C["s"])
            mk(v.tensor_scalar(out=rdmp[:], in0=xrow[:], scalar1=rsm[:, 1:2],
                               scalar2=rsm[:, 8:9], op0=ALU.subtract, op1=ALU.mult),
               sV, "v")
        m_rowln = C["v"]

        for kt in range(KT):
            bk = 4 + kt % 2
            @block.tensor
            def _(t, kt=kt, bk=bk):
                t.wait_ge(sV, m_rowln)
                bank_wait(t, bk)
                mk(t.transpose(out=pb[bk][0:128, kt:kt + 1],
                               in_=rdmp[0:1, kt * 128:(kt + 1) * 128],
                               identity=ident[0:1, 0:1]), sT, "t")

            @block.vector
            def _(v, kt=kt, bk=bk):
                v.wait_ge(sT, C["t"])
                m = mk(v.tensor_copy(out=h_bnd[:, kt, :], in_=pb[bk][0:128, kt:kt + 1]),
                       sV, "v")
                bank_mark[bk] = ("v", m)
        m_hbnd_init = ("v", C["v"])
        m_setup_done_v = C["v"]

        ses.close()
        x_tm = [TA(f"x_tm{i}", [128, 2, D], F32) for i in range(2)]
        xn = TA("xn", [128, 2, D], F32)
        xm_tm = TA("xm_tm", [128, 2, D], F32)
        dmp = TA("dmp", [128, 2, D], F32)
        sm = TA("sm", [128, 2, 24], F32)
        h_ext = TA("h_ext", [128, KT, CT + 1], F32)
        diffb = TA("diffb", [128, KT, CT], BF16)
        mixb = [TA(f"mixb{i}", [128, KT, CT], BF16) for i in range(2)]
        sgb = TA("sgb", [128, KT, CT], BF16)
        rb = TA("rb", [128, KT, CT], F32)
        kb = TA("kb", [128, KT, CT], F32)
        kvb = TA("kvb", [128, KT, CT], F32)
        wkvb = TA("wkvb", [128, KT, CT], F32)
        sqb = TA("sqb", [128, KT, CT], F32)
        tmp1 = TA("tmp1", [128, CT], F32)
        h2f = TA("h2f", [128, KT, CT], F32)
        mean_sb = TA("mean_sb", [16, CT], F32)
        msq_sb = TA("msq_sb", [16, CT], F32)
        var16 = TA("var16", [16, CT], F32)
        l16 = TA("l16", [16, CT], F32)
        r16 = var16
        ltm_s = TA("ltm_s", [128, 2, 8], F32)
        r_sm = TA("r_sm", [128, 2, 6], F32)
        edmp = TA("edmp", [128, 2, 8], F32)
        scale_tm = TA("scale_tm", [128, 2, 8], F32)
        scaleT = TA("scaleT", [8, CT], F32)
        lgs = scaleT

        @block.gpsimd
        def _(g):
            g.wait_ge(sV, m_setup_done_v)
            mk(g.dma_start(out=x_tm[0][:].rearrange("p a b -> p (a b)"),
                           in_=x_in[0:128, :]), sQG, "qg")

        m_x = {0: C["qg"]}
        m_xlast = {}
        m_carry = {}
        m_hbnd = {-1: m_hbnd_init}
        m_h2dma = {}
        m_scdma = {}
        m_xmiddma = {}
        m_kvd = {}

        def emit_ln(src, dst, slot, c, extra_wait_v=None, extra_wait_qs=None):
            @block.vector
            def _(v):
                v.wait_ge(sQG, m_x[c])
                if extra_wait_v is not None:
                    v.wait_ge(sV, extra_wait_v)
                if extra_wait_qs is not None:
                    v.wait_ge(sQS, extra_wait_qs)
                for tt in range(2):
                    mk(v.tensor_reduce(out=sm[:, tt, slot:slot + 1],
                                       in_=src[:, tt, :], axis=AX, op=ALU.add), sV, "v")

            @block.scalar
            def _(sc):
                sc.wait_ge(sV, C["v"])
                sc.wait_ge(sQG, m_x[c])
                for tt in range(2):
                    mk(sc.mul(sm[:, tt, slot + 1:slot + 2],
                              sm[:, tt, slot:slot + 1], 1.0 / D), sS, "s")
                    mk(sc.activation(out=dmp[:, tt, :], in_=src[:, tt, :],
                                     func=AF.Square,
                                     accum_out=sm[:, tt, slot + 2:slot + 3]), sS, "s")

            @block.vector
            def _(v):
                v.wait_ge(sS, C["s"])
                for tt in range(2):
                    v.tensor_tensor(out=sm[:, tt, slot + 3:slot + 4],
                                    in0=sm[:, tt, slot + 1:slot + 2],
                                    in1=sm[:, tt, slot + 1:slot + 2], op=ALU.mult)
                    mk(v.tensor_scalar(out=sm[:, tt, slot + 4:slot + 5],
                                       in0=sm[:, tt, slot + 2:slot + 3],
                                       scalar1=1.0 / D, scalar2=None, op0=ALU.mult),
                       sV, "v")

            @block.gpsimd
            def _(g):
                g.wait_ge(sV, C["v"])
                for tt in range(2):
                    mk(g.tensor_tensor(out=sm[:, tt, slot + 5:slot + 6],
                                       in0=sm[:, tt, slot + 4:slot + 5],
                                       in1=sm[:, tt, slot + 3:slot + 4],
                                       op=ALU.subtract), sG, "g")

            @block.scalar
            def _(sc):
                sc.wait_ge(sG, C["g"])
                for tt in range(2):
                    mk(sc.activation(out=sm[:, tt, slot + 6:slot + 7],
                                     in_=sm[:, tt, slot + 5:slot + 6],
                                     func=AF.Ln, bias=epsv[:]), sS, "s")

            @block.vector
            def _(v):
                v.wait_ge(sS, C["s"])
                for tt in range(2):
                    mk(v.tensor_scalar(out=sm[:, tt, slot + 7:slot + 8],
                                       in0=sm[:, tt, slot + 6:slot + 7],
                                       scalar1=-0.5, scalar2=None, op0=ALU.mult), sV, "v")

            @block.scalar
            def _(sc):
                sc.wait_ge(sV, C["v"])
                for tt in range(2):
                    mk(sc.activation(out=sm[:, tt, slot + 8:slot + 9],
                                     in_=sm[:, tt, slot + 7:slot + 8],
                                     func=AF.Exp), sS, "s")

            @block.vector
            def _(v):
                v.wait_ge(sS, C["s"])
                for tt in range(2):
                    mk(v.tensor_scalar(out=dst[:, tt, :], in0=src[:, tt, :],
                                       scalar1=sm[:, tt, slot + 1:slot + 2],
                                       scalar2=sm[:, tt, slot + 8:slot + 9],
                                       op0=ALU.subtract, op1=ALU.mult), sV, "v")

        m_ln1d = {}
        emit_ln(x_tm[0], xn, 0, 0)
        m_ln1d[0] = C["v"]

        for c in range(CH):
            t0 = c * CT
            if c + 1 < CH:
                @block.gpsimd
                def _(g, c=c):
                    if c >= 1:
                        g.wait_ge(sV, m_xlast.get(c - 1, 0))
                    mk(g.dma_start(
                        out=x_tm[(c + 1) % 2][:].rearrange("p a b -> p (a b)"),
                        in_=x_in[(c + 1) * 128:(c + 2) * 128, :]), sQG, "qg")
                    m_x[c + 1] = C["qg"]

            xcur = x_tm[c % 2]
            m_ln1 = m_ln1d[c]

            @block.gpsimd
            def _(g, c=c):
                nm, val = m_hbnd[c - 1]
                g.wait_ge(SEM[nm], val)
                if c > 0:
                    g.wait_ge(sV, m_kvd[c - 1])
                mk(g.tensor_copy(out=h_ext[:, :, 0:1], in_=h_bnd[:]), sG, "g")
            m_bcol = C["g"]

            base_t = C["t"]
            base_v = C["v"]
            @block.tensor
            def _(t, base_v=base_v):
                t.wait_ge(sV, m_ln1)
                for i in range(16):
                    bk = 4 + i % 2
                    if i < 2:
                        bank_wait(t, bk)
                    else:
                        t.wait_ge(sV, base_v + (i - 2) + 1)
                    tt, kt = divmod(i, KT)
                    mk(t.transpose(out=pb[bk][0:128, 0:128],
                                   in_=xn[:, tt, kt * 128:(kt + 1) * 128],
                                   identity=ident[:]), sT, "t")

            @block.vector
            def _(v, c=c, base_t=base_t):
                if c > 0:
                    nm, val = m_hbnd[c - 1]
                    v.wait_ge(SEM[nm], val)
                for i in range(16):
                    bk = 4 + i % 2
                    tt, kt = divmod(i, KT)
                    v.wait_ge(sT, base_t + i + 1)
                    m = mk(v.tensor_copy(
                        out=h_ext[:, kt, 1 + tt * 128:1 + (tt + 1) * 128],
                        in_=pb[bk][0:128, 0:128]), sV, "v")
                    bank_mark[bk] = ("v", m)
            m_hx = C["v"]

            @block.scalar
            def _(sc, c=c, m_bcol=m_bcol):
                sc.wait_ge(sV, m_hx)
                sc.wait_ge(sG, m_bcol)
                m_hbnd[c] = ("s", mk(sc.copy(h_bnd[:], h_ext[:, :, CT:CT + 1]), sS, "s"))

            @block.vector
            def _(v, c=c, m_bcol=m_bcol, m_hx=m_hx):
                v.wait_ge(sG, m_bcol)
                v.wait_ge(sV, m_hx)
                if c == 0:
                    v.wait_ge(sV, m_wt)
                else:
                    v.wait_ge(sQS, m_h2dma[c - 1])
                mk(v.tensor_tensor(out=sqb[:], in0=h_ext[:, :, 1:CT + 1],
                                   in1=h_ext[:, :, 0:CT], op=ALU.subtract), sV, "v")
            m_diff = C["v"]

            m_mixmm = {}
            for wi in range(4):
                @block.vector
                def _(v, wi=wi):
                    v.wait_ge(sV, m_diff)
                    if wi >= 2:
                        v.wait_ge(sT, m_mixmm[wi - 2])
                    last = None
                    for kt in range(KT):
                        last = v.scalar_tensor_tensor(
                            out=mixb[wi % 2][:, kt, :], in0=sqb[:, kt, :],
                            scalar=mvec[:, wi, kt:kt + 1], in1=h_ext[:, kt, 0:CT],
                            op0=ALU.mult, op1=ALU.add)
                    mk(last, sV, "v")
                m_mix = C["v"]

                base_tm = C["t"]
                base_ev = C["s"] if wi < 3 else C["v"]
                @block.tensor
                def _(t, wi=wi, m_mix=m_mix, base_ev=base_ev):
                    t.wait_ge(sV, m_mix)
                    for mt in range(KT):
                        bk = mt % 4
                        if mt < 4:
                            bank_wait(t, bk)
                        else:
                            t.wait_ge(sS if wi < 3 else sV, base_ev + (mt - 4) + 1)
                        last = None
                        for kt in range(KT):
                            last = t.matmul(out=pb[bk][0:128, 0:CT],
                                            lhsT=WT[:, wi, kt, mt, :],
                                            rhs=mixb[wi % 2][:, kt, :],
                                            start=(kt == 0), stop=(kt == KT - 1))
                        mk(last, sT, "t")
                m_mixmm[wi] = C["t"]

                if wi < 3:
                    func = AF.Sigmoid if wi == 0 else AF.Identity
                    dstb = (sgb, rb, kb)[wi]
                    @block.scalar
                    def _(sc, wi=wi, func=func, dstb=dstb, base_tm=base_tm):
                        for mt in range(KT):
                            bk = mt % 4
                            sc.wait_ge(sT, base_tm + mt + 1)
                            m = mk(sc.activation(out=dstb[:, mt, :],
                                                 in_=pb[bk][0:128, 0:CT], func=func,
                                                 bias=bias_sb[:, wi, mt:mt + 1]), sS, "s")
                            bank_mark[bk] = ("s", m)
                else:
                    @block.vector
                    def _(v, base_tm=base_tm, kevm=C["s"]):
                        v.wait_ge(sS, kevm)
                        for mt in range(KT):
                            bk = mt % 4
                            v.wait_ge(sT, base_tm + mt + 1)
                            m = mk(v.scalar_tensor_tensor(
                                out=kvb[:, mt, :], in0=pb[bk][0:128, 0:CT],
                                scalar=bias_sb[:, 3, mt:mt + 1], in1=kb[:, mt, :],
                                op0=ALU.add, op1=ALU.mult), sV, "v")
                            bank_mark[bk] = ("v", m)
            m_kv = C["v"]
            m_kvd[c] = m_kv

            base_sc = C["v"]
            @block.vector
            def _(v, c=c):
                v.wait_ge(sV, m_kv)
                if c == 0:
                    v.wait_ge(sS, m_decrep)
                else:
                    nm, val = m_carry[c - 1]
                    v.wait_ge(SEM[nm], val)
                for kt in range(KT):
                    mk(v.tensor_tensor_scan(
                        out=wkvb[:, kt, :],
                        data0=decv[:, kt:kt + 1].broadcast_to([128, CT]),
                        data1=kvb[:, kt, :],
                        initial=(0.0 if c == 0 else carry_s[:, kt, 0:1]),
                        op0=ALU.mult, op1=ALU.add), sV, "v")
                    mk(v.scalar_tensor_tensor(
                        out=sqb[:, kt, :], in0=kvb[:, kt, :],
                        scalar=uvec[:, kt:kt + 1], in1=wkvb[:, kt, :],
                        op0=ALU.mult, op1=ALU.add), sV, "v")
                    mk(v.tensor_tensor(out=kvb[:, kt, :], in0=sqb[:, kt, :],
                                       in1=rb[:, kt, :], op=ALU.mult), sV, "v")
            m_rw = C["v"]

            base_sq = C["s"]
            @block.scalar
            def _(sc, base_sc=base_sc):
                for kt in range(KT):
                    sc.wait_ge(sV, base_sc + 3 * kt + 3)
                    mk(sc.activation(out=sqb[:, kt, :], in_=kvb[:, kt, :],
                                     func=AF.Square), sS, "s")
            m_sq = C["s"]

            @block.scalar
            def _(sc, c=c, m_rw=m_rw):
                sc.wait_ge(sV, m_rw)
                m_carry[c] = ("s", mk(sc.copy(carry_s[:], wkvb[:, :, CT - 1:CT]), sS, "s"))

            @block.tensor
            def _(t, base_sq=base_sq):
                bank_wait(t, 6)
                bank_wait(t, 7)
                last = None
                for kt in range(KT):
                    t.wait_ge(sS, base_sq + kt + 1)
                    t.matmul(out=pb[6][0:16, 0:CT], lhsT=Gmat[:, kt, :],
                             rhs=kvb[:, kt, :], start=(kt == 0), stop=(kt == KT - 1))
                    last = t.matmul(out=pb[7][0:16, 0:CT], lhsT=Gmat[:, kt, :],
                                    rhs=sqb[:, kt, :], start=(kt == 0), stop=(kt == KT - 1))
                mk(last, sT, "t")
            m_stmm = C["t"]

            @block.vector
            def _(v):
                v.wait_ge(sT, m_stmm)
                v.tensor_copy(out=mean_sb[:], in_=pb[6][0:16, 0:CT])
                m = mk(v.tensor_copy(out=msq_sb[:], in_=pb[7][0:16, 0:CT]), sV, "v")
                bank_mark[6] = ("v", m)
                bank_mark[7] = ("v", m)
                v.tensor_tensor(out=var16[:], in0=mean_sb[:], in1=mean_sb[:], op=ALU.mult)
                mk(v.tensor_tensor(out=var16[:], in0=msq_sb[:], in1=var16[:],
                                   op=ALU.subtract), sV, "v")

            @block.scalar
            def _(sc):
                sc.wait_ge(sV, C["v"])
                mk(sc.activation(out=l16[:], in_=var16[:], func=AF.Ln,
                                 bias=epsv[0:16, 0:1]), sS, "s")

            @block.vector
            def _(v):
                v.wait_ge(sS, C["s"])
                mk(v.tensor_scalar(out=l16[:], in0=l16[:], scalar1=-0.5, scalar2=None,
                                   op0=ALU.mult), sV, "v")

            @block.scalar
            def _(sc):
                sc.wait_ge(sV, C["v"])
                mk(sc.activation(out=r16[:], in_=l16[:], func=AF.Exp), sS, "s")
            m_r16 = C["s"]

            base_tb = C["t"]
            base_vb = C["v"]
            @block.tensor
            def _(t, base_vb=base_vb):
                t.wait_ge(sS, m_r16)
                for kt in range(KT):
                    bkA = 4 + (kt % 2)
                    bkB = 6 + (kt % 2)
                    if kt < 2:
                        bank_wait(t, bkA)
                        bank_wait(t, bkB)
                    else:
                        t.wait_ge(sV, base_vb + (kt - 2) + 1)
                    t.matmul(out=pb[bkA][0:128, 0:CT], lhsT=Emat[:, kt, :],
                             rhs=mean_sb[:], start=True, stop=True)
                    mk(t.matmul(out=pb[bkB][0:128, 0:CT], lhsT=Emat[:, kt, :],
                                rhs=r16[:], start=True, stop=True), sT, "t")

            @block.vector
            def _(v, base_tb=base_tb):
                for kt in range(KT):
                    bkA = 4 + (kt % 2)
                    bkB = 6 + (kt % 2)
                    v.wait_ge(sT, base_tb + kt + 1)
                    v.tensor_tensor(out=tmp1[:], in0=kvb[:, kt, :],
                                    in1=pb[bkA][0:128, 0:CT], op=ALU.subtract)
                    v.tensor_tensor(out=tmp1[:], in0=tmp1[:],
                                    in1=pb[bkB][0:128, 0:CT], op=ALU.mult)
                    m = mk(v.tensor_tensor(out=diffb[:, kt, :], in0=tmp1[:],
                                           in1=sgb[:, kt, :], op=ALU.mult), sV, "v")
                    bank_mark[bkA] = ("v", m)
                    bank_mark[bkB] = ("v", m)
            m_opre = C["v"]

            base_tw = C["t"]
            base_sw = C["s"]
            @block.tensor
            def _(t, base_sw=base_sw):
                t.wait_ge(sV, m_opre)
                for mt in range(KT):
                    bk = mt % 4
                    if mt < 4:
                        bank_wait(t, bk)
                    else:
                        t.wait_ge(sS, base_sw + (mt - 4) + 1)
                    last = None
                    for kt in range(KT):
                        last = t.matmul(out=pb[bk][0:128, 0:CT],
                                        lhsT=WT[:, 4, kt, mt, :],
                                        rhs=diffb[:, kt, :],
                                        start=(kt == 0), stop=(kt == KT - 1))
                    mk(last, sT, "t")

            @block.scalar
            def _(sc, base_tw=base_tw):
                for mt in range(KT):
                    bk = mt % 4
                    sc.wait_ge(sT, base_tw + mt + 1)
                    m = mk(sc.activation(out=sqb[:, mt, :], in_=pb[bk][0:128, 0:CT],
                                         func=AF.Identity,
                                         bias=bias_sb[:, 4, mt:mt + 1]), sS, "s")
                    bank_mark[bk] = ("s", m)
            m_ao = C["s"]

            if c + 1 < CH:
                emit_ln(x_tm[(c + 1) % 2], xn, 0, c + 1,
                        extra_wait_v=m_hx,
                        extra_wait_qs=m_xmiddma.get(c - 1))
                m_ln1d[c + 1] = C["v"]

            base_t10 = C["t"]
            base_v10 = C["v"]
            @block.tensor
            def _(t, base_v10=base_v10):
                t.wait_ge(sS, m_ao)
                for i in range(16):
                    bk = 4 + i % 2
                    tt, kt = divmod(i, KT)
                    if i < 2:
                        bank_wait(t, bk)
                    else:
                        t.wait_ge(sV, base_v10 + (i - 2) + 1)
                    mk(t.transpose(out=pb[bk][0:128, 0:128],
                                   in_=sqb[:, kt, tt * 128:(tt + 1) * 128],
                                   identity=ident[:]), sT, "t")

            @block.vector
            def _(v, base_t10=base_t10, c=c):
                if c > 0:
                    v.wait_ge(sQS, m_xmiddma[c - 1])
                for i in range(16):
                    bk = 4 + i % 2
                    tt, kt = divmod(i, KT)
                    v.wait_ge(sT, base_t10 + i + 1)
                    m = mk(v.tensor_tensor(
                        out=xm_tm[:, tt, kt * 128:(kt + 1) * 128],
                        in0=pb[bk][0:128, 0:128],
                        in1=xcur[:, tt, kt * 128:(kt + 1) * 128],
                        op=ALU.add), sV, "v")
                    bank_mark[bk] = ("v", m)
            m_xm = C["v"]

            @block.sync
            def _(q, c=c, t0=t0):
                q.wait_ge(sV, m_xm)
                m_xmiddma[c] = mk(
                    q.dma_start(out=x_mid[t0:t0 + CT, :].rearrange("(t p) n -> p t n", p=128),
                                in_=xm_tm[:]), sQS, "qs")
                if dbg:
                    mk(q.dma_start(out=dxmid[t0:t0 + CT, :].rearrange("(t p) n -> p t n", p=128),
                                   in_=xm_tm[:]), sQS, "qs")

            emit_ln(xm_tm, xcur, 12, c, extra_wait_v=m_xm)
            m_h2n = C["v"]

            base_t12 = C["t"]
            base_v12 = C["v"]
            @block.tensor
            def _(t, base_v12=base_v12):
                t.wait_ge(sV, m_h2n)
                for i in range(16):
                    bk = 4 + i % 2
                    tt, kt = divmod(i, KT)
                    if i < 2:
                        bank_wait(t, bk)
                    else:
                        t.wait_ge(sV, base_v12 + (i - 2) + 1)
                    mk(t.transpose(out=pb[bk][0:128, 0:128],
                                   in_=xcur[:, tt, kt * 128:(kt + 1) * 128],
                                   identity=ident[:]), sT, "t")

            @block.vector
            def _(v, base_t12=base_t12):
                for i in range(16):
                    bk = 4 + i % 2
                    tt, kt = divmod(i, KT)
                    v.wait_ge(sT, base_t12 + i + 1)
                    m = mk(v.tensor_copy(out=h2f[:, kt, tt * 128:(tt + 1) * 128],
                                         in_=pb[bk][0:128, 0:128]), sV, "v")
                    bank_mark[bk] = ("v", m)
            m_h2f = C["v"]

            @block.vector
            def _(v):
                mk(v.tensor_copy(out=diffb[:], in_=h2f[:]), sV, "v")
            m_h2b = C["v"]

            @block.sync
            def _(q, c=c, t0=t0):
                q.wait_ge(sV, m_h2b)
                g2, off = divmod(t0, FT)
                m_h2dma[c] = mk(q.dma_start(
                    out=h2_loc[g2 * D:(g2 + 1) * D, :].rearrange(
                        "(kt p) n -> p kt n", p=128)[:, :, off:off + CT],
                    in_=diffb[:]), sQS, "qs")
                if dbg:
                    mk(q.dma_start(
                        out=dh2.rearrange("(kt p) n -> p kt n", p=128)[:, :, t0:t0 + CT],
                        in_=h2f[:]), sQS, "qs")

            @block.tensor
            def _(t):
                t.wait_ge(sV, m_h2f)
                bank_wait(t, 6)
                last = None
                for kt in range(KT):
                    last = t.matmul(out=pb[6][0:8, 0:CT], lhsT=gate_sb[:, kt, :],
                                    rhs=h2f[:, kt, :], start=(kt == 0), stop=(kt == KT - 1))
                mk(last, sT, "t")

            @block.vector
            def _(v):
                v.wait_ge(sT, C["t"])
                m = mk(v.tensor_copy(out=lgs[:], in_=pb[6][0:8, 0:CT]), sV, "v")
                bank_mark[6] = ("v", m)
            m_lgs = C["v"]

            for tt in range(2):
                bk = 4 + tt
                @block.tensor
                def _(t, tt=tt, bk=bk):
                    t.wait_ge(sV, m_lgs)
                    bank_wait(t, bk)
                    mk(t.transpose(out=pb[bk][0:128, 0:8],
                                   in_=lgs[0:8, tt * 128:(tt + 1) * 128],
                                   identity=ident[0:8, 0:8]), sT, "t")
            m_ltr = C["t"]

            @block.vector
            def _(v, m_ltr=m_ltr):
                v.wait_ge(sT, m_ltr)
                for tt in range(2):
                    mk(v.tensor_reduce(out=r_sm[:, tt, 0:1], in_=pb[4 + tt][0:128, 0:8],
                                       axis=AX, op=ALU.max, negate=True), sV, "v")

            @block.scalar
            def _(sc, m_ltr=m_ltr):
                sc.wait_ge(sV, C["v"])
                sc.wait_ge(sT, m_ltr)
                for tt in range(2):
                    sc.copy(ltm_s[:, tt, :], pb[4 + tt][0:128, 0:8])
                    mk(sc.mul(r_sm[:, tt, 1:2], r_sm[:, tt, 0:1], -1.0), sS, "s")
                    m = mk(sc.activation(out=edmp[:, tt, :], in_=pb[4 + tt][0:128, 0:8],
                                         func=AF.Exp, bias=r_sm[:, tt, 0:1],
                                         accum_out=r_sm[:, tt, 2:3]), sS, "s")
                    bank_mark[4 + tt] = ("s", m)

            @block.vector
            def _(v):
                v.wait_ge(sS, C["s"])
                for tt in range(2):
                    mk(v.reciprocal(out=r_sm[:, tt, 3:4], in_=r_sm[:, tt, 2:3]), sV, "v")

            @block.scalar
            def _(sc):
                sc.wait_ge(sV, C["v"])
                for tt in range(2):
                    mk(sc.copy(r_sm[:, tt, 4:5], r_sm[:, tt, 3:4]), sS, "s")

            @block.vector
            def _(v):
                v.wait_ge(sS, C["s"])
                for tt in range(2):
                    mk(v.tensor_scalar(out=scale_tm[:, tt, :], in0=ltm_s[:, tt, :],
                                       scalar1=r_sm[:, tt, 1:2],
                                       scalar2=r_sm[:, tt, 4:5],
                                       op0=ALU.is_equal, op1=ALU.mult), sV, "v")
            m_sctm = C["v"]

            @block.tensor
            def _(t, m_sctm=m_sctm):
                t.wait_ge(sV, m_sctm)
                bank_wait(t, 6)
                for tt in range(2):
                    mk(t.transpose(out=pb[6][0:8, tt * 128:(tt + 1) * 128],
                                   in_=scale_tm[:, tt, :],
                                   identity=ident[:]), sT, "t")

            @block.vector
            def _(v, c=c):
                v.wait_ge(sT, C["t"])
                if c > 0:
                    v.wait_ge(sNR, m_scdma[c - 1])
                m = mk(v.tensor_copy(out=scaleT[:], in_=pb[6][0:8, 0:256]), sV, "v")
                bank_mark[6] = ("v", m)

            m_scT = C["v"]

            @block.sync
            def _(q, c=c, t0=t0):
                q.wait_ge(sV, m_scT)
                m_scdma[c] = mk(q.dma_start(out=sc_loc[:, t0:t0 + CT], in_=scaleT[:]),
                                sNR, "nr")
                if dbg:
                    mk(q.dma_start(out=dscale[:, t0:t0 + CT], in_=scaleT[:]), sQS, "qs")

            m_xlast[c] = C["v"]

            if (c % (FT // CT)) == (FT // CT) - 1:
                g2 = c // (FT // CT)
                @block.gpsimd
                def _(g, g2=g2, c=c):
                    g.wait_ge(sQS, m_h2dma[c])
                    with allow_ici_all_to_all():
                        mk(g.collective_compute(
                            "AllGather", ALU.bypass, replica_groups=RG,
                            ins=[h2_loc[g2 * D:(g2 + 1) * D, :].opt()],
                            outs=[h2_all[g2 * NB * D:(g2 + 1) * NB * D, :].opt()]),
                           sCC, "cc")

        @block.gpsimd
        def _(g):
            g.wait_ge(sQS, C["qs"])
            g.wait_ge(sNR, C["nr"])
            with allow_ici_all_to_all():
                mk(g.collective_compute("AllToAll", ALU.bypass, replica_groups=RG,
                                        ins=[sc_loc.opt()], outs=[sc_c.opt()]),
                   sCC, "cc")
        m_ag = C["cc"]

        attn_done_v, attn_done_s, attn_done_t = C["v"], C["s"], C["t"]
        attn_done_g = C["g"]
        aes.close()

        fes = contextlib.ExitStack()
        TF = lambda name, shape, dt: fes.enter_context(nc.sbuf_tensor(name, shape, dt))
        w1b = TF("w1b", [128, KT, DFF], BF16)
        w2b = TF("w2b", [128, 32, D], BF16)
        h2s = TF("h2s", [128, KT, FT], BF16)
        h_sub = TF("h_sub", [128, 32, FT], BF16)
        ych = TF("ych", [128, 2, D], F32)
        sct = TF("sct", [128, 2, 4], F32)
        b1sb = TF("b1sb", [128, 32], F32)
        b2st = TF("b2st", [1, D], F32)
        scrow = b2st[0:1, 0:FT]
        b2row = TF("b2row", [1, D], BF16)
        onesb = TF("onesb", [1, 128], BF16)
        xmt = TF("xmt", [128, D], F32)
        yrt = TF("yrt", [128, D], F32)
        ot = TF("ot", [128, D], F32)
        fstg = TF("fstg", [128, 3, D], F32)

        @block.sync
        def _(q):
            q.wait_ge(sV, attn_done_v)
            q.wait_ge(sS, attn_done_s)
            q.wait_ge(sT, attn_done_t)
            q.wait_ge(sG, attn_done_g)
            mk(q.dma_start(out=b1sb[:], in_=b1_in), sQS, "qs")
            mk(q.dma_start(out=b2st[:], in_=b2_in[None, :]), sNR, "nr")
        m_fb = C["qs"]
        m_b2d = C["nr"]

        @block.gpsimd
        def _(g):
            g.wait_ge(sQS, m_fb)
            mk(g.memset(onesb[:], 1.0), sG, "g")
        m_ones = C["g"]

        @block.scalar
        def _(sc):
            sc.wait_ge(sNR, m_b2d)
            mk(sc.copy(b2row[:], b2st[:]), sS, "s")
        m_b2 = C["s"]

        base_wv = C["v"]

        def wsrc(i):
            if i < 32:
                return w1_in[i * 128:(i + 1) * 128, :]
            return w2_in[(i - 32) * 128:(i - 31) * 128, :]

        def wdst(i):
            if i < 32:
                kt, qr = divmod(i, 4)
                return w1b[:, kt, qr * D:(qr + 1) * D]
            return w2b[:, i - 32, :]

        @block.sync
        def _(q):
            q.wait_ge(sCC, m_ag)
            for i in range(64):
                if i >= 3:
                    q.wait_ge(sV, base_wv + (i - 2))
                mk(q.dma_start(out=fstg[:, i % 3, :], in_=wsrc(i)), sQS, "qs")
            for e in range(2):
                q.wait_ge(sV, base_wv + 62 + e)
                mk(q.dma_start(out=fstg[:, (64 + e) % 3, :], in_=w1_in[0:128, :]),
                   sQS, "qs")
        base_wq = C["qs"] - 66 * 16

        @block.vector
        def _(v):
            for i in range(64):
                v.wait_ge(sQS, base_wq + (i + 3) * 16)
                mk(v.tensor_copy(out=wdst(i), in_=fstg[:, i % 3, :]), sV, "v")
        m_wv = C["v"]
        m_w1v = base_wv + 32

        m_h2s = {}
        m_sct = {}
        m_sctr = {-1: 0}
        m_A_last = {-1: 0}
        m_ydma = {}
        m_ytsd = {}
        m_rs = {}
        fbank = {i: ("v", 0) for i in range(8)}

        def fbank_wait(eng, i):
            nm, val = fbank[i]
            if val > 0:
                eng.wait_ge(SEM[nm], val)

        m_oadd = [0]
        m_odma = [0]

        def emit_final(gidx):
            for tm in range(FT // 128):
                row = gidx * FT + tm * 128
                @block.gpsimd
                def _(g, row=row, gidx=gidx):
                    g.wait_ge(sCC, m_rs[gidx])
                    if m_oadd[0] > 0:
                        g.wait_ge(sV, m_oadd[0])
                    mk(g.dma_start(out=yrt[:], in_=y_red[row:row + 128, :]), sQG, "qg")
                    mk(g.dma_start(out=xmt[:], in_=x_mid[row:row + 128, :]), sQG, "qg")

                @block.vector
                def _(v, row=row):
                    v.wait_ge(sQG, C["qg"])
                    if m_odma[0] > 0:
                        v.wait_ge(sSO, m_odma[0])
                    m_oadd[0] = mk(v.tensor_tensor(out=ot[:], in0=yrt[:], in1=xmt[:],
                                                   op=ALU.add), sV, "v")

                @block.gpsimd
                def _(g, row=row):
                    g.wait_ge(sV, m_oadd[0])
                    m_odma[0] = mk(g.dma_start(out=out_t[row:row + 128, :], in_=ot[:]),
                                   sSO, "so")

        def emit_rs(gidx):
            @block.gpsimd
            def _(g, gidx=gidx):
                for cc2 in range(gidx * NB, (gidx + 1) * NB):
                    g.wait_ge(sQS, m_ydma[(cc2, 1)])
                base = gidx * NB * FT
                m_rs[gidx] = mk(g.collective_compute(
                    "ReduceScatter", ALU.add, replica_groups=RG,
                    ins=[y_loc[base:base + NB * FT, :].opt()],
                    outs=[y_red[gidx * FT:(gidx + 1) * FT, :].opt()]), sCC, "cc")

        chunk = 0
        for gidx in range(NG):
            for b in range(NB):
                t0 = gidx * FT
                if b == 2 and gidx > 0:
                    emit_final(gidx - 1)
                @block.gpsimd
                def _(g, gidx=gidx, b=b, chunk=chunk, t0=t0):
                    g.wait_ge(sCC, m_ag)
                    g.wait_ge(sT, m_A_last[chunk - 1])
                    base = (gidx * NB + b) * D
                    m_h2s[chunk] = mk(g.dma_start(
                        out=h2s[:],
                        in_=h2_all[base:base + D, :].rearrange("(kt p) n -> p kt n", p=128)),
                        sQG, "qg")
                    g.wait_ge(sT, m_sctr[chunk - 1])
                    if chunk == 0:
                        g.wait_ge(sS, m_b2)
                    m_sct[chunk] = mk(g.dma_start(
                        out=scrow, in_=sc_c[b:b + 1, t0:t0 + FT]), sNR, "nr")
                if b == 0 and gidx > 0:
                    emit_rs(gidx - 1)

                base_t = C["t"]
                base_s = C["s"]
                @block.tensor
                def _(t, chunk=chunk, base_s=base_s):
                    t.wait_ge(sQG, m_h2s[chunk])
                    if chunk == 0:
                        t.wait_ge(sV, m_w1v)
                        t.wait_ge(sG, m_ones)
                    for j in range(32):
                        bk = j % 4
                        if j < 4:
                            fbank_wait(t, bk)
                        else:
                            t.wait_ge(sS, base_s + (j - 4) + 1)
                        last = None
                        for kt in range(KT):
                            last = t.matmul(out=pb[bk][0:128, 0:FT],
                                            lhsT=w1b[:, kt, j * 128:(j + 1) * 128],
                                            rhs=h2s[:, kt, :],
                                            start=(kt == 0), stop=(kt == KT - 1))
                        mk(last, sT, "t")

                @block.scalar
                def _(sc, base_t=base_t):
                    for j in range(32):
                        bk = j % 4
                        sc.wait_ge(sT, base_t + j + 1)
                        m = mk(sc.activation(out=h_sub[:, j, :], in_=pb[bk][0:128, 0:FT],
                                             func=AF.Gelu, bias=b1sb[:, j:j + 1]), sS, "s")
                        fbank[bk] = ("s", m)
                m_A_last[chunk] = C["t"]
                m_gelu_all = C["s"]

                @block.tensor
                def _(t, chunk=chunk):
                    t.wait_ge(sNR, m_sct[chunk])
                    fbank_wait(t, 0)
                    last = None
                    for tm in range(4):
                        last = t.transpose(out=pb[0][0:128, tm:tm + 1],
                                           in_=b2st[0:1, tm * 128:(tm + 1) * 128],
                                           identity=ident[0:1, 0:1])
                    m_sctr[chunk] = mk(last, sT, "t")

                @block.scalar
                def _(sc, chunk=chunk):
                    sc.wait_ge(sT, m_sctr[chunk])
                    if chunk >= 2:
                        sc.wait_ge(sV, m_ytsd[chunk - 2])
                    m = mk(sc.copy(sct[:, chunk % 2, :], pb[0][0:128, 0:4]), sS, "s")
                    fbank[0] = ("s", m)
                m_sctev = C["s"]

                base_t2 = C["t"]
                base_v2 = C["v"]
                @block.tensor
                def _(t, m_gelu_all=m_gelu_all, base_v2=base_v2, chunk=chunk):
                    t.wait_ge(sS, m_gelu_all)
                    t.wait_ge(sS, m_b2)
                    if chunk == 0:
                        t.wait_ge(sV, m_wv)
                    for q2 in range(8):
                        tm, dc = divmod(q2, 2)
                        bk = 4 + q2 % 4
                        if q2 < 4:
                            fbank_wait(t, bk)
                        else:
                            t.wait_ge(sV, base_v2 + (q2 - 4) + 1)
                        for j in range(32):
                            t.matmul(out=pb[bk][0:128, 0:FT],
                                     lhsT=h_sub[:, j, tm * 128:(tm + 1) * 128],
                                     rhs=w2b[:, j, dc * 512:(dc + 1) * 512],
                                     start=(j == 0), stop=False)
                        mk(t.matmul(out=pb[bk][0:128, 0:FT],
                                    lhsT=onesb[:],
                                    rhs=b2row[0:1, dc * 512:(dc + 1) * 512],
                                    start=False, stop=True), sT, "t")

                for half in range(2):
                    @block.vector
                    def _(v, half=half, chunk=chunk, m_sctev=m_sctev,
                          base_t2=base_t2):
                        for q2 in range(half * 4, half * 4 + 4):
                            tm, dc = divmod(q2, 2)
                            bk = 4 + q2 % 4
                            v.wait_ge(sT, base_t2 + q2 + 1)
                            if q2 == 0:
                                v.wait_ge(sS, m_sctev)
                                if chunk >= 1:
                                    v.wait_ge(sQS, m_ydma[(chunk - 1, 1)])
                            elif q2 == 4:
                                v.wait_ge(sQS, m_ydma[(chunk, 0)])
                            m = mk(v.tensor_scalar(
                                out=ych[:, tm % 2, dc * 512:(dc + 1) * 512],
                                in0=pb[bk][0:128, 0:FT],
                                scalar1=sct[:, chunk % 2, tm:tm + 1], scalar2=None,
                                op0=ALU.mult), sV, "v")
                            fbank[bk] = ("v", m)

                    @block.sync
                    def _(q, gidx=gidx, b=b, chunk=chunk, half=half):
                        q.wait_ge(sV, C["v"])
                        base = (gidx * NB + b) * FT + half * 256
                        m_ydma[(chunk, half)] = mk(q.dma_start(
                            out=y_loc[base:base + 256, :].rearrange(
                                "(t p) n -> p t n", p=128),
                            in_=ych[:]), sQS, "qs")
                m_ytsd[chunk] = C["v"]
                chunk += 1

        emit_rs(NG - 1)
        emit_final(NG - 1)

        if dbg:
            @block.sync
            def _(q):
                q.wait_ge(sSO, C["so"])
                mk(q.dma_start(out=dyloc, in_=y_loc), sQS, "qs")
                mk(q.dma_start(out=dyred, in_=y_red), sQS, "qs")
                mk(q.dma_start(out=dw1b, in_=w1b[:].rearrange("p a b -> p (a b)")), sQS, "qs")
                mk(q.dma_start(out=dw2b, in_=w2b[:].rearrange("p a b -> p (a b)")), sQS, "qs")
                mk(q.dma_start(out=db1, in_=b1sb[:]), sQS, "qs")
                mk(q.dma_start(out=db2, in_=b2row[:]), sQS, "qs")
                mk(q.dma_start(out=dhsub, in_=h_sub[:].rearrange("p a b -> p (a b)")), sQS, "qs")

        @block.gpsimd
        def _(g):
            g.wait_ge(sSO, C["so"])
            g.wait_ge(sQS, C["qs"])
            g.wait_ge(sQG, C["qg"])
            g.wait_ge(sCC, C["cc"])

        fes.close()
    return nc


def make_in_maps(inputs, N):
    CH = N // CT
    x = np.asarray(inputs["L0_attn_x"])
    vec = lambda v: np.ascontiguousarray(
        np.asarray(v, np.float32).reshape(KT, 128).T)
    maps = []
    for c in range(NB):
        xt = np.ascontiguousarray(
            x[c, :N].reshape(CH, 2, 128, D).transpose(0, 2, 1, 3).reshape(CH * 128, 2 * D))
        w1t = np.ascontiguousarray(
            np.asarray(inputs["L0_ffn_w1"])[c].reshape(KT, 128, 4, D)
            .transpose(0, 2, 1, 3).reshape(32 * 128, D))
        m = dict(
            x=xt,
            xlast=np.ascontiguousarray(x[c, N - 1:N]),
            Wg=inputs["L0_attn_Wg_w"], bg=vec(inputs["L0_attn_Wg_b"]),
            Wr=inputs["L0_attn_Wr_w"], br=vec(inputs["L0_attn_Wr_b"]),
            Wk=inputs["L0_attn_Wk_w"], bk=vec(inputs["L0_attn_Wk_b"]),
            Wv=inputs["L0_attn_Wv_w"], bv=vec(inputs["L0_attn_Wv_b"]),
            Wo=inputs["L0_attn_Wo_w"], bo=vec(inputs["L0_attn_Wo_b"]),
            mg=vec(inputs["L0_attn_mg"]), mr=vec(inputs["L0_attn_mr"]),
            mk=vec(inputs["L0_attn_mk"]), mv=vec(inputs["L0_attn_mv"]),
            wdec=vec(np.asarray(inputs["L0_attn_w"]).reshape(-1)),
            uvec=vec(np.asarray(inputs["L0_attn_u"]).reshape(-1)),
            gate=np.ascontiguousarray(
                np.asarray(inputs["L0_ffn_gate_w"], np.float32)
                .reshape(KT, 128, NEXP).transpose(1, 0, 2)),
            w1e=w1t,
            b1e=np.ascontiguousarray(
                np.asarray(inputs["L0_ffn_b1"])[c].reshape(32, 128).T),
            w2e=np.asarray(inputs["L0_ffn_w2"])[c],
            b2e=np.asarray(inputs["L0_ffn_b2"])[c],
        )
        maps.append({k: np.ascontiguousarray(np.asarray(v), dtype=np.float32)
                     for k, v in m.items()})
    return maps

def kernel(**inputs):
    nc = build_program(N=4096, dbg=False)
    in_maps = make_in_maps(inputs, 4096)
    res = run_bass_kernel_spmd(nc, in_maps, core_ids=list(range(8)))
    return np.stack([np.asarray(res.results[c]["out"]) for c in range(8)])


# version 10
# speedup vs baseline: 1.0285x; 1.0285x over previous
import contextlib
import numpy as np
import sys
for _p in ("/opt/trn_rl_repo",):
    if _p not in sys.path:
        sys.path.append(_p)
import concourse.bass as bass
import concourse.mybir as mybir
from concourse.bass_utils import run_bass_kernel_spmd
from concourse.aot_env import allow_ici_all_to_all

F32 = mybir.dt.float32
BF16 = mybir.dt.bfloat16
ALU = mybir.AluOpType
AF = mybir.ActivationFunctionType
AX = mybir.AxisListType.X
RG = [[0, 1, 2, 3, 4, 5, 6, 7]]

D = 1024
KT = 8
CT = 256
FT = 512
DFF = 4096
NEXP = 8
NB = 8


def build_program(N=4096, dbg=False):
    CH = N // CT
    NG = N // FT
    nc = bass.Bass("TRN2", target_bir_lowering=False)

    x_in = nc.dram_tensor("x", [CH * 128, 2 * D], F32, kind="ExternalInput").ap()
    xlast_in = nc.dram_tensor("xlast", [1, D], F32, kind="ExternalInput").ap()
    W_in, b_in, m_in = {}, {}, {}
    for wi, nm in enumerate("grkvo"):
        W_in[wi] = nc.dram_tensor(f"W{nm}", [D, D], F32, kind="ExternalInput").ap()
        b_in[wi] = nc.dram_tensor(f"b{nm}", [128, KT], F32, kind="ExternalInput").ap()
    for wi, nm in enumerate("grkv"):
        m_in[wi] = nc.dram_tensor(f"m{nm}", [128, KT], F32, kind="ExternalInput").ap()
    wdec_in = nc.dram_tensor("wdec", [128, KT], F32, kind="ExternalInput").ap()
    u_in = nc.dram_tensor("uvec", [128, KT], F32, kind="ExternalInput").ap()
    gate_in = nc.dram_tensor("gate", [128, KT, NEXP], F32, kind="ExternalInput").ap()
    w1_in = nc.dram_tensor("w1e", [32 * 128, D], F32, kind="ExternalInput").ap()
    b1_in = nc.dram_tensor("b1e", [128, 32], F32, kind="ExternalInput").ap()
    w2_in = nc.dram_tensor("w2e", [DFF, D], F32, kind="ExternalInput").ap()
    b2_in = nc.dram_tensor("b2e", [D], F32, kind="ExternalInput").ap()
    out_t = nc.dram_tensor("out", [N, D], F32, kind="ExternalOutput").ap()
    if dbg:
        dxmid = nc.dram_tensor("dxmid", [N, D], F32, kind="ExternalOutput").ap()
        dh2 = nc.dram_tensor("dh2", [D, N], F32, kind="ExternalOutput").ap()
        dscale = nc.dram_tensor("dscale", [NEXP, N], F32, kind="ExternalOutput").ap()
        dyloc = nc.dram_tensor("dyloc", [NB * N, D], F32, kind="ExternalOutput").ap()
        dyred = nc.dram_tensor("dyred", [N, D], F32, kind="ExternalOutput").ap()
        dw1b = nc.dram_tensor("dw1b", [128, KT * DFF], BF16, kind="ExternalOutput").ap()
        dw2b = nc.dram_tensor("dw2b", [128, 32 * D], BF16, kind="ExternalOutput").ap()
        db1 = nc.dram_tensor("db1", [128, 32], F32, kind="ExternalOutput").ap()
        db2 = nc.dram_tensor("db2", [1, D], BF16, kind="ExternalOutput").ap()
        dhsub = nc.dram_tensor("dhsub", [128, 32 * FT], BF16, kind="ExternalOutput").ap()

    x_mid = nc.dram_tensor("x_mid", [N, D], F32, kind="Internal").ap()
    h2_loc = nc.dram_tensor("h2_loc", [NG * D, FT], BF16, kind="Internal").ap()
    h2_all = nc.dram_tensor("h2_all", [NB * NG * D, FT], BF16, kind="Internal",
                            addr_space="Local").ap()
    sc_loc = nc.dram_tensor("sc_loc", [NEXP, N], F32, kind="Internal").ap()
    sc_c = nc.dram_tensor("sc_c", [NB, N], F32, kind="Internal",
                          addr_space="Local").ap()
    y_loc = nc.dram_tensor("y_loc", [NB * N, D], F32, kind="Internal").ap()
    y_red = nc.dram_tensor("y_red", [N, D], F32, kind="Internal",
                           addr_space="Local").ap()

    C = {}

    def mk(instr, sem, key, n=None):
        if n is None:
            n = 16 if key in ("qs", "qg", "so", "nr") else 1
        instr.then_inc(sem, n)
        C[key] = C.get(key, 0) + n
        return C[key]

    es = contextlib.ExitStack()
    with es:
        pb = [es.enter_context(nc.psum_tensor(f"pb{i}", [128, 512], F32)) for i in range(8)]
        epsv = es.enter_context(nc.sbuf_tensor("epsv", [128, 1], F32))
        ident = es.enter_context(nc.sbuf_tensor("ident", [128, 128], F32))
        sQS = es.enter_context(nc.semaphore("qs"))
        sQG = es.enter_context(nc.semaphore("qg"))
        sT = es.enter_context(nc.semaphore("t"))
        sV = es.enter_context(nc.semaphore("v"))
        sS = es.enter_context(nc.semaphore("s"))
        sG = es.enter_context(nc.semaphore("g"))
        sCC = es.enter_context(nc.semaphore("cc"))
        sSO = es.enter_context(nc.semaphore("so"))
        sNR = es.enter_context(nc.semaphore("nr"))
        SEM = {"v": sV, "s": sS, "t": sT, "g": sG, "qs": sQS, "qg": sQG,
               "cc": sCC, "so": sSO, "nr": sNR}

        aes = contextlib.ExitStack()
        TA = lambda name, shape, dt: aes.enter_context(nc.sbuf_tensor(name, shape, dt))
        WT = TA("WT", [128, 5, KT, KT, 128], BF16)
        bias_sb = TA("bias_sb", [128, 5, KT], F32)
        mvec = TA("mvec", [128, 4, KT], F32)
        uvec = TA("uvec_sb", [128, KT], F32)
        e1v = TA("e1v", [128, KT], F32)
        ngv = TA("ngv", [128, KT], F32)
        decv = TA("decv", [128, KT], F32)
        Gmat = TA("Gmat", [128, KT, 16], F32)
        Emat = TA("Emat", [16, KT, 128], F32)
        gate_sb = TA("gate_sb", [128, KT, NEXP], F32)
        h_bnd = TA("h_bnd", [128, KT, 1], F32)
        carry_s = TA("carry_s", [128, KT, 1], F32)
        ses = contextlib.ExitStack()
        wstg = ses.enter_context(nc.sbuf_tensor("wstg", [128, 2, D], F32))
        xrow = ses.enter_context(nc.sbuf_tensor("xrow", [1, D], F32))
        rsm = ses.enter_context(nc.sbuf_tensor("rsm", [1, 12], F32))
        Gp = ses.enter_context(nc.sbuf_tensor("Gp", [128, KT, 16], F32))
        rdmp = ses.enter_context(nc.sbuf_tensor("rdmp", [1, D], F32))

        block = es.enter_context(nc.Block())

        bank_mark = {i: ("v", 0) for i in range(8)}

        def bank_wait(eng, i):
            nm, val = bank_mark[i]
            if val > 0:
                eng.wait_ge(SEM[nm], val)

        @block.gpsimd
        def _(g):
            mk(g.memset(epsv[:], 1e-5), sG, "g")
            mk(g.memset(ident[:], 0.0), sG, "g")
            g.wait_ge(sG, C["g"])
            mk(g.affine_select(out=ident[:], in_=ident[:], compare_op=ALU.not_equal,
                               fill=1.0, base=0, pattern=[[-1, 128]],
                               channel_multiplier=1), sG, "g")
            mk(g.memset(Gmat[:], 0.0), sG, "g")
            mk(g.memset(Gp[:], 0.0), sG, "g")
            g.wait_ge(sG, C["g"])
            for kt in range(KT):
                mk(g.memset(Gmat[0:64, kt, 2 * kt:2 * kt + 1], 1.0 / 64), sG, "g")
                mk(g.memset(Gmat[64:128, kt, 2 * kt + 1:2 * kt + 2], 1.0 / 64),
                   sG, "g")
                mk(g.memset(Gp[0:64, kt, 2 * kt:2 * kt + 1], 1.0), sG, "g")
            for kt in range(KT):
                mk(g.memset(Gp[64:128, kt, 2 * kt + 1:2 * kt + 2], 1.0), sG, "g")
        m_setup_g = C["g"]

        @block.sync
        def _(q):
            for wi in range(5):
                mk(q.dma_start(out=bias_sb[:, wi, :], in_=b_in[wi]), sQS, "qs")
            for wi in range(4):
                mk(q.dma_start(out=mvec[:, wi, :], in_=m_in[wi]), sQS, "qs")
            mk(q.dma_start(out=uvec[:], in_=u_in), sQS, "qs")
            mk(q.dma_start(out=e1v[:], in_=wdec_in), sQS, "qs")
            mk(q.dma_start(out=gate_sb[:], in_=gate_in), sQS, "qs")
            mk(q.dma_start(out=xrow[:], in_=xlast_in), sNR, "nr")
        m_vecs = C["qs"]
        m_xrow = C["nr"]

        @block.scalar
        def _(sc):
            sc.wait_ge(sQS, m_vecs)
            mk(sc.activation(out=decv[:], in_=e1v[:], func=AF.Exp), sS, "s")

        @block.vector
        def _(v):
            v.wait_ge(sS, C["s"])
            mk(v.tensor_scalar(out=ngv[:], in0=decv[:], scalar1=-1.0, scalar2=None,
                               op0=ALU.mult), sV, "v")

        @block.scalar
        def _(sc):
            sc.wait_ge(sV, C["v"])
            mk(sc.activation(out=decv[:], in_=ngv[:], func=AF.Exp), sS, "s")
        m_dec = C["s"]

        m_decrep = m_dec

        for kt in range(KT):
            bk = kt % 2
            @block.tensor
            def _(t, kt=kt, bk=bk):
                t.wait_ge(sG, m_setup_g)
                bank_wait(t, bk)
                mk(t.transpose(out=pb[bk][0:16, 0:128], in_=Gp[:, kt, :],
                               identity=ident[:]), sT, "t")

            @block.vector
            def _(v, kt=kt, bk=bk):
                v.wait_ge(sT, C["t"])
                m = mk(v.tensor_copy(out=Emat[:, kt, :], in_=pb[bk][0:16, 0:128]),
                       sV, "v")
                bank_mark[bk] = ("v", m)

        wstg_guard = {}

        @block.sync
        def _(q):
            q.wait_ge(sQS, m_vecs)
            mk(q.dma_start(out=wstg[:, 0, :], in_=W_in[0][0:128, :]), sQS, "qs")
        m_wstg0 = C["qs"]

        for idx in range(5 * KT):
            wi, mt = divmod(idx, KT)
            @block.sync
            def _(q, idx=idx):
                if idx + 1 < 5 * KT:
                    if idx >= 1:
                        q.wait_ge(sT, wstg_guard[idx - 1])
                    else:
                        q.wait_ge(sQS, m_wstg0)
                    wi2, mt2 = divmod(idx + 1, KT)
                    mk(q.dma_start(out=wstg[:, (idx + 1) % 2, :],
                                   in_=W_in[wi2][mt2 * 128:(mt2 + 1) * 128, :]),
                       sQS, "qs")
                else:
                    q.wait_ge(sT, wstg_guard[idx - 1])
                    mk(q.dma_start(out=wstg[:, (idx + 1) % 2, :],
                                   in_=W_in[0][0:128, :]), sQS, "qs")
            m_wdma = C["qs"]
            for kt2 in range(KT):
                bk = 2 + kt2 % 2
                @block.tensor
                def _(t, wi=wi, mt=mt, idx=idx, kt2=kt2, bk=bk, m_wdma=m_wdma):
                    if kt2 == 0:
                        t.wait_ge(sQS, m_wdma)
                    bank_wait(t, bk)
                    m = mk(t.transpose(out=pb[bk][0:128, 0:128],
                                       in_=wstg[:, idx % 2, kt2 * 128:(kt2 + 1) * 128],
                                       identity=ident[:]), sT, "t")
                    if kt2 == KT - 1:
                        wstg_guard[idx] = m

                @block.vector
                def _(v, wi=wi, mt=mt, kt2=kt2, bk=bk):
                    v.wait_ge(sT, C["t"])
                    m = mk(v.tensor_copy(out=WT[:, wi, kt2, mt, :],
                                         in_=pb[bk][0:128, 0:128]), sV, "v")
                    bank_mark[bk] = ("v", m)
        m_wt = C["v"]

        @block.vector
        def _(v):
            v.wait_ge(sQS, m_vecs)
            v.wait_ge(sNR, m_xrow)
            mk(v.tensor_reduce(out=rsm[:, 0:1], in_=xrow[:], axis=AX, op=ALU.add),
               sV, "v")

        @block.scalar
        def _(sc):
            sc.wait_ge(sV, C["v"])
            mk(sc.mul(rsm[:, 1:2], rsm[:, 0:1], 1.0 / D), sS, "s")
            mk(sc.activation(out=rdmp[:], in_=xrow[:], func=AF.Square,
                             accum_out=rsm[:, 2:3]), sS, "s")

        @block.vector
        def _(v):
            v.wait_ge(sS, C["s"])
            v.tensor_tensor(out=rsm[:, 3:4], in0=rsm[:, 1:2], in1=rsm[:, 1:2], op=ALU.mult)
            mk(v.tensor_scalar(out=rsm[:, 4:5], in0=rsm[:, 2:3], scalar1=1.0 / D,
                               scalar2=None, op0=ALU.mult), sV, "v")

        @block.gpsimd
        def _(g):
            g.wait_ge(sV, C["v"])
            mk(g.tensor_tensor(out=rsm[:, 5:6], in0=rsm[:, 4:5], in1=rsm[:, 3:4],
                               op=ALU.subtract), sG, "g")

        @block.scalar
        def _(sc):
            sc.wait_ge(sG, C["g"])
            mk(sc.activation(out=rsm[:, 6:7], in_=rsm[:, 5:6], func=AF.Ln,
                             bias=epsv[0:1, 0:1]), sS, "s")

        @block.vector
        def _(v):
            v.wait_ge(sS, C["s"])
            mk(v.tensor_scalar(out=rsm[:, 7:8], in0=rsm[:, 6:7], scalar1=-0.5,
                               scalar2=None, op0=ALU.mult), sV, "v")

        @block.scalar
        def _(sc):
            sc.wait_ge(sV, C["v"])
            mk(sc.activation(out=rsm[:, 8:9], in_=rsm[:, 7:8], func=AF.Exp), sS, "s")

        @block.vector
        def _(v):
            v.wait_ge(sS, C["s"])
            mk(v.tensor_scalar(out=rdmp[:], in0=xrow[:], scalar1=rsm[:, 1:2],
                               scalar2=rsm[:, 8:9], op0=ALU.subtract, op1=ALU.mult),
               sV, "v")
        m_rowln = C["v"]

        for kt in range(KT):
            bk = 4 + kt % 2
            @block.tensor
            def _(t, kt=kt, bk=bk):
                t.wait_ge(sV, m_rowln)
                bank_wait(t, bk)
                mk(t.transpose(out=pb[bk][0:128, kt:kt + 1],
                               in_=rdmp[0:1, kt * 128:(kt + 1) * 128],
                               identity=ident[0:1, 0:1]), sT, "t")

            @block.vector
            def _(v, kt=kt, bk=bk):
                v.wait_ge(sT, C["t"])
                m = mk(v.tensor_copy(out=h_bnd[:, kt, :], in_=pb[bk][0:128, kt:kt + 1]),
                       sV, "v")
                bank_mark[bk] = ("v", m)
        m_hbnd_init = ("v", C["v"])
        m_setup_done_v = C["v"]

        ses.close()
        x_tm = [TA(f"x_tm{i}", [128, 2, D], F32) for i in range(2)]
        xn = TA("xn", [128, 2, D], F32)
        xm_tm = TA("xm_tm", [128, 2, D], F32)
        dmp = TA("dmp", [128, 2, D], F32)
        sm = TA("sm", [128, 2, 24], F32)
        h_ext = TA("h_ext", [128, KT, CT + 1], F32)
        diffb = TA("diffb", [128, KT, CT], BF16)
        mixb = [TA(f"mixb{i}", [128, KT, CT], BF16) for i in range(2)]
        sgb = TA("sgb", [128, KT, CT], BF16)
        rb = TA("rb", [128, KT, CT], F32)
        kb = TA("kb", [128, KT, CT], F32)
        kvb = TA("kvb", [128, KT, CT], F32)
        wkvb = TA("wkvb", [128, KT, CT], F32)
        sqb = TA("sqb", [128, KT, CT], F32)
        tmp1 = TA("tmp1", [128, CT], F32)
        h2f = TA("h2f", [128, KT, CT], F32)
        mean_sb = TA("mean_sb", [16, CT], F32)
        msq_sb = TA("msq_sb", [16, CT], F32)
        var16 = TA("var16", [16, CT], F32)
        l16 = TA("l16", [16, CT], F32)
        r16 = var16
        ltm_s = TA("ltm_s", [128, 2, 8], F32)
        r_sm = TA("r_sm", [128, 2, 6], F32)
        edmp = TA("edmp", [128, 2, 8], F32)
        scale_tm = TA("scale_tm", [128, 2, 8], F32)
        scaleT = TA("scaleT", [8, CT], F32)
        lgs = scaleT

        @block.gpsimd
        def _(g):
            g.wait_ge(sV, m_setup_done_v)
            mk(g.dma_start(out=x_tm[0][:].rearrange("p a b -> p (a b)"),
                           in_=x_in[0:128, :]), sQG, "qg")

        m_x = {0: C["qg"]}
        m_xlast = {}
        m_carry = {}
        m_hbnd = {-1: m_hbnd_init}
        m_h2dma = {}
        m_scdma = {}
        m_xmiddma = {}
        m_kvd = {}

        def emit_ln(src, dst, slot, c, extra_wait_v=None, extra_wait_qs=None):
            @block.vector
            def _(v):
                v.wait_ge(sQG, m_x[c])
                if extra_wait_v is not None:
                    v.wait_ge(sV, extra_wait_v)
                if extra_wait_qs is not None:
                    v.wait_ge(sQS, extra_wait_qs)
                for tt in range(2):
                    mk(v.tensor_reduce(out=sm[:, tt, slot:slot + 1],
                                       in_=src[:, tt, :], axis=AX, op=ALU.add), sV, "v")

            @block.scalar
            def _(sc):
                sc.wait_ge(sV, C["v"])
                sc.wait_ge(sQG, m_x[c])
                for tt in range(2):
                    mk(sc.mul(sm[:, tt, slot + 1:slot + 2],
                              sm[:, tt, slot:slot + 1], 1.0 / D), sS, "s")
                    mk(sc.activation(out=dmp[:, tt, :], in_=src[:, tt, :],
                                     func=AF.Square,
                                     accum_out=sm[:, tt, slot + 2:slot + 3]), sS, "s")

            @block.vector
            def _(v):
                v.wait_ge(sS, C["s"])
                for tt in range(2):
                    v.tensor_tensor(out=sm[:, tt, slot + 3:slot + 4],
                                    in0=sm[:, tt, slot + 1:slot + 2],
                                    in1=sm[:, tt, slot + 1:slot + 2], op=ALU.mult)
                    mk(v.tensor_scalar(out=sm[:, tt, slot + 4:slot + 5],
                                       in0=sm[:, tt, slot + 2:slot + 3],
                                       scalar1=1.0 / D, scalar2=None, op0=ALU.mult),
                       sV, "v")

            @block.gpsimd
            def _(g):
                g.wait_ge(sV, C["v"])
                for tt in range(2):
                    mk(g.tensor_tensor(out=sm[:, tt, slot + 5:slot + 6],
                                       in0=sm[:, tt, slot + 4:slot + 5],
                                       in1=sm[:, tt, slot + 3:slot + 4],
                                       op=ALU.subtract), sG, "g")

            @block.scalar
            def _(sc):
                sc.wait_ge(sG, C["g"])
                for tt in range(2):
                    mk(sc.activation(out=sm[:, tt, slot + 6:slot + 7],
                                     in_=sm[:, tt, slot + 5:slot + 6],
                                     func=AF.Ln, bias=epsv[:]), sS, "s")

            @block.vector
            def _(v):
                v.wait_ge(sS, C["s"])
                for tt in range(2):
                    mk(v.tensor_scalar(out=sm[:, tt, slot + 7:slot + 8],
                                       in0=sm[:, tt, slot + 6:slot + 7],
                                       scalar1=-0.5, scalar2=None, op0=ALU.mult), sV, "v")

            @block.scalar
            def _(sc):
                sc.wait_ge(sV, C["v"])
                for tt in range(2):
                    mk(sc.activation(out=sm[:, tt, slot + 8:slot + 9],
                                     in_=sm[:, tt, slot + 7:slot + 8],
                                     func=AF.Exp), sS, "s")

            @block.vector
            def _(v):
                v.wait_ge(sS, C["s"])
                for tt in range(2):
                    mk(v.tensor_scalar(out=dst[:, tt, :], in0=src[:, tt, :],
                                       scalar1=sm[:, tt, slot + 1:slot + 2],
                                       scalar2=sm[:, tt, slot + 8:slot + 9],
                                       op0=ALU.subtract, op1=ALU.mult), sV, "v")

        m_ln1d = {}
        emit_ln(x_tm[0], xn, 0, 0)
        m_ln1d[0] = C["v"]

        for c in range(CH):
            t0 = c * CT
            if c + 1 < CH:
                @block.gpsimd
                def _(g, c=c):
                    if c >= 1:
                        g.wait_ge(sV, m_xlast.get(c - 1, 0))
                    mk(g.dma_start(
                        out=x_tm[(c + 1) % 2][:].rearrange("p a b -> p (a b)"),
                        in_=x_in[(c + 1) * 128:(c + 2) * 128, :]), sQG, "qg")
                    m_x[c + 1] = C["qg"]

            xcur = x_tm[c % 2]
            m_ln1 = m_ln1d[c]

            @block.gpsimd
            def _(g, c=c):
                nm, val = m_hbnd[c - 1]
                g.wait_ge(SEM[nm], val)
                if c > 0:
                    g.wait_ge(sV, m_kvd[c - 1])
                mk(g.tensor_copy(out=h_ext[:, :, 0:1], in_=h_bnd[:]), sG, "g")
            m_bcol = C["g"]

            base_t = C["t"]
            base_v = C["v"]
            @block.tensor
            def _(t, base_v=base_v):
                t.wait_ge(sV, m_ln1)
                for i in range(16):
                    bk = 4 + i % 2
                    if i < 2:
                        bank_wait(t, bk)
                    else:
                        t.wait_ge(sV, base_v + (i - 2) + 1)
                    tt, kt = divmod(i, KT)
                    mk(t.transpose(out=pb[bk][0:128, 0:128],
                                   in_=xn[:, tt, kt * 128:(kt + 1) * 128],
                                   identity=ident[:]), sT, "t")

            @block.vector
            def _(v, c=c, base_t=base_t):
                if c > 0:
                    nm, val = m_hbnd[c - 1]
                    v.wait_ge(SEM[nm], val)
                for i in range(16):
                    bk = 4 + i % 2
                    tt, kt = divmod(i, KT)
                    v.wait_ge(sT, base_t + i + 1)
                    m = mk(v.tensor_copy(
                        out=h_ext[:, kt, 1 + tt * 128:1 + (tt + 1) * 128],
                        in_=pb[bk][0:128, 0:128]), sV, "v")
                    bank_mark[bk] = ("v", m)
            m_hx = C["v"]

            @block.scalar
            def _(sc, c=c, m_bcol=m_bcol):
                sc.wait_ge(sV, m_hx)
                sc.wait_ge(sG, m_bcol)
                m_hbnd[c] = ("s", mk(sc.copy(h_bnd[:], h_ext[:, :, CT:CT + 1]), sS, "s"))

            @block.vector
            def _(v, c=c, m_bcol=m_bcol, m_hx=m_hx):
                v.wait_ge(sG, m_bcol)
                v.wait_ge(sV, m_hx)
                if c == 0:
                    v.wait_ge(sV, m_wt)
                else:
                    v.wait_ge(sQS, m_h2dma[c - 1])
                mk(v.tensor_tensor(out=sqb[:], in0=h_ext[:, :, 1:CT + 1],
                                   in1=h_ext[:, :, 0:CT], op=ALU.subtract), sV, "v")
            m_diff = C["v"]

            m_mixmm = {}
            for wi in range(4):
                @block.vector
                def _(v, wi=wi):
                    v.wait_ge(sV, m_diff)
                    if wi >= 2:
                        v.wait_ge(sT, m_mixmm[wi - 2])
                    last = None
                    for kt in range(KT):
                        last = v.scalar_tensor_tensor(
                            out=mixb[wi % 2][:, kt, :], in0=sqb[:, kt, :],
                            scalar=mvec[:, wi, kt:kt + 1], in1=h_ext[:, kt, 0:CT],
                            op0=ALU.mult, op1=ALU.add)
                    mk(last, sV, "v")
                m_mix = C["v"]

                base_tm = C["t"]
                base_ev = C["s"] if wi < 3 else C["v"]
                @block.tensor
                def _(t, wi=wi, m_mix=m_mix, base_ev=base_ev):
                    t.wait_ge(sV, m_mix)
                    for mt in range(KT):
                        bk = mt % 4
                        if mt < 4:
                            bank_wait(t, bk)
                        else:
                            t.wait_ge(sS if wi < 3 else sV, base_ev + (mt - 4) + 1)
                        last = None
                        for kt in range(KT):
                            last = t.matmul(out=pb[bk][0:128, 0:CT],
                                            lhsT=WT[:, wi, kt, mt, :],
                                            rhs=mixb[wi % 2][:, kt, :],
                                            start=(kt == 0), stop=(kt == KT - 1))
                        mk(last, sT, "t")
                m_mixmm[wi] = C["t"]

                if wi < 3:
                    func = AF.Sigmoid if wi == 0 else AF.Identity
                    dstb = (sgb, rb, kb)[wi]
                    @block.scalar
                    def _(sc, wi=wi, func=func, dstb=dstb, base_tm=base_tm):
                        for mt in range(KT):
                            bk = mt % 4
                            sc.wait_ge(sT, base_tm + mt + 1)
                            m = mk(sc.activation(out=dstb[:, mt, :],
                                                 in_=pb[bk][0:128, 0:CT], func=func,
                                                 bias=bias_sb[:, wi, mt:mt + 1]), sS, "s")
                            bank_mark[bk] = ("s", m)
                else:
                    @block.vector
                    def _(v, base_tm=base_tm, kevm=C["s"]):
                        v.wait_ge(sS, kevm)
                        for mt in range(KT):
                            bk = mt % 4
                            v.wait_ge(sT, base_tm + mt + 1)
                            m = mk(v.scalar_tensor_tensor(
                                out=kvb[:, mt, :], in0=pb[bk][0:128, 0:CT],
                                scalar=bias_sb[:, 3, mt:mt + 1], in1=kb[:, mt, :],
                                op0=ALU.add, op1=ALU.mult), sV, "v")
                            bank_mark[bk] = ("v", m)
            m_kv = C["v"]
            m_kvd[c] = m_kv

            base_sc = C["v"]
            @block.vector
            def _(v, c=c):
                v.wait_ge(sV, m_kv)
                if c == 0:
                    v.wait_ge(sS, m_decrep)
                else:
                    nm, val = m_carry[c - 1]
                    v.wait_ge(SEM[nm], val)
                for kt in range(KT):
                    mk(v.tensor_tensor_scan(
                        out=wkvb[:, kt, :],
                        data0=decv[:, kt:kt + 1].broadcast_to([128, CT]),
                        data1=kvb[:, kt, :],
                        initial=(0.0 if c == 0 else carry_s[:, kt, 0:1]),
                        op0=ALU.mult, op1=ALU.add), sV, "v")
                    mk(v.scalar_tensor_tensor(
                        out=sqb[:, kt, :], in0=kvb[:, kt, :],
                        scalar=uvec[:, kt:kt + 1], in1=wkvb[:, kt, :],
                        op0=ALU.mult, op1=ALU.add), sV, "v")
                    mk(v.tensor_tensor(out=kvb[:, kt, :], in0=sqb[:, kt, :],
                                       in1=rb[:, kt, :], op=ALU.mult), sV, "v")
            m_rw = C["v"]

            base_sq = C["s"]
            @block.scalar
            def _(sc, base_sc=base_sc):
                for kt in range(KT):
                    sc.wait_ge(sV, base_sc + 3 * kt + 3)
                    mk(sc.activation(out=sqb[:, kt, :], in_=kvb[:, kt, :],
                                     func=AF.Square), sS, "s")
            m_sq = C["s"]

            @block.scalar
            def _(sc, c=c, m_rw=m_rw):
                sc.wait_ge(sV, m_rw)
                m_carry[c] = ("s", mk(sc.copy(carry_s[:], wkvb[:, :, CT - 1:CT]), sS, "s"))

            @block.tensor
            def _(t, base_sq=base_sq):
                bank_wait(t, 6)
                bank_wait(t, 7)
                last = None
                for kt in range(KT):
                    t.wait_ge(sS, base_sq + kt + 1)
                    t.matmul(out=pb[6][0:16, 0:CT], lhsT=Gmat[:, kt, :],
                             rhs=kvb[:, kt, :], start=(kt == 0), stop=(kt == KT - 1))
                    last = t.matmul(out=pb[7][0:16, 0:CT], lhsT=Gmat[:, kt, :],
                                    rhs=sqb[:, kt, :], start=(kt == 0), stop=(kt == KT - 1))
                mk(last, sT, "t")
            m_stmm = C["t"]

            @block.vector
            def _(v):
                v.wait_ge(sT, m_stmm)
                v.tensor_copy(out=mean_sb[:], in_=pb[6][0:16, 0:CT])
                m = mk(v.tensor_copy(out=msq_sb[:], in_=pb[7][0:16, 0:CT]), sV, "v")
                bank_mark[6] = ("v", m)
                bank_mark[7] = ("v", m)
                v.tensor_tensor(out=var16[:], in0=mean_sb[:], in1=mean_sb[:], op=ALU.mult)
                mk(v.tensor_tensor(out=var16[:], in0=msq_sb[:], in1=var16[:],
                                   op=ALU.subtract), sV, "v")

            @block.scalar
            def _(sc):
                sc.wait_ge(sV, C["v"])
                mk(sc.activation(out=l16[:], in_=var16[:], func=AF.Ln,
                                 bias=epsv[0:16, 0:1]), sS, "s")

            @block.vector
            def _(v):
                v.wait_ge(sS, C["s"])
                mk(v.tensor_scalar(out=l16[:], in0=l16[:], scalar1=-0.5, scalar2=None,
                                   op0=ALU.mult), sV, "v")

            @block.scalar
            def _(sc):
                sc.wait_ge(sV, C["v"])
                mk(sc.activation(out=r16[:], in_=l16[:], func=AF.Exp), sS, "s")
            m_r16 = C["s"]

            base_tb = C["t"]
            base_vb = C["v"]
            @block.tensor
            def _(t, base_vb=base_vb):
                t.wait_ge(sS, m_r16)
                for kt in range(KT):
                    bkA = 4 + (kt % 2)
                    bkB = 6 + (kt % 2)
                    if kt < 2:
                        bank_wait(t, bkA)
                        bank_wait(t, bkB)
                    else:
                        t.wait_ge(sV, base_vb + (kt - 2) + 1)
                    t.matmul(out=pb[bkA][0:128, 0:CT], lhsT=Emat[:, kt, :],
                             rhs=mean_sb[:], start=True, stop=True)
                    mk(t.matmul(out=pb[bkB][0:128, 0:CT], lhsT=Emat[:, kt, :],
                                rhs=r16[:], start=True, stop=True), sT, "t")

            @block.vector
            def _(v, base_tb=base_tb):
                for kt in range(KT):
                    bkA = 4 + (kt % 2)
                    bkB = 6 + (kt % 2)
                    v.wait_ge(sT, base_tb + kt + 1)
                    v.tensor_tensor(out=tmp1[:], in0=kvb[:, kt, :],
                                    in1=pb[bkA][0:128, 0:CT], op=ALU.subtract)
                    v.tensor_tensor(out=tmp1[:], in0=tmp1[:],
                                    in1=pb[bkB][0:128, 0:CT], op=ALU.mult)
                    m = mk(v.tensor_tensor(out=diffb[:, kt, :], in0=tmp1[:],
                                           in1=sgb[:, kt, :], op=ALU.mult), sV, "v")
                    bank_mark[bkA] = ("v", m)
                    bank_mark[bkB] = ("v", m)
            m_opre = C["v"]

            base_tw = C["t"]
            base_sw = C["s"]
            @block.tensor
            def _(t, base_sw=base_sw):
                t.wait_ge(sV, m_opre)
                for mt in range(KT):
                    bk = mt % 4
                    if mt < 4:
                        bank_wait(t, bk)
                    else:
                        t.wait_ge(sS, base_sw + (mt - 4) + 1)
                    last = None
                    for kt in range(KT):
                        last = t.matmul(out=pb[bk][0:128, 0:CT],
                                        lhsT=WT[:, 4, kt, mt, :],
                                        rhs=diffb[:, kt, :],
                                        start=(kt == 0), stop=(kt == KT - 1))
                    mk(last, sT, "t")

            @block.scalar
            def _(sc, base_tw=base_tw):
                for mt in range(KT):
                    bk = mt % 4
                    sc.wait_ge(sT, base_tw + mt + 1)
                    m = mk(sc.activation(out=sqb[:, mt, :], in_=pb[bk][0:128, 0:CT],
                                         func=AF.Identity,
                                         bias=bias_sb[:, 4, mt:mt + 1]), sS, "s")
                    bank_mark[bk] = ("s", m)
            m_ao = C["s"]

            if c + 1 < CH:
                emit_ln(x_tm[(c + 1) % 2], xn, 0, c + 1,
                        extra_wait_v=m_hx,
                        extra_wait_qs=m_xmiddma.get(c - 1))
                m_ln1d[c + 1] = C["v"]

            base_t10 = C["t"]
            base_v10 = C["v"]
            @block.tensor
            def _(t, base_v10=base_v10):
                t.wait_ge(sS, m_ao)
                for i in range(16):
                    bk = 4 + i % 2
                    tt, kt = divmod(i, KT)
                    if i < 2:
                        bank_wait(t, bk)
                    else:
                        t.wait_ge(sV, base_v10 + (i - 2) + 1)
                    mk(t.transpose(out=pb[bk][0:128, 0:128],
                                   in_=sqb[:, kt, tt * 128:(tt + 1) * 128],
                                   identity=ident[:]), sT, "t")

            @block.vector
            def _(v, base_t10=base_t10, c=c):
                if c > 0:
                    v.wait_ge(sQS, m_xmiddma[c - 1])
                for i in range(16):
                    bk = 4 + i % 2
                    tt, kt = divmod(i, KT)
                    v.wait_ge(sT, base_t10 + i + 1)
                    m = mk(v.tensor_tensor(
                        out=xm_tm[:, tt, kt * 128:(kt + 1) * 128],
                        in0=pb[bk][0:128, 0:128],
                        in1=xcur[:, tt, kt * 128:(kt + 1) * 128],
                        op=ALU.add), sV, "v")
                    bank_mark[bk] = ("v", m)
            m_xm = C["v"]

            @block.sync
            def _(q, c=c, t0=t0):
                q.wait_ge(sV, m_xm)
                m_xmiddma[c] = mk(
                    q.dma_start(out=x_mid[t0:t0 + CT, :].rearrange("(t p) n -> p t n", p=128),
                                in_=xm_tm[:]), sQS, "qs")
                if dbg:
                    mk(q.dma_start(out=dxmid[t0:t0 + CT, :].rearrange("(t p) n -> p t n", p=128),
                                   in_=xm_tm[:]), sQS, "qs")

            emit_ln(xm_tm, xcur, 12, c, extra_wait_v=m_xm)
            m_h2n = C["v"]

            base_t12 = C["t"]
            base_v12 = C["v"]
            @block.tensor
            def _(t, base_v12=base_v12):
                t.wait_ge(sV, m_h2n)
                for i in range(16):
                    bk = 4 + i % 2
                    tt, kt = divmod(i, KT)
                    if i < 2:
                        bank_wait(t, bk)
                    else:
                        t.wait_ge(sV, base_v12 + (i - 2) + 1)
                    mk(t.transpose(out=pb[bk][0:128, 0:128],
                                   in_=xcur[:, tt, kt * 128:(kt + 1) * 128],
                                   identity=ident[:]), sT, "t")

            @block.vector
            def _(v, base_t12=base_t12):
                for i in range(16):
                    bk = 4 + i % 2
                    tt, kt = divmod(i, KT)
                    v.wait_ge(sT, base_t12 + i + 1)
                    m = mk(v.tensor_copy(out=h2f[:, kt, tt * 128:(tt + 1) * 128],
                                         in_=pb[bk][0:128, 0:128]), sV, "v")
                    bank_mark[bk] = ("v", m)
            m_h2f = C["v"]

            @block.vector
            def _(v):
                mk(v.tensor_copy(out=diffb[:], in_=h2f[:]), sV, "v")
            m_h2b = C["v"]

            @block.sync
            def _(q, c=c, t0=t0):
                q.wait_ge(sV, m_h2b)
                g2, off = divmod(t0, FT)
                m_h2dma[c] = mk(q.dma_start(
                    out=h2_loc[g2 * D:(g2 + 1) * D, :].rearrange(
                        "(kt p) n -> p kt n", p=128)[:, :, off:off + CT],
                    in_=diffb[:]), sQS, "qs")
                if dbg:
                    mk(q.dma_start(
                        out=dh2.rearrange("(kt p) n -> p kt n", p=128)[:, :, t0:t0 + CT],
                        in_=h2f[:]), sQS, "qs")

            @block.tensor
            def _(t):
                t.wait_ge(sV, m_h2f)
                bank_wait(t, 6)
                last = None
                for kt in range(KT):
                    last = t.matmul(out=pb[6][0:8, 0:CT], lhsT=gate_sb[:, kt, :],
                                    rhs=h2f[:, kt, :], start=(kt == 0), stop=(kt == KT - 1))
                mk(last, sT, "t")

            @block.vector
            def _(v):
                v.wait_ge(sT, C["t"])
                m = mk(v.tensor_copy(out=lgs[:], in_=pb[6][0:8, 0:CT]), sV, "v")
                bank_mark[6] = ("v", m)
            m_lgs = C["v"]

            for tt in range(2):
                bk = 4 + tt
                @block.tensor
                def _(t, tt=tt, bk=bk):
                    t.wait_ge(sV, m_lgs)
                    bank_wait(t, bk)
                    mk(t.transpose(out=pb[bk][0:128, 0:8],
                                   in_=lgs[0:8, tt * 128:(tt + 1) * 128],
                                   identity=ident[0:8, 0:8]), sT, "t")
            m_ltr = C["t"]

            @block.vector
            def _(v, m_ltr=m_ltr):
                v.wait_ge(sT, m_ltr)
                for tt in range(2):
                    mk(v.tensor_reduce(out=r_sm[:, tt, 0:1], in_=pb[4 + tt][0:128, 0:8],
                                       axis=AX, op=ALU.max, negate=True), sV, "v")

            @block.scalar
            def _(sc, m_ltr=m_ltr):
                sc.wait_ge(sV, C["v"])
                sc.wait_ge(sT, m_ltr)
                for tt in range(2):
                    sc.copy(ltm_s[:, tt, :], pb[4 + tt][0:128, 0:8])
                    mk(sc.mul(r_sm[:, tt, 1:2], r_sm[:, tt, 0:1], -1.0), sS, "s")
                    m = mk(sc.activation(out=edmp[:, tt, :], in_=pb[4 + tt][0:128, 0:8],
                                         func=AF.Exp, bias=r_sm[:, tt, 0:1],
                                         accum_out=r_sm[:, tt, 2:3]), sS, "s")
                    bank_mark[4 + tt] = ("s", m)

            @block.vector
            def _(v):
                v.wait_ge(sS, C["s"])
                for tt in range(2):
                    mk(v.reciprocal(out=r_sm[:, tt, 3:4], in_=r_sm[:, tt, 2:3]), sV, "v")

            @block.scalar
            def _(sc):
                sc.wait_ge(sV, C["v"])
                for tt in range(2):
                    mk(sc.copy(r_sm[:, tt, 4:5], r_sm[:, tt, 3:4]), sS, "s")

            @block.vector
            def _(v):
                v.wait_ge(sS, C["s"])
                for tt in range(2):
                    mk(v.tensor_scalar(out=scale_tm[:, tt, :], in0=ltm_s[:, tt, :],
                                       scalar1=r_sm[:, tt, 1:2],
                                       scalar2=r_sm[:, tt, 4:5],
                                       op0=ALU.is_equal, op1=ALU.mult), sV, "v")
            m_sctm = C["v"]

            @block.tensor
            def _(t, m_sctm=m_sctm):
                t.wait_ge(sV, m_sctm)
                bank_wait(t, 6)
                for tt in range(2):
                    mk(t.transpose(out=pb[6][0:8, tt * 128:(tt + 1) * 128],
                                   in_=scale_tm[:, tt, :],
                                   identity=ident[:]), sT, "t")

            @block.vector
            def _(v, c=c):
                v.wait_ge(sT, C["t"])
                if c > 0:
                    v.wait_ge(sNR, m_scdma[c - 1])
                m = mk(v.tensor_copy(out=scaleT[:], in_=pb[6][0:8, 0:256]), sV, "v")
                bank_mark[6] = ("v", m)

            m_scT = C["v"]

            @block.sync
            def _(q, c=c, t0=t0):
                q.wait_ge(sV, m_scT)
                m_scdma[c] = mk(q.dma_start(out=sc_loc[:, t0:t0 + CT], in_=scaleT[:]),
                                sNR, "nr")
                if dbg:
                    mk(q.dma_start(out=dscale[:, t0:t0 + CT], in_=scaleT[:]), sQS, "qs")

            m_xlast[c] = C["v"]

            if (c % (FT // CT)) == (FT // CT) - 1:
                g2 = c // (FT // CT)
                @block.gpsimd
                def _(g, g2=g2, c=c):
                    g.wait_ge(sQS, m_h2dma[c])
                    with allow_ici_all_to_all():
                        mk(g.collective_compute(
                            "AllGather", ALU.bypass, replica_groups=RG,
                            ins=[h2_loc[g2 * D:(g2 + 1) * D, :].opt()],
                            outs=[h2_all[g2 * NB * D:(g2 + 1) * NB * D, :].opt()]),
                           sCC, "cc")

        @block.gpsimd
        def _(g):
            g.wait_ge(sQS, C["qs"])
            g.wait_ge(sNR, C["nr"])
            with allow_ici_all_to_all():
                mk(g.collective_compute("AllToAll", ALU.bypass, replica_groups=RG,
                                        ins=[sc_loc.opt()], outs=[sc_c.opt()]),
                   sCC, "cc")
        m_ag = C["cc"]

        attn_done_v, attn_done_s, attn_done_t = C["v"], C["s"], C["t"]
        attn_done_g = C["g"]
        aes.close()

        fes = contextlib.ExitStack()
        TF = lambda name, shape, dt: fes.enter_context(nc.sbuf_tensor(name, shape, dt))
        w1b = TF("w1b", [128, KT, DFF], BF16)
        w2b = TF("w2b", [128, 32, D], BF16)
        h2s = TF("h2s", [128, KT, FT], BF16)
        h_sub = TF("h_sub", [128, 32, FT], BF16)
        ych = TF("ych", [128, 2, D], F32)
        sct = TF("sct", [128, 2, 4], F32)
        b1sb = TF("b1sb", [128, 32], F32)
        b2st = TF("b2st", [1, D], F32)
        scrow = b2st[0:1, 0:FT]
        b2row = TF("b2row", [1, D], BF16)
        onesb = TF("onesb", [1, 128], BF16)
        xmt = TF("xmt", [128, D], F32)
        yrt = TF("yrt", [128, D], F32)
        ot = TF("ot", [128, D], F32)
        fstg = TF("fstg", [128, 3, D], F32)

        @block.sync
        def _(q):
            q.wait_ge(sV, attn_done_v)
            q.wait_ge(sS, attn_done_s)
            q.wait_ge(sT, attn_done_t)
            q.wait_ge(sG, attn_done_g)
            mk(q.dma_start(out=b1sb[:], in_=b1_in), sQS, "qs")
            mk(q.dma_start(out=b2st[:], in_=b2_in[None, :]), sNR, "nr")
        m_fb = C["qs"]
        m_b2d = C["nr"]

        @block.gpsimd
        def _(g):
            g.wait_ge(sQS, m_fb)
            mk(g.memset(onesb[:], 1.0), sG, "g")
        m_ones = C["g"]

        @block.scalar
        def _(sc):
            sc.wait_ge(sNR, m_b2d)
            mk(sc.copy(b2row[:], b2st[:]), sS, "s")
        m_b2 = C["s"]

        base_wv = C["v"]

        def wsrc(i):
            if i < 32:
                return w1_in[i * 128:(i + 1) * 128, :]
            return w2_in[(i - 32) * 128:(i - 31) * 128, :]

        def wdst(i):
            if i < 32:
                kt, qr = divmod(i, 4)
                return w1b[:, kt, qr * D:(qr + 1) * D]
            return w2b[:, i - 32, :]

        @block.sync
        def _(q):
            q.wait_ge(sCC, m_ag)
            for i in range(64):
                if i >= 3:
                    q.wait_ge(sV, base_wv + (i - 2))
                mk(q.dma_start(out=fstg[:, i % 3, :], in_=wsrc(i)), sQS, "qs")
            for e in range(2):
                q.wait_ge(sV, base_wv + 62 + e)
                mk(q.dma_start(out=fstg[:, (64 + e) % 3, :], in_=w1_in[0:128, :]),
                   sQS, "qs")
        base_wq = C["qs"] - 66 * 16

        @block.vector
        def _(v):
            for i in range(64):
                v.wait_ge(sQS, base_wq + (i + 3) * 16)
                mk(v.tensor_copy(out=wdst(i), in_=fstg[:, i % 3, :]), sV, "v")
        m_wv = C["v"]
        m_w1v = base_wv + 32

        m_h2s = {}
        m_sct = {}
        m_sctr = {-1: 0}
        m_A_last = {-1: 0}
        m_ydma = {}
        m_ytsd = {}
        m_rs = {}
        fbank = {i: ("v", 0) for i in range(8)}

        def fbank_wait(eng, i):
            nm, val = fbank[i]
            if val > 0:
                eng.wait_ge(SEM[nm], val)

        m_oadd = [0]
        m_odma = [0]

        def emit_final(gidx):
            for tm in range(FT // 128):
                row = gidx * FT + tm * 128
                @block.gpsimd
                def _(g, row=row, gidx=gidx):
                    g.wait_ge(sCC, m_rs[gidx])
                    if m_oadd[0] > 0:
                        g.wait_ge(sV, m_oadd[0])
                    mk(g.dma_start(out=yrt[:], in_=y_red[row:row + 128, :]), sQG, "qg")
                    mk(g.dma_start(out=xmt[:], in_=x_mid[row:row + 128, :]), sQG, "qg")

                @block.vector
                def _(v, row=row):
                    v.wait_ge(sQG, C["qg"])
                    if m_odma[0] > 0:
                        v.wait_ge(sSO, m_odma[0])
                    m_oadd[0] = mk(v.tensor_tensor(out=ot[:], in0=yrt[:], in1=xmt[:],
                                                   op=ALU.add), sV, "v")

                @block.gpsimd
                def _(g, row=row):
                    g.wait_ge(sV, m_oadd[0])
                    m_odma[0] = mk(g.dma_start(out=out_t[row:row + 128, :], in_=ot[:]),
                                   sSO, "so")

        def emit_rs(gidx, wait_h2s=None):
            @block.gpsimd
            def _(g, gidx=gidx):
                for cc2 in range(gidx * NB, (gidx + 1) * NB):
                    g.wait_ge(sQS, m_ydma[(cc2, 1)])
                if wait_h2s is not None:
                    g.wait_ge(sQG, wait_h2s)
                base = gidx * NB * FT
                m_rs[gidx] = mk(g.collective_compute(
                    "ReduceScatter", ALU.add, replica_groups=RG,
                    ins=[y_loc[base:base + NB * FT, :].opt()],
                    outs=[y_red[gidx * FT:(gidx + 1) * FT, :].opt()]), sCC, "cc")

        chunk = 0
        for gidx in range(NG):
            for b in range(NB):
                t0 = gidx * FT
                if b == 2 and gidx > 0:
                    emit_final(gidx - 1)
                @block.gpsimd
                def _(g, gidx=gidx, b=b, chunk=chunk, t0=t0):
                    g.wait_ge(sCC, m_ag)
                    g.wait_ge(sT, m_A_last[chunk - 1])
                    base = (gidx * NB + b) * D
                    m_h2s[chunk] = mk(g.dma_start(
                        out=h2s[:],
                        in_=h2_all[base:base + D, :].rearrange("(kt p) n -> p kt n", p=128)),
                        sQG, "qg")
                    g.wait_ge(sT, m_sctr[chunk - 1])
                    if chunk == 0:
                        g.wait_ge(sS, m_b2)
                    m_sct[chunk] = mk(g.dma_start(
                        out=scrow, in_=sc_c[b:b + 1, t0:t0 + FT]), sNR, "nr")
                if b == 0 and gidx > 0:
                    emit_rs(gidx - 1, wait_h2s=m_h2s[chunk])

                base_t = C["t"]
                base_s = C["s"]
                @block.tensor
                def _(t, chunk=chunk, base_s=base_s):
                    t.wait_ge(sQG, m_h2s[chunk])
                    if chunk == 0:
                        t.wait_ge(sV, m_w1v)
                        t.wait_ge(sG, m_ones)
                    for j in range(32):
                        bk = j % 4
                        if j < 4:
                            fbank_wait(t, bk)
                        else:
                            t.wait_ge(sS, base_s + (j - 4) + 1)
                        last = None
                        for kt in range(KT):
                            last = t.matmul(out=pb[bk][0:128, 0:FT],
                                            lhsT=w1b[:, kt, j * 128:(j + 1) * 128],
                                            rhs=h2s[:, kt, :],
                                            start=(kt == 0), stop=(kt == KT - 1))
                        mk(last, sT, "t")

                @block.scalar
                def _(sc, base_t=base_t):
                    for j in range(32):
                        bk = j % 4
                        sc.wait_ge(sT, base_t + j + 1)
                        m = mk(sc.activation(out=h_sub[:, j, :], in_=pb[bk][0:128, 0:FT],
                                             func=AF.Gelu, bias=b1sb[:, j:j + 1]), sS, "s")
                        fbank[bk] = ("s", m)
                m_A_last[chunk] = C["t"]
                m_gelu_all = C["s"]

                @block.tensor
                def _(t, chunk=chunk):
                    t.wait_ge(sNR, m_sct[chunk])
                    fbank_wait(t, 0)
                    last = None
                    for tm in range(4):
                        last = t.transpose(out=pb[0][0:128, tm:tm + 1],
                                           in_=b2st[0:1, tm * 128:(tm + 1) * 128],
                                           identity=ident[0:1, 0:1])
                    m_sctr[chunk] = mk(last, sT, "t")

                @block.scalar
                def _(sc, chunk=chunk):
                    sc.wait_ge(sT, m_sctr[chunk])
                    if chunk >= 2:
                        sc.wait_ge(sV, m_ytsd[chunk - 2])
                    m = mk(sc.copy(sct[:, chunk % 2, :], pb[0][0:128, 0:4]), sS, "s")
                    fbank[0] = ("s", m)
                m_sctev = C["s"]

                base_t2 = C["t"]
                base_v2 = C["v"]
                @block.tensor
                def _(t, m_gelu_all=m_gelu_all, base_v2=base_v2, chunk=chunk):
                    t.wait_ge(sS, m_gelu_all)
                    t.wait_ge(sS, m_b2)
                    if chunk == 0:
                        t.wait_ge(sV, m_wv)
                    for q2 in range(8):
                        tm, dc = divmod(q2, 2)
                        bk = 4 + q2 % 4
                        if q2 < 4:
                            fbank_wait(t, bk)
                        else:
                            t.wait_ge(sV, base_v2 + (q2 - 4) + 1)
                        for j in range(32):
                            t.matmul(out=pb[bk][0:128, 0:FT],
                                     lhsT=h_sub[:, j, tm * 128:(tm + 1) * 128],
                                     rhs=w2b[:, j, dc * 512:(dc + 1) * 512],
                                     start=(j == 0), stop=False)
                        mk(t.matmul(out=pb[bk][0:128, 0:FT],
                                    lhsT=onesb[:],
                                    rhs=b2row[0:1, dc * 512:(dc + 1) * 512],
                                    start=False, stop=True), sT, "t")

                for half in range(2):
                    @block.vector
                    def _(v, half=half, chunk=chunk, m_sctev=m_sctev,
                          base_t2=base_t2):
                        for q2 in range(half * 4, half * 4 + 4):
                            tm, dc = divmod(q2, 2)
                            bk = 4 + q2 % 4
                            v.wait_ge(sT, base_t2 + q2 + 1)
                            if q2 == 0:
                                v.wait_ge(sS, m_sctev)
                                if chunk >= 1:
                                    v.wait_ge(sQS, m_ydma[(chunk - 1, 1)])
                            elif q2 == 4:
                                v.wait_ge(sQS, m_ydma[(chunk, 0)])
                            m = mk(v.tensor_scalar(
                                out=ych[:, tm % 2, dc * 512:(dc + 1) * 512],
                                in0=pb[bk][0:128, 0:FT],
                                scalar1=sct[:, chunk % 2, tm:tm + 1], scalar2=None,
                                op0=ALU.mult), sV, "v")
                            fbank[bk] = ("v", m)

                    @block.sync
                    def _(q, gidx=gidx, b=b, chunk=chunk, half=half):
                        q.wait_ge(sV, C["v"])
                        base = (gidx * NB + b) * FT + half * 256
                        m_ydma[(chunk, half)] = mk(q.dma_start(
                            out=y_loc[base:base + 256, :].rearrange(
                                "(t p) n -> p t n", p=128),
                            in_=ych[:]), sQS, "qs")
                m_ytsd[chunk] = C["v"]
                chunk += 1

        emit_rs(NG - 1)
        emit_final(NG - 1)

        if dbg:
            @block.sync
            def _(q):
                q.wait_ge(sSO, C["so"])
                mk(q.dma_start(out=dyloc, in_=y_loc), sQS, "qs")
                mk(q.dma_start(out=dyred, in_=y_red), sQS, "qs")
                mk(q.dma_start(out=dw1b, in_=w1b[:].rearrange("p a b -> p (a b)")), sQS, "qs")
                mk(q.dma_start(out=dw2b, in_=w2b[:].rearrange("p a b -> p (a b)")), sQS, "qs")
                mk(q.dma_start(out=db1, in_=b1sb[:]), sQS, "qs")
                mk(q.dma_start(out=db2, in_=b2row[:]), sQS, "qs")
                mk(q.dma_start(out=dhsub, in_=h_sub[:].rearrange("p a b -> p (a b)")), sQS, "qs")

        @block.gpsimd
        def _(g):
            g.wait_ge(sSO, C["so"])
            g.wait_ge(sQS, C["qs"])
            g.wait_ge(sQG, C["qg"])
            g.wait_ge(sCC, C["cc"])

        fes.close()
    return nc


def make_in_maps(inputs, N):
    CH = N // CT
    x = np.asarray(inputs["L0_attn_x"])
    vec = lambda v: np.ascontiguousarray(
        np.asarray(v, np.float32).reshape(KT, 128).T)
    maps = []
    for c in range(NB):
        xt = np.ascontiguousarray(
            x[c, :N].reshape(CH, 2, 128, D).transpose(0, 2, 1, 3).reshape(CH * 128, 2 * D))
        w1t = np.ascontiguousarray(
            np.asarray(inputs["L0_ffn_w1"])[c].reshape(KT, 128, 4, D)
            .transpose(0, 2, 1, 3).reshape(32 * 128, D))
        m = dict(
            x=xt,
            xlast=np.ascontiguousarray(x[c, N - 1:N]),
            Wg=inputs["L0_attn_Wg_w"], bg=vec(inputs["L0_attn_Wg_b"]),
            Wr=inputs["L0_attn_Wr_w"], br=vec(inputs["L0_attn_Wr_b"]),
            Wk=inputs["L0_attn_Wk_w"], bk=vec(inputs["L0_attn_Wk_b"]),
            Wv=inputs["L0_attn_Wv_w"], bv=vec(inputs["L0_attn_Wv_b"]),
            Wo=inputs["L0_attn_Wo_w"], bo=vec(inputs["L0_attn_Wo_b"]),
            mg=vec(inputs["L0_attn_mg"]), mr=vec(inputs["L0_attn_mr"]),
            mk=vec(inputs["L0_attn_mk"]), mv=vec(inputs["L0_attn_mv"]),
            wdec=vec(np.asarray(inputs["L0_attn_w"]).reshape(-1)),
            uvec=vec(np.asarray(inputs["L0_attn_u"]).reshape(-1)),
            gate=np.ascontiguousarray(
                np.asarray(inputs["L0_ffn_gate_w"], np.float32)
                .reshape(KT, 128, NEXP).transpose(1, 0, 2)),
            w1e=w1t,
            b1e=np.ascontiguousarray(
                np.asarray(inputs["L0_ffn_b1"])[c].reshape(32, 128).T),
            w2e=np.asarray(inputs["L0_ffn_w2"])[c],
            b2e=np.asarray(inputs["L0_ffn_b2"])[c],
        )
        maps.append({k: np.ascontiguousarray(np.asarray(v), dtype=np.float32)
                     for k, v in m.items()})
    return maps

def kernel(**inputs):
    nc = build_program(N=4096, dbg=False)
    in_maps = make_in_maps(inputs, 4096)
    res = run_bass_kernel_spmd(nc, in_maps, core_ids=list(range(8)))
    return np.stack([np.asarray(res.results[c]["out"]) for c in range(8)])


# version 11
# speedup vs baseline: 1.0323x; 1.0037x over previous
import contextlib
import numpy as np
import sys
for _p in ("/opt/trn_rl_repo",):
    if _p not in sys.path:
        sys.path.append(_p)
import concourse.bass as bass
import concourse.mybir as mybir
from concourse.bass_utils import run_bass_kernel_spmd
from concourse.aot_env import allow_ici_all_to_all

F32 = mybir.dt.float32
BF16 = mybir.dt.bfloat16
ALU = mybir.AluOpType
AF = mybir.ActivationFunctionType
AX = mybir.AxisListType.X
RG = [[0, 1, 2, 3, 4, 5, 6, 7]]

D = 1024
KT = 8
CT = 256
FT = 512
DFF = 4096
NEXP = 8
NB = 8


def build_program(N=4096, dbg=False):
    CH = N // CT
    NG = N // FT
    nc = bass.Bass("TRN2", target_bir_lowering=False)

    x_in = nc.dram_tensor("x", [CH * 128, 2 * D], F32, kind="ExternalInput").ap()
    xlast_in = nc.dram_tensor("xlast", [1, D], F32, kind="ExternalInput").ap()
    W_in, b_in, m_in = {}, {}, {}
    for wi, nm in enumerate("grkvo"):
        W_in[wi] = nc.dram_tensor(f"W{nm}", [D, D], F32, kind="ExternalInput").ap()
        b_in[wi] = nc.dram_tensor(f"b{nm}", [128, KT], F32, kind="ExternalInput").ap()
    for wi, nm in enumerate("grkv"):
        m_in[wi] = nc.dram_tensor(f"m{nm}", [128, KT], F32, kind="ExternalInput").ap()
    wdec_in = nc.dram_tensor("wdec", [128, KT], F32, kind="ExternalInput").ap()
    u_in = nc.dram_tensor("uvec", [128, KT], F32, kind="ExternalInput").ap()
    gate_in = nc.dram_tensor("gate", [128, KT, NEXP], F32, kind="ExternalInput").ap()
    w1_in = nc.dram_tensor("w1e", [32 * 128, D], F32, kind="ExternalInput").ap()
    b1_in = nc.dram_tensor("b1e", [128, 32], F32, kind="ExternalInput").ap()
    w2_in = nc.dram_tensor("w2e", [DFF, D], F32, kind="ExternalInput").ap()
    b2_in = nc.dram_tensor("b2e", [D], F32, kind="ExternalInput").ap()
    out_t = nc.dram_tensor("out", [N, D], F32, kind="ExternalOutput").ap()
    if dbg:
        dxmid = nc.dram_tensor("dxmid", [N, D], F32, kind="ExternalOutput").ap()
        dh2 = nc.dram_tensor("dh2", [D, N], F32, kind="ExternalOutput").ap()
        dscale = nc.dram_tensor("dscale", [NEXP, N], F32, kind="ExternalOutput").ap()
        dyloc = nc.dram_tensor("dyloc", [NB * N, D], F32, kind="ExternalOutput").ap()
        dyred = nc.dram_tensor("dyred", [N, D], F32, kind="ExternalOutput").ap()
        dw1b = nc.dram_tensor("dw1b", [128, KT * DFF], BF16, kind="ExternalOutput").ap()
        dw2b = nc.dram_tensor("dw2b", [128, 32 * D], BF16, kind="ExternalOutput").ap()
        db1 = nc.dram_tensor("db1", [128, 32], F32, kind="ExternalOutput").ap()
        db2 = nc.dram_tensor("db2", [1, D], BF16, kind="ExternalOutput").ap()
        dhsub = nc.dram_tensor("dhsub", [128, 32 * FT], BF16, kind="ExternalOutput").ap()

    x_mid = nc.dram_tensor("x_mid", [N, D], F32, kind="Internal").ap()
    h2_loc = nc.dram_tensor("h2_loc", [NG * D, FT], BF16, kind="Internal").ap()
    h2_all = nc.dram_tensor("h2_all", [NB * NG * D, FT], BF16, kind="Internal",
                            addr_space="Local").ap()
    sc_loc = nc.dram_tensor("sc_loc", [NEXP, N], F32, kind="Internal").ap()
    sc_c = nc.dram_tensor("sc_c", [NB, N], F32, kind="Internal",
                          addr_space="Local").ap()
    y_loc = nc.dram_tensor("y_loc", [NB * N, D], F32, kind="Internal").ap()
    y_red = nc.dram_tensor("y_red", [N, D], F32, kind="Internal",
                           addr_space="Local").ap()

    C = {}

    def mk(instr, sem, key, n=None):
        if n is None:
            n = 16 if key in ("qs", "qg", "so", "nr") else 1
        instr.then_inc(sem, n)
        C[key] = C.get(key, 0) + n
        return C[key]

    es = contextlib.ExitStack()
    with es:
        pb = [es.enter_context(nc.psum_tensor(f"pb{i}", [128, 512], F32)) for i in range(8)]
        epsv = es.enter_context(nc.sbuf_tensor("epsv", [128, 1], F32))
        ident = es.enter_context(nc.sbuf_tensor("ident", [128, 128], F32))
        sQS = es.enter_context(nc.semaphore("qs"))
        sQG = es.enter_context(nc.semaphore("qg"))
        sT = es.enter_context(nc.semaphore("t"))
        sV = es.enter_context(nc.semaphore("v"))
        sS = es.enter_context(nc.semaphore("s"))
        sG = es.enter_context(nc.semaphore("g"))
        sCC = es.enter_context(nc.semaphore("cc"))
        sSO = es.enter_context(nc.semaphore("so"))
        sNR = es.enter_context(nc.semaphore("nr"))
        SEM = {"v": sV, "s": sS, "t": sT, "g": sG, "qs": sQS, "qg": sQG,
               "cc": sCC, "so": sSO, "nr": sNR}

        aes = contextlib.ExitStack()
        TA = lambda name, shape, dt: aes.enter_context(nc.sbuf_tensor(name, shape, dt))
        WT = TA("WT", [128, 5, KT, KT, 128], BF16)
        bias_sb = TA("bias_sb", [128, 5, KT], F32)
        mvec = TA("mvec", [128, 4, KT], F32)
        uvec = TA("uvec_sb", [128, KT], F32)
        e1v = TA("e1v", [128, KT], F32)
        ngv = TA("ngv", [128, KT], F32)
        decv = TA("decv", [128, KT], F32)
        Gmat = TA("Gmat", [128, KT, 16], F32)
        Emat = TA("Emat", [16, KT, 128], F32)
        gate_sb = TA("gate_sb", [128, KT, NEXP], F32)
        h_bnd = TA("h_bnd", [128, KT, 1], F32)
        carry_s = TA("carry_s", [128, KT, 1], F32)
        ses = contextlib.ExitStack()
        wstg = ses.enter_context(nc.sbuf_tensor("wstg", [128, 2, D], F32))
        xrow = ses.enter_context(nc.sbuf_tensor("xrow", [1, D], F32))
        rsm = ses.enter_context(nc.sbuf_tensor("rsm", [1, 12], F32))
        Gp = ses.enter_context(nc.sbuf_tensor("Gp", [128, KT, 16], F32))
        rdmp = ses.enter_context(nc.sbuf_tensor("rdmp", [1, D], F32))

        block = es.enter_context(nc.Block())

        bank_mark = {i: ("v", 0) for i in range(8)}

        def bank_wait(eng, i):
            nm, val = bank_mark[i]
            if val > 0:
                eng.wait_ge(SEM[nm], val)

        @block.gpsimd
        def _(g):
            mk(g.memset(epsv[:], 1e-5), sG, "g")
            mk(g.memset(ident[:], 0.0), sG, "g")
            g.wait_ge(sG, C["g"])
            mk(g.affine_select(out=ident[:], in_=ident[:], compare_op=ALU.not_equal,
                               fill=1.0, base=0, pattern=[[-1, 128]],
                               channel_multiplier=1), sG, "g")
            mk(g.memset(Gmat[:], 0.0), sG, "g")
            mk(g.memset(Gp[:], 0.0), sG, "g")
            g.wait_ge(sG, C["g"])
            for kt in range(KT):
                mk(g.memset(Gmat[0:64, kt, 2 * kt:2 * kt + 1], 1.0 / 64), sG, "g")
                mk(g.memset(Gmat[64:128, kt, 2 * kt + 1:2 * kt + 2], 1.0 / 64),
                   sG, "g")
                mk(g.memset(Gp[0:64, kt, 2 * kt:2 * kt + 1], 1.0), sG, "g")
            for kt in range(KT):
                mk(g.memset(Gp[64:128, kt, 2 * kt + 1:2 * kt + 2], 1.0), sG, "g")
        m_setup_g = C["g"]

        @block.sync
        def _(q):
            for wi in range(5):
                mk(q.dma_start(out=bias_sb[:, wi, :], in_=b_in[wi]), sQS, "qs")
            for wi in range(4):
                mk(q.dma_start(out=mvec[:, wi, :], in_=m_in[wi]), sQS, "qs")
            mk(q.dma_start(out=uvec[:], in_=u_in), sQS, "qs")
            mk(q.dma_start(out=e1v[:], in_=wdec_in), sQS, "qs")
            mk(q.dma_start(out=gate_sb[:], in_=gate_in), sQS, "qs")
            mk(q.dma_start(out=xrow[:], in_=xlast_in), sNR, "nr")
        m_vecs = C["qs"]
        m_xrow = C["nr"]

        @block.scalar
        def _(sc):
            sc.wait_ge(sQS, m_vecs)
            mk(sc.activation(out=decv[:], in_=e1v[:], func=AF.Exp), sS, "s")

        @block.vector
        def _(v):
            v.wait_ge(sS, C["s"])
            mk(v.tensor_scalar(out=ngv[:], in0=decv[:], scalar1=-1.0, scalar2=None,
                               op0=ALU.mult), sV, "v")

        @block.scalar
        def _(sc):
            sc.wait_ge(sV, C["v"])
            mk(sc.activation(out=decv[:], in_=ngv[:], func=AF.Exp), sS, "s")
        m_dec = C["s"]

        m_decrep = m_dec

        for kt in range(KT):
            bk = kt % 2
            @block.tensor
            def _(t, kt=kt, bk=bk):
                t.wait_ge(sG, m_setup_g)
                bank_wait(t, bk)
                mk(t.transpose(out=pb[bk][0:16, 0:128], in_=Gp[:, kt, :],
                               identity=ident[:]), sT, "t")

            @block.vector
            def _(v, kt=kt, bk=bk):
                v.wait_ge(sT, C["t"])
                m = mk(v.tensor_copy(out=Emat[:, kt, :], in_=pb[bk][0:16, 0:128]),
                       sV, "v")
                bank_mark[bk] = ("v", m)

        wstg_guard = {}

        @block.sync
        def _(q):
            q.wait_ge(sQS, m_vecs)
            mk(q.dma_start(out=wstg[:, 0, :], in_=W_in[0][0:128, :]), sQS, "qs")
        m_wstg0 = C["qs"]

        for idx in range(5 * KT):
            wi, mt = divmod(idx, KT)
            @block.sync
            def _(q, idx=idx):
                if idx + 1 < 5 * KT:
                    if idx >= 1:
                        q.wait_ge(sT, wstg_guard[idx - 1])
                    else:
                        q.wait_ge(sQS, m_wstg0)
                    wi2, mt2 = divmod(idx + 1, KT)
                    mk(q.dma_start(out=wstg[:, (idx + 1) % 2, :],
                                   in_=W_in[wi2][mt2 * 128:(mt2 + 1) * 128, :]),
                       sQS, "qs")
                else:
                    q.wait_ge(sT, wstg_guard[idx - 1])
                    mk(q.dma_start(out=wstg[:, (idx + 1) % 2, :],
                                   in_=W_in[0][0:128, :]), sQS, "qs")
            m_wdma = C["qs"]
            for kt2 in range(KT):
                bk = 2 + kt2 % 2
                @block.tensor
                def _(t, wi=wi, mt=mt, idx=idx, kt2=kt2, bk=bk, m_wdma=m_wdma):
                    if kt2 == 0:
                        t.wait_ge(sQS, m_wdma)
                    bank_wait(t, bk)
                    m = mk(t.transpose(out=pb[bk][0:128, 0:128],
                                       in_=wstg[:, idx % 2, kt2 * 128:(kt2 + 1) * 128],
                                       identity=ident[:]), sT, "t")
                    if kt2 == KT - 1:
                        wstg_guard[idx] = m

                @block.vector
                def _(v, wi=wi, mt=mt, kt2=kt2, bk=bk):
                    v.wait_ge(sT, C["t"])
                    m = mk(v.tensor_copy(out=WT[:, wi, kt2, mt, :],
                                         in_=pb[bk][0:128, 0:128]), sV, "v")
                    bank_mark[bk] = ("v", m)
        m_wt = C["v"]

        @block.vector
        def _(v):
            v.wait_ge(sQS, m_vecs)
            v.wait_ge(sNR, m_xrow)
            mk(v.tensor_reduce(out=rsm[:, 0:1], in_=xrow[:], axis=AX, op=ALU.add),
               sV, "v")

        @block.scalar
        def _(sc):
            sc.wait_ge(sV, C["v"])
            mk(sc.mul(rsm[:, 1:2], rsm[:, 0:1], 1.0 / D), sS, "s")
            mk(sc.activation(out=rdmp[:], in_=xrow[:], func=AF.Square,
                             accum_out=rsm[:, 2:3]), sS, "s")

        @block.vector
        def _(v):
            v.wait_ge(sS, C["s"])
            v.tensor_tensor(out=rsm[:, 3:4], in0=rsm[:, 1:2], in1=rsm[:, 1:2], op=ALU.mult)
            mk(v.tensor_scalar(out=rsm[:, 4:5], in0=rsm[:, 2:3], scalar1=1.0 / D,
                               scalar2=None, op0=ALU.mult), sV, "v")

        @block.gpsimd
        def _(g):
            g.wait_ge(sV, C["v"])
            mk(g.tensor_tensor(out=rsm[:, 5:6], in0=rsm[:, 4:5], in1=rsm[:, 3:4],
                               op=ALU.subtract), sG, "g")

        @block.scalar
        def _(sc):
            sc.wait_ge(sG, C["g"])
            mk(sc.activation(out=rsm[:, 6:7], in_=rsm[:, 5:6], func=AF.Ln,
                             bias=epsv[0:1, 0:1]), sS, "s")

        @block.vector
        def _(v):
            v.wait_ge(sS, C["s"])
            mk(v.tensor_scalar(out=rsm[:, 7:8], in0=rsm[:, 6:7], scalar1=-0.5,
                               scalar2=None, op0=ALU.mult), sV, "v")

        @block.scalar
        def _(sc):
            sc.wait_ge(sV, C["v"])
            mk(sc.activation(out=rsm[:, 8:9], in_=rsm[:, 7:8], func=AF.Exp), sS, "s")

        @block.vector
        def _(v):
            v.wait_ge(sS, C["s"])
            mk(v.tensor_scalar(out=rdmp[:], in0=xrow[:], scalar1=rsm[:, 1:2],
                               scalar2=rsm[:, 8:9], op0=ALU.subtract, op1=ALU.mult),
               sV, "v")
        m_rowln = C["v"]

        for kt in range(KT):
            bk = 4 + kt % 2
            @block.tensor
            def _(t, kt=kt, bk=bk):
                t.wait_ge(sV, m_rowln)
                bank_wait(t, bk)
                mk(t.transpose(out=pb[bk][0:128, kt:kt + 1],
                               in_=rdmp[0:1, kt * 128:(kt + 1) * 128],
                               identity=ident[0:1, 0:1]), sT, "t")

            @block.vector
            def _(v, kt=kt, bk=bk):
                v.wait_ge(sT, C["t"])
                m = mk(v.tensor_copy(out=h_bnd[:, kt, :], in_=pb[bk][0:128, kt:kt + 1]),
                       sV, "v")
                bank_mark[bk] = ("v", m)
        m_hbnd_init = ("v", C["v"])
        m_setup_done_v = C["v"]

        ses.close()
        x_tm = [TA(f"x_tm{i}", [128, 2, D], F32) for i in range(2)]
        xn = TA("xn", [128, 2, D], F32)
        xm_tm = TA("xm_tm", [128, 2, D], F32)
        dmp = TA("dmp", [128, 2, D], F32)
        sm = TA("sm", [128, 2, 24], F32)
        h_ext = TA("h_ext", [128, KT, CT + 1], F32)
        diffb = TA("diffb", [128, KT, CT], BF16)
        mixb = [TA(f"mixb{i}", [128, KT, CT], BF16) for i in range(2)]
        sgb = TA("sgb", [128, KT, CT], BF16)
        rb = TA("rb", [128, KT, CT], F32)
        kb = TA("kb", [128, KT, CT], F32)
        kvb = TA("kvb", [128, KT, CT], F32)
        wkvb = TA("wkvb", [128, KT, CT], F32)
        sqb = TA("sqb", [128, KT, CT], F32)
        tmp1 = TA("tmp1", [128, CT], F32)
        h2f = TA("h2f", [128, KT, CT], F32)
        mean_sb = TA("mean_sb", [16, CT], F32)
        msq_sb = TA("msq_sb", [16, CT], F32)
        var16 = TA("var16", [16, CT], F32)
        l16 = TA("l16", [16, CT], F32)
        r16 = var16
        ltm_s = TA("ltm_s", [128, 2, 8], F32)
        r_sm = TA("r_sm", [128, 2, 6], F32)
        edmp = TA("edmp", [128, 2, 8], F32)
        scale_tm = TA("scale_tm", [128, 2, 8], F32)
        scaleT = TA("scaleT", [8, CT], F32)
        lgs = scaleT

        @block.gpsimd
        def _(g):
            g.wait_ge(sV, m_setup_done_v)
            mk(g.dma_start(out=x_tm[0][:].rearrange("p a b -> p (a b)"),
                           in_=x_in[0:128, :]), sQG, "qg")

        m_x = {0: C["qg"]}
        m_xlast = {}
        m_carry = {}
        m_hbnd = {-1: m_hbnd_init}
        m_h2dma = {}
        m_scdma = {}
        m_xmiddma = {}
        m_kvd = {}

        def emit_ln(src, dst, slot, c, extra_wait_v=None, extra_wait_qs=None):
            @block.vector
            def _(v):
                v.wait_ge(sQG, m_x[c])
                if extra_wait_v is not None:
                    v.wait_ge(sV, extra_wait_v)
                if extra_wait_qs is not None:
                    v.wait_ge(sQS, extra_wait_qs)
                for tt in range(2):
                    mk(v.tensor_reduce(out=sm[:, tt, slot:slot + 1],
                                       in_=src[:, tt, :], axis=AX, op=ALU.add), sV, "v")

            @block.scalar
            def _(sc):
                sc.wait_ge(sV, C["v"])
                sc.wait_ge(sQG, m_x[c])
                for tt in range(2):
                    mk(sc.mul(sm[:, tt, slot + 1:slot + 2],
                              sm[:, tt, slot:slot + 1], 1.0 / D), sS, "s")
                    mk(sc.activation(out=dmp[:, tt, :], in_=src[:, tt, :],
                                     func=AF.Square,
                                     accum_out=sm[:, tt, slot + 2:slot + 3]), sS, "s")

            @block.vector
            def _(v):
                v.wait_ge(sS, C["s"])
                for tt in range(2):
                    v.tensor_tensor(out=sm[:, tt, slot + 3:slot + 4],
                                    in0=sm[:, tt, slot + 1:slot + 2],
                                    in1=sm[:, tt, slot + 1:slot + 2], op=ALU.mult)
                    mk(v.tensor_scalar(out=sm[:, tt, slot + 4:slot + 5],
                                       in0=sm[:, tt, slot + 2:slot + 3],
                                       scalar1=1.0 / D, scalar2=None, op0=ALU.mult),
                       sV, "v")

            @block.gpsimd
            def _(g):
                g.wait_ge(sV, C["v"])
                for tt in range(2):
                    mk(g.tensor_tensor(out=sm[:, tt, slot + 5:slot + 6],
                                       in0=sm[:, tt, slot + 4:slot + 5],
                                       in1=sm[:, tt, slot + 3:slot + 4],
                                       op=ALU.subtract), sG, "g")

            @block.scalar
            def _(sc):
                sc.wait_ge(sG, C["g"])
                for tt in range(2):
                    mk(sc.activation(out=sm[:, tt, slot + 6:slot + 7],
                                     in_=sm[:, tt, slot + 5:slot + 6],
                                     func=AF.Ln, bias=epsv[:]), sS, "s")

            @block.vector
            def _(v):
                v.wait_ge(sS, C["s"])
                for tt in range(2):
                    mk(v.tensor_scalar(out=sm[:, tt, slot + 7:slot + 8],
                                       in0=sm[:, tt, slot + 6:slot + 7],
                                       scalar1=-0.5, scalar2=None, op0=ALU.mult), sV, "v")

            @block.scalar
            def _(sc):
                sc.wait_ge(sV, C["v"])
                for tt in range(2):
                    mk(sc.activation(out=sm[:, tt, slot + 8:slot + 9],
                                     in_=sm[:, tt, slot + 7:slot + 8],
                                     func=AF.Exp), sS, "s")

            @block.vector
            def _(v):
                v.wait_ge(sS, C["s"])
                for tt in range(2):
                    mk(v.tensor_scalar(out=dst[:, tt, :], in0=src[:, tt, :],
                                       scalar1=sm[:, tt, slot + 1:slot + 2],
                                       scalar2=sm[:, tt, slot + 8:slot + 9],
                                       op0=ALU.subtract, op1=ALU.mult), sV, "v")

        m_ln1d = {}
        emit_ln(x_tm[0], xn, 0, 0)
        m_ln1d[0] = C["v"]

        for c in range(CH):
            t0 = c * CT
            if c + 1 < CH:
                @block.gpsimd
                def _(g, c=c):
                    if c >= 1:
                        g.wait_ge(sV, m_xlast.get(c - 1, 0))
                    mk(g.dma_start(
                        out=x_tm[(c + 1) % 2][:].rearrange("p a b -> p (a b)"),
                        in_=x_in[(c + 1) * 128:(c + 2) * 128, :]), sQG, "qg")
                    m_x[c + 1] = C["qg"]

            xcur = x_tm[c % 2]
            m_ln1 = m_ln1d[c]

            @block.gpsimd
            def _(g, c=c):
                nm, val = m_hbnd[c - 1]
                g.wait_ge(SEM[nm], val)
                if c > 0:
                    g.wait_ge(sV, m_kvd[c - 1])
                mk(g.tensor_copy(out=h_ext[:, :, 0:1], in_=h_bnd[:]), sG, "g")
            m_bcol = C["g"]

            base_t = C["t"]
            base_v = C["v"]
            @block.tensor
            def _(t, base_v=base_v):
                t.wait_ge(sV, m_ln1)
                for i in range(16):
                    bk = 4 + i % 2
                    if i < 2:
                        bank_wait(t, bk)
                    else:
                        t.wait_ge(sV, base_v + (i - 2) + 1)
                    tt, kt = divmod(i, KT)
                    mk(t.transpose(out=pb[bk][0:128, 0:128],
                                   in_=xn[:, tt, kt * 128:(kt + 1) * 128],
                                   identity=ident[:]), sT, "t")

            @block.vector
            def _(v, c=c, base_t=base_t):
                if c > 0:
                    nm, val = m_hbnd[c - 1]
                    v.wait_ge(SEM[nm], val)
                for i in range(16):
                    bk = 4 + i % 2
                    tt, kt = divmod(i, KT)
                    v.wait_ge(sT, base_t + i + 1)
                    m = mk(v.tensor_copy(
                        out=h_ext[:, kt, 1 + tt * 128:1 + (tt + 1) * 128],
                        in_=pb[bk][0:128, 0:128]), sV, "v")
                    bank_mark[bk] = ("v", m)
            m_hx = C["v"]

            @block.scalar
            def _(sc, c=c, m_bcol=m_bcol):
                sc.wait_ge(sV, m_hx)
                sc.wait_ge(sG, m_bcol)
                m_hbnd[c] = ("s", mk(sc.copy(h_bnd[:], h_ext[:, :, CT:CT + 1]), sS, "s"))

            @block.vector
            def _(v, c=c, m_bcol=m_bcol, m_hx=m_hx):
                v.wait_ge(sG, m_bcol)
                v.wait_ge(sV, m_hx)
                if c == 0:
                    v.wait_ge(sV, m_wt)
                else:
                    v.wait_ge(sQS, m_h2dma[c - 1])
                mk(v.tensor_tensor(out=sqb[:], in0=h_ext[:, :, 1:CT + 1],
                                   in1=h_ext[:, :, 0:CT], op=ALU.subtract), sV, "v")
            m_diff = C["v"]

            m_mixmm = {}
            for wi in range(4):
                @block.vector
                def _(v, wi=wi):
                    v.wait_ge(sV, m_diff)
                    if wi >= 2:
                        v.wait_ge(sT, m_mixmm[wi - 2])
                    last = None
                    for kt in range(KT):
                        last = v.scalar_tensor_tensor(
                            out=mixb[wi % 2][:, kt, :], in0=sqb[:, kt, :],
                            scalar=mvec[:, wi, kt:kt + 1], in1=h_ext[:, kt, 0:CT],
                            op0=ALU.mult, op1=ALU.add)
                    mk(last, sV, "v")
                m_mix = C["v"]

                base_tm = C["t"]
                base_ev = C["s"] if wi < 3 else C["v"]
                @block.tensor
                def _(t, wi=wi, m_mix=m_mix, base_ev=base_ev):
                    t.wait_ge(sV, m_mix)
                    for mt in range(KT):
                        bk = mt % 4
                        if mt < 4:
                            bank_wait(t, bk)
                        else:
                            t.wait_ge(sS if wi < 3 else sV, base_ev + (mt - 4) + 1)
                        last = None
                        for kt in range(KT):
                            last = t.matmul(out=pb[bk][0:128, 0:CT],
                                            lhsT=WT[:, wi, kt, mt, :],
                                            rhs=mixb[wi % 2][:, kt, :],
                                            start=(kt == 0), stop=(kt == KT - 1))
                        mk(last, sT, "t")
                m_mixmm[wi] = C["t"]

                if wi < 3:
                    func = AF.Sigmoid if wi == 0 else AF.Identity
                    dstb = (sgb, rb, kb)[wi]
                    @block.scalar
                    def _(sc, wi=wi, func=func, dstb=dstb, base_tm=base_tm):
                        for mt in range(KT):
                            bk = mt % 4
                            sc.wait_ge(sT, base_tm + mt + 1)
                            m = mk(sc.activation(out=dstb[:, mt, :],
                                                 in_=pb[bk][0:128, 0:CT], func=func,
                                                 bias=bias_sb[:, wi, mt:mt + 1]), sS, "s")
                            bank_mark[bk] = ("s", m)
                else:
                    @block.vector
                    def _(v, base_tm=base_tm, kevm=C["s"]):
                        v.wait_ge(sS, kevm)
                        for mt in range(KT):
                            bk = mt % 4
                            v.wait_ge(sT, base_tm + mt + 1)
                            m = mk(v.scalar_tensor_tensor(
                                out=kvb[:, mt, :], in0=pb[bk][0:128, 0:CT],
                                scalar=bias_sb[:, 3, mt:mt + 1], in1=kb[:, mt, :],
                                op0=ALU.add, op1=ALU.mult), sV, "v")
                            bank_mark[bk] = ("v", m)
            m_kv = C["v"]
            m_kvd[c] = m_kv

            base_sc = C["v"]
            @block.vector
            def _(v, c=c):
                v.wait_ge(sV, m_kv)
                if c == 0:
                    v.wait_ge(sS, m_decrep)
                else:
                    nm, val = m_carry[c - 1]
                    v.wait_ge(SEM[nm], val)
                for kt in range(KT):
                    mk(v.tensor_tensor_scan(
                        out=wkvb[:, kt, :],
                        data0=decv[:, kt:kt + 1].broadcast_to([128, CT]),
                        data1=kvb[:, kt, :],
                        initial=(0.0 if c == 0 else carry_s[:, kt, 0:1]),
                        op0=ALU.mult, op1=ALU.add), sV, "v")
                    mk(v.scalar_tensor_tensor(
                        out=sqb[:, kt, :], in0=kvb[:, kt, :],
                        scalar=uvec[:, kt:kt + 1], in1=wkvb[:, kt, :],
                        op0=ALU.mult, op1=ALU.add), sV, "v")
                    mk(v.tensor_tensor(out=kvb[:, kt, :], in0=sqb[:, kt, :],
                                       in1=rb[:, kt, :], op=ALU.mult), sV, "v")
            m_rw = C["v"]

            base_sq = C["s"]
            @block.scalar
            def _(sc, base_sc=base_sc):
                for kt in range(KT):
                    sc.wait_ge(sV, base_sc + 3 * kt + 3)
                    mk(sc.activation(out=sqb[:, kt, :], in_=kvb[:, kt, :],
                                     func=AF.Square), sS, "s")
            m_sq = C["s"]

            @block.scalar
            def _(sc, c=c, m_rw=m_rw):
                sc.wait_ge(sV, m_rw)
                m_carry[c] = ("s", mk(sc.copy(carry_s[:], wkvb[:, :, CT - 1:CT]), sS, "s"))

            @block.tensor
            def _(t, base_sq=base_sq):
                bank_wait(t, 6)
                bank_wait(t, 7)
                last = None
                for kt in range(KT):
                    t.wait_ge(sS, base_sq + kt + 1)
                    t.matmul(out=pb[6][0:16, 0:CT], lhsT=Gmat[:, kt, :],
                             rhs=kvb[:, kt, :], start=(kt == 0), stop=(kt == KT - 1))
                    last = t.matmul(out=pb[7][0:16, 0:CT], lhsT=Gmat[:, kt, :],
                                    rhs=sqb[:, kt, :], start=(kt == 0), stop=(kt == KT - 1))
                mk(last, sT, "t")
            m_stmm = C["t"]

            @block.vector
            def _(v):
                v.wait_ge(sT, m_stmm)
                v.tensor_copy(out=mean_sb[:], in_=pb[6][0:16, 0:CT])
                m = mk(v.tensor_copy(out=msq_sb[:], in_=pb[7][0:16, 0:CT]), sV, "v")
                bank_mark[6] = ("v", m)
                bank_mark[7] = ("v", m)
                v.tensor_tensor(out=var16[:], in0=mean_sb[:], in1=mean_sb[:], op=ALU.mult)
                mk(v.tensor_tensor(out=var16[:], in0=msq_sb[:], in1=var16[:],
                                   op=ALU.subtract), sV, "v")

            @block.scalar
            def _(sc):
                sc.wait_ge(sV, C["v"])
                mk(sc.activation(out=l16[:], in_=var16[:], func=AF.Ln,
                                 bias=epsv[0:16, 0:1]), sS, "s")

            @block.vector
            def _(v):
                v.wait_ge(sS, C["s"])
                mk(v.tensor_scalar(out=l16[:], in0=l16[:], scalar1=-0.5, scalar2=None,
                                   op0=ALU.mult), sV, "v")

            @block.scalar
            def _(sc):
                sc.wait_ge(sV, C["v"])
                mk(sc.activation(out=r16[:], in_=l16[:], func=AF.Exp), sS, "s")
            m_r16 = C["s"]

            base_tb = C["t"]
            base_vb = C["v"]
            @block.tensor
            def _(t, base_vb=base_vb):
                t.wait_ge(sS, m_r16)
                for kt in range(KT):
                    bkA = 4 + (kt % 2)
                    bkB = 6 + (kt % 2)
                    if kt < 2:
                        bank_wait(t, bkA)
                        bank_wait(t, bkB)
                    else:
                        t.wait_ge(sV, base_vb + (kt - 2) + 1)
                    t.matmul(out=pb[bkA][0:128, 0:CT], lhsT=Emat[:, kt, :],
                             rhs=mean_sb[:], start=True, stop=True)
                    mk(t.matmul(out=pb[bkB][0:128, 0:CT], lhsT=Emat[:, kt, :],
                                rhs=r16[:], start=True, stop=True), sT, "t")

            @block.vector
            def _(v, base_tb=base_tb):
                for kt in range(KT):
                    bkA = 4 + (kt % 2)
                    bkB = 6 + (kt % 2)
                    v.wait_ge(sT, base_tb + kt + 1)
                    v.tensor_tensor(out=tmp1[:], in0=kvb[:, kt, :],
                                    in1=pb[bkA][0:128, 0:CT], op=ALU.subtract)
                    v.tensor_tensor(out=tmp1[:], in0=tmp1[:],
                                    in1=pb[bkB][0:128, 0:CT], op=ALU.mult)
                    m = mk(v.tensor_tensor(out=diffb[:, kt, :], in0=tmp1[:],
                                           in1=sgb[:, kt, :], op=ALU.mult), sV, "v")
                    bank_mark[bkA] = ("v", m)
                    bank_mark[bkB] = ("v", m)
            m_opre = C["v"]

            base_tw = C["t"]
            base_sw = C["s"]
            @block.tensor
            def _(t, base_sw=base_sw):
                t.wait_ge(sV, m_opre)
                for mt in range(KT):
                    bk = mt % 4
                    if mt < 4:
                        bank_wait(t, bk)
                    else:
                        t.wait_ge(sS, base_sw + (mt - 4) + 1)
                    last = None
                    for kt in range(KT):
                        last = t.matmul(out=pb[bk][0:128, 0:CT],
                                        lhsT=WT[:, 4, kt, mt, :],
                                        rhs=diffb[:, kt, :],
                                        start=(kt == 0), stop=(kt == KT - 1))
                    mk(last, sT, "t")

            @block.scalar
            def _(sc, base_tw=base_tw):
                for mt in range(KT):
                    bk = mt % 4
                    sc.wait_ge(sT, base_tw + mt + 1)
                    m = mk(sc.activation(out=sqb[:, mt, :], in_=pb[bk][0:128, 0:CT],
                                         func=AF.Identity,
                                         bias=bias_sb[:, 4, mt:mt + 1]), sS, "s")
                    bank_mark[bk] = ("s", m)
            m_ao = C["s"]

            if c + 1 < CH:
                emit_ln(x_tm[(c + 1) % 2], xn, 0, c + 1,
                        extra_wait_v=m_hx,
                        extra_wait_qs=m_xmiddma.get(c - 1))
                m_ln1d[c + 1] = C["v"]

            base_t10 = C["t"]
            base_v10 = C["v"]
            @block.tensor
            def _(t, base_v10=base_v10):
                t.wait_ge(sS, m_ao)
                for i in range(16):
                    bk = 4 + i % 2
                    tt, kt = divmod(i, KT)
                    if i < 2:
                        bank_wait(t, bk)
                    else:
                        t.wait_ge(sV, base_v10 + (i - 2) + 1)
                    mk(t.transpose(out=pb[bk][0:128, 0:128],
                                   in_=sqb[:, kt, tt * 128:(tt + 1) * 128],
                                   identity=ident[:]), sT, "t")

            @block.vector
            def _(v, base_t10=base_t10, c=c):
                if c > 0:
                    v.wait_ge(sQS, m_xmiddma[c - 1])
                for i in range(16):
                    bk = 4 + i % 2
                    tt, kt = divmod(i, KT)
                    v.wait_ge(sT, base_t10 + i + 1)
                    m = mk(v.tensor_tensor(
                        out=xm_tm[:, tt, kt * 128:(kt + 1) * 128],
                        in0=pb[bk][0:128, 0:128],
                        in1=xcur[:, tt, kt * 128:(kt + 1) * 128],
                        op=ALU.add), sV, "v")
                    bank_mark[bk] = ("v", m)
            m_xm = C["v"]

            @block.sync
            def _(q, c=c, t0=t0):
                q.wait_ge(sV, m_xm)
                m_xmiddma[c] = mk(
                    q.dma_start(out=x_mid[t0:t0 + CT, :].rearrange("(t p) n -> p t n", p=128),
                                in_=xm_tm[:]), sQS, "qs")
                if dbg:
                    mk(q.dma_start(out=dxmid[t0:t0 + CT, :].rearrange("(t p) n -> p t n", p=128),
                                   in_=xm_tm[:]), sQS, "qs")

            emit_ln(xm_tm, xcur, 12, c, extra_wait_v=m_xm)
            m_h2n = C["v"]

            base_t12 = C["t"]
            base_v12 = C["v"]
            @block.tensor
            def _(t, base_v12=base_v12):
                t.wait_ge(sV, m_h2n)
                for i in range(16):
                    bk = 4 + i % 2
                    tt, kt = divmod(i, KT)
                    if i < 2:
                        bank_wait(t, bk)
                    else:
                        t.wait_ge(sV, base_v12 + (i - 2) + 1)
                    mk(t.transpose(out=pb[bk][0:128, 0:128],
                                   in_=xcur[:, tt, kt * 128:(kt + 1) * 128],
                                   identity=ident[:]), sT, "t")

            @block.vector
            def _(v, base_t12=base_t12):
                for i in range(16):
                    bk = 4 + i % 2
                    tt, kt = divmod(i, KT)
                    v.wait_ge(sT, base_t12 + i + 1)
                    m = mk(v.tensor_copy(out=h2f[:, kt, tt * 128:(tt + 1) * 128],
                                         in_=pb[bk][0:128, 0:128]), sV, "v")
                    bank_mark[bk] = ("v", m)
            m_h2f = C["v"]

            @block.vector
            def _(v):
                mk(v.tensor_copy(out=diffb[:], in_=h2f[:]), sV, "v")
            m_h2b = C["v"]

            @block.sync
            def _(q, c=c, t0=t0):
                q.wait_ge(sV, m_h2b)
                g2, off = divmod(t0, FT)
                m_h2dma[c] = mk(q.dma_start(
                    out=h2_loc[g2 * D:(g2 + 1) * D, :].rearrange(
                        "(p kt) n -> p kt n", kt=KT)[:, :, off:off + CT],
                    in_=diffb[:]), sQS, "qs")
                if dbg:
                    mk(q.dma_start(
                        out=dh2.rearrange("(kt p) n -> p kt n", p=128)[:, :, t0:t0 + CT],
                        in_=h2f[:]), sQS, "qs")

            @block.tensor
            def _(t):
                t.wait_ge(sV, m_h2f)
                bank_wait(t, 6)
                last = None
                for kt in range(KT):
                    last = t.matmul(out=pb[6][0:8, 0:CT], lhsT=gate_sb[:, kt, :],
                                    rhs=h2f[:, kt, :], start=(kt == 0), stop=(kt == KT - 1))
                mk(last, sT, "t")

            @block.vector
            def _(v):
                v.wait_ge(sT, C["t"])
                m = mk(v.tensor_copy(out=lgs[:], in_=pb[6][0:8, 0:CT]), sV, "v")
                bank_mark[6] = ("v", m)
            m_lgs = C["v"]

            for tt in range(2):
                bk = 4 + tt
                @block.tensor
                def _(t, tt=tt, bk=bk):
                    t.wait_ge(sV, m_lgs)
                    bank_wait(t, bk)
                    mk(t.transpose(out=pb[bk][0:128, 0:8],
                                   in_=lgs[0:8, tt * 128:(tt + 1) * 128],
                                   identity=ident[0:8, 0:8]), sT, "t")
            m_ltr = C["t"]

            @block.vector
            def _(v, m_ltr=m_ltr):
                v.wait_ge(sT, m_ltr)
                for tt in range(2):
                    mk(v.tensor_reduce(out=r_sm[:, tt, 0:1], in_=pb[4 + tt][0:128, 0:8],
                                       axis=AX, op=ALU.max, negate=True), sV, "v")

            @block.scalar
            def _(sc, m_ltr=m_ltr):
                sc.wait_ge(sV, C["v"])
                sc.wait_ge(sT, m_ltr)
                for tt in range(2):
                    sc.copy(ltm_s[:, tt, :], pb[4 + tt][0:128, 0:8])
                    mk(sc.mul(r_sm[:, tt, 1:2], r_sm[:, tt, 0:1], -1.0), sS, "s")
                    m = mk(sc.activation(out=edmp[:, tt, :], in_=pb[4 + tt][0:128, 0:8],
                                         func=AF.Exp, bias=r_sm[:, tt, 0:1],
                                         accum_out=r_sm[:, tt, 2:3]), sS, "s")
                    bank_mark[4 + tt] = ("s", m)

            @block.vector
            def _(v):
                v.wait_ge(sS, C["s"])
                for tt in range(2):
                    mk(v.reciprocal(out=r_sm[:, tt, 3:4], in_=r_sm[:, tt, 2:3]), sV, "v")

            @block.scalar
            def _(sc):
                sc.wait_ge(sV, C["v"])
                for tt in range(2):
                    mk(sc.copy(r_sm[:, tt, 4:5], r_sm[:, tt, 3:4]), sS, "s")

            @block.vector
            def _(v):
                v.wait_ge(sS, C["s"])
                for tt in range(2):
                    mk(v.tensor_scalar(out=scale_tm[:, tt, :], in0=ltm_s[:, tt, :],
                                       scalar1=r_sm[:, tt, 1:2],
                                       scalar2=r_sm[:, tt, 4:5],
                                       op0=ALU.is_equal, op1=ALU.mult), sV, "v")
            m_sctm = C["v"]

            @block.tensor
            def _(t, m_sctm=m_sctm):
                t.wait_ge(sV, m_sctm)
                bank_wait(t, 6)
                for tt in range(2):
                    mk(t.transpose(out=pb[6][0:8, tt * 128:(tt + 1) * 128],
                                   in_=scale_tm[:, tt, :],
                                   identity=ident[:]), sT, "t")

            @block.vector
            def _(v, c=c):
                v.wait_ge(sT, C["t"])
                if c > 0:
                    v.wait_ge(sNR, m_scdma[c - 1])
                m = mk(v.tensor_copy(out=scaleT[:], in_=pb[6][0:8, 0:256]), sV, "v")
                bank_mark[6] = ("v", m)

            m_scT = C["v"]

            @block.sync
            def _(q, c=c, t0=t0):
                q.wait_ge(sV, m_scT)
                m_scdma[c] = mk(q.dma_start(out=sc_loc[:, t0:t0 + CT], in_=scaleT[:]),
                                sNR, "nr")
                if dbg:
                    mk(q.dma_start(out=dscale[:, t0:t0 + CT], in_=scaleT[:]), sQS, "qs")

            m_xlast[c] = C["v"]

            if (c % (FT // CT)) == (FT // CT) - 1:
                g2 = c // (FT // CT)
                @block.gpsimd
                def _(g, g2=g2, c=c):
                    g.wait_ge(sQS, m_h2dma[c])
                    with allow_ici_all_to_all():
                        mk(g.collective_compute(
                            "AllGather", ALU.bypass, replica_groups=RG,
                            ins=[h2_loc[g2 * D:(g2 + 1) * D, :].opt()],
                            outs=[h2_all[g2 * NB * D:(g2 + 1) * NB * D, :].opt()]),
                           sCC, "cc")

        @block.gpsimd
        def _(g):
            g.wait_ge(sQS, C["qs"])
            g.wait_ge(sNR, C["nr"])
            with allow_ici_all_to_all():
                mk(g.collective_compute("AllToAll", ALU.bypass, replica_groups=RG,
                                        ins=[sc_loc.opt()], outs=[sc_c.opt()]),
                   sCC, "cc")
        m_ag = C["cc"]

        attn_done_v, attn_done_s, attn_done_t = C["v"], C["s"], C["t"]
        attn_done_g = C["g"]
        aes.close()

        fes = contextlib.ExitStack()
        TF = lambda name, shape, dt: fes.enter_context(nc.sbuf_tensor(name, shape, dt))
        w1b = TF("w1b", [128, KT, DFF], BF16)
        w2b = TF("w2b", [128, 32, D], BF16)
        h2s = TF("h2s", [128, KT, FT], BF16)
        h_sub = TF("h_sub", [128, 32, FT], BF16)
        ych = TF("ych", [128, 2, D], F32)
        sct = TF("sct", [128, 2, 4], F32)
        b1sb = TF("b1sb", [128, 32], F32)
        b2st = TF("b2st", [1, D], F32)
        scrow = b2st[0:1, 0:FT]
        b2row = TF("b2row", [1, D], BF16)
        onesb = TF("onesb", [1, 128], BF16)
        xmt = TF("xmt", [128, D], F32)
        yrt = TF("yrt", [128, D], F32)
        ot = TF("ot", [128, D], F32)
        fstg = TF("fstg", [128, 3, D], F32)

        @block.sync
        def _(q):
            q.wait_ge(sV, attn_done_v)
            q.wait_ge(sS, attn_done_s)
            q.wait_ge(sT, attn_done_t)
            q.wait_ge(sG, attn_done_g)
            mk(q.dma_start(out=b1sb[:], in_=b1_in), sQS, "qs")
            mk(q.dma_start(out=b2st[:], in_=b2_in[None, :]), sNR, "nr")
        m_fb = C["qs"]
        m_b2d = C["nr"]

        @block.gpsimd
        def _(g):
            g.wait_ge(sQS, m_fb)
            mk(g.memset(onesb[:], 1.0), sG, "g")
        m_ones = C["g"]

        @block.scalar
        def _(sc):
            sc.wait_ge(sNR, m_b2d)
            mk(sc.copy(b2row[:], b2st[:]), sS, "s")
        m_b2 = C["s"]

        base_wv = C["v"]

        def wsrc(i):
            if i < 32:
                return w1_in[i * 128:(i + 1) * 128, :]
            return w2_in[(i - 32) * 128:(i - 31) * 128, :]

        def wdst(i):
            if i < 32:
                kt, qr = divmod(i, 4)
                return w1b[:, kt, qr * D:(qr + 1) * D]
            return w2b[:, i - 32, :]

        @block.sync
        def _(q):
            q.wait_ge(sCC, m_ag)
            for i in range(64):
                if i >= 3:
                    q.wait_ge(sV, base_wv + (i - 2))
                mk(q.dma_start(out=fstg[:, i % 3, :], in_=wsrc(i)), sQS, "qs")
            for e in range(2):
                q.wait_ge(sV, base_wv + 62 + e)
                mk(q.dma_start(out=fstg[:, (64 + e) % 3, :], in_=w1_in[0:128, :]),
                   sQS, "qs")
        base_wq = C["qs"] - 66 * 16

        @block.vector
        def _(v):
            for i in range(64):
                v.wait_ge(sQS, base_wq + (i + 3) * 16)
                mk(v.tensor_copy(out=wdst(i), in_=fstg[:, i % 3, :]), sV, "v")
        m_wv = C["v"]
        m_w1v = base_wv + 32

        m_h2s = {}
        m_sct = {}
        m_sctr = {-1: 0}
        m_A_last = {-1: 0}
        m_ydma = {}
        m_ytsd = {}
        m_rs = {}
        fbank = {i: ("v", 0) for i in range(8)}

        def fbank_wait(eng, i):
            nm, val = fbank[i]
            if val > 0:
                eng.wait_ge(SEM[nm], val)

        m_oadd = [0]
        m_odma = [0]

        def emit_final(gidx):
            for tm in range(FT // 128):
                row = gidx * FT + tm * 128
                @block.gpsimd
                def _(g, row=row, gidx=gidx):
                    g.wait_ge(sCC, m_rs[gidx])
                    if m_oadd[0] > 0:
                        g.wait_ge(sV, m_oadd[0])
                    mk(g.dma_start(out=yrt[:], in_=y_red[row:row + 128, :]), sQG, "qg")
                    mk(g.dma_start(out=xmt[:], in_=x_mid[row:row + 128, :]), sQG, "qg")

                @block.vector
                def _(v, row=row):
                    v.wait_ge(sQG, C["qg"])
                    if m_odma[0] > 0:
                        v.wait_ge(sSO, m_odma[0])
                    m_oadd[0] = mk(v.tensor_tensor(out=ot[:], in0=yrt[:], in1=xmt[:],
                                                   op=ALU.add), sV, "v")

                @block.gpsimd
                def _(g, row=row):
                    g.wait_ge(sV, m_oadd[0])
                    m_odma[0] = mk(g.dma_start(out=out_t[row:row + 128, :], in_=ot[:]),
                                   sSO, "so")

        def emit_rs(gidx):
            @block.gpsimd
            def _(g, gidx=gidx):
                for cc2 in range(gidx * NB, (gidx + 1) * NB):
                    g.wait_ge(sQS, m_ydma[(cc2, 1)])
                base = gidx * NB * FT
                m_rs[gidx] = mk(g.collective_compute(
                    "ReduceScatter", ALU.add, replica_groups=RG,
                    ins=[y_loc[base:base + NB * FT, :].opt()],
                    outs=[y_red[gidx * FT:(gidx + 1) * FT, :].opt()]), sCC, "cc")

        chunk = 0
        for gidx in range(NG):
            for b in range(NB):
                t0 = gidx * FT
                if b == 2 and gidx > 0:
                    emit_final(gidx - 1)
                @block.gpsimd
                def _(g, gidx=gidx, b=b, chunk=chunk, t0=t0):
                    g.wait_ge(sCC, m_ag)
                    g.wait_ge(sT, m_A_last[chunk - 1])
                    base = (gidx * NB + b) * D
                    m_h2s[chunk] = mk(g.dma_start(
                        out=h2s[:],
                        in_=h2_all[base:base + D, :].rearrange("(p kt) n -> p kt n", kt=KT)),
                        sQG, "qg")
                    g.wait_ge(sT, m_sctr[chunk - 1])
                    if chunk == 0:
                        g.wait_ge(sS, m_b2)
                    m_sct[chunk] = mk(g.dma_start(
                        out=scrow, in_=sc_c[b:b + 1, t0:t0 + FT]), sNR, "nr")
                if b == 0 and gidx > 0:
                    emit_rs(gidx - 1)

                base_t = C["t"]
                base_s = C["s"]
                @block.tensor
                def _(t, chunk=chunk, base_s=base_s):
                    t.wait_ge(sQG, m_h2s[chunk])
                    if chunk == 0:
                        t.wait_ge(sV, m_w1v)
                        t.wait_ge(sG, m_ones)
                    for j in range(32):
                        bk = j % 4
                        if j < 4:
                            fbank_wait(t, bk)
                        else:
                            t.wait_ge(sS, base_s + (j - 4) + 1)
                        last = None
                        for kt in range(KT):
                            last = t.matmul(out=pb[bk][0:128, 0:FT],
                                            lhsT=w1b[:, kt, j * 128:(j + 1) * 128],
                                            rhs=h2s[:, kt, :],
                                            start=(kt == 0), stop=(kt == KT - 1))
                        mk(last, sT, "t")

                @block.scalar
                def _(sc, base_t=base_t):
                    for j in range(32):
                        bk = j % 4
                        sc.wait_ge(sT, base_t + j + 1)
                        m = mk(sc.activation(out=h_sub[:, j, :], in_=pb[bk][0:128, 0:FT],
                                             func=AF.Gelu, bias=b1sb[:, j:j + 1]), sS, "s")
                        fbank[bk] = ("s", m)
                m_A_last[chunk] = C["t"]
                m_gelu_all = C["s"]

                @block.tensor
                def _(t, chunk=chunk):
                    t.wait_ge(sNR, m_sct[chunk])
                    fbank_wait(t, 0)
                    last = None
                    for tm in range(4):
                        last = t.transpose(out=pb[0][0:128, tm:tm + 1],
                                           in_=b2st[0:1, tm * 128:(tm + 1) * 128],
                                           identity=ident[0:1, 0:1])
                    m_sctr[chunk] = mk(last, sT, "t")

                @block.scalar
                def _(sc, chunk=chunk):
                    sc.wait_ge(sT, m_sctr[chunk])
                    if chunk >= 2:
                        sc.wait_ge(sV, m_ytsd[chunk - 2])
                    m = mk(sc.copy(sct[:, chunk % 2, :], pb[0][0:128, 0:4]), sS, "s")
                    fbank[0] = ("s", m)
                m_sctev = C["s"]

                base_t2 = C["t"]
                base_v2 = C["v"]
                @block.tensor
                def _(t, m_gelu_all=m_gelu_all, base_v2=base_v2, chunk=chunk):
                    t.wait_ge(sS, m_gelu_all)
                    t.wait_ge(sS, m_b2)
                    if chunk == 0:
                        t.wait_ge(sV, m_wv)
                    for q2 in range(8):
                        tm, dc = divmod(q2, 2)
                        bk = 4 + q2 % 4
                        if q2 < 4:
                            fbank_wait(t, bk)
                        else:
                            t.wait_ge(sV, base_v2 + (q2 - 4) + 1)
                        for j in range(32):
                            t.matmul(out=pb[bk][0:128, 0:FT],
                                     lhsT=h_sub[:, j, tm * 128:(tm + 1) * 128],
                                     rhs=w2b[:, j, dc * 512:(dc + 1) * 512],
                                     start=(j == 0), stop=False)
                        mk(t.matmul(out=pb[bk][0:128, 0:FT],
                                    lhsT=onesb[:],
                                    rhs=b2row[0:1, dc * 512:(dc + 1) * 512],
                                    start=False, stop=True), sT, "t")

                for half in range(2):
                    @block.vector
                    def _(v, half=half, chunk=chunk, m_sctev=m_sctev,
                          base_t2=base_t2):
                        for q2 in range(half * 4, half * 4 + 4):
                            tm, dc = divmod(q2, 2)
                            bk = 4 + q2 % 4
                            v.wait_ge(sT, base_t2 + q2 + 1)
                            if q2 == 0:
                                v.wait_ge(sS, m_sctev)
                                if chunk >= 1:
                                    v.wait_ge(sQS, m_ydma[(chunk - 1, 1)])
                            elif q2 == 4:
                                v.wait_ge(sQS, m_ydma[(chunk, 0)])
                            m = mk(v.tensor_scalar(
                                out=ych[:, tm % 2, dc * 512:(dc + 1) * 512],
                                in0=pb[bk][0:128, 0:FT],
                                scalar1=sct[:, chunk % 2, tm:tm + 1], scalar2=None,
                                op0=ALU.mult), sV, "v")
                            fbank[bk] = ("v", m)

                    @block.sync
                    def _(q, gidx=gidx, b=b, chunk=chunk, half=half):
                        q.wait_ge(sV, C["v"])
                        base = (gidx * NB + b) * FT + half * 256
                        m_ydma[(chunk, half)] = mk(q.dma_start(
                            out=y_loc[base:base + 256, :].rearrange(
                                "(t p) n -> p t n", p=128),
                            in_=ych[:]), sQS, "qs")
                m_ytsd[chunk] = C["v"]
                chunk += 1

        emit_rs(NG - 1)
        emit_final(NG - 1)

        if dbg:
            @block.sync
            def _(q):
                q.wait_ge(sSO, C["so"])
                mk(q.dma_start(out=dyloc, in_=y_loc), sQS, "qs")
                mk(q.dma_start(out=dyred, in_=y_red), sQS, "qs")
                mk(q.dma_start(out=dw1b, in_=w1b[:].rearrange("p a b -> p (a b)")), sQS, "qs")
                mk(q.dma_start(out=dw2b, in_=w2b[:].rearrange("p a b -> p (a b)")), sQS, "qs")
                mk(q.dma_start(out=db1, in_=b1sb[:]), sQS, "qs")
                mk(q.dma_start(out=db2, in_=b2row[:]), sQS, "qs")
                mk(q.dma_start(out=dhsub, in_=h_sub[:].rearrange("p a b -> p (a b)")), sQS, "qs")

        @block.gpsimd
        def _(g):
            g.wait_ge(sSO, C["so"])
            g.wait_ge(sQS, C["qs"])
            g.wait_ge(sQG, C["qg"])
            g.wait_ge(sCC, C["cc"])

        fes.close()
    return nc


def make_in_maps(inputs, N):
    CH = N // CT
    x = np.asarray(inputs["L0_attn_x"])
    vec = lambda v: np.ascontiguousarray(
        np.asarray(v, np.float32).reshape(KT, 128).T)
    maps = []
    for c in range(NB):
        xt = np.ascontiguousarray(
            x[c, :N].reshape(CH, 2, 128, D).transpose(0, 2, 1, 3).reshape(CH * 128, 2 * D))
        w1t = np.ascontiguousarray(
            np.asarray(inputs["L0_ffn_w1"])[c].reshape(KT, 128, 4, D)
            .transpose(0, 2, 1, 3).reshape(32 * 128, D))
        m = dict(
            x=xt,
            xlast=np.ascontiguousarray(x[c, N - 1:N]),
            Wg=inputs["L0_attn_Wg_w"], bg=vec(inputs["L0_attn_Wg_b"]),
            Wr=inputs["L0_attn_Wr_w"], br=vec(inputs["L0_attn_Wr_b"]),
            Wk=inputs["L0_attn_Wk_w"], bk=vec(inputs["L0_attn_Wk_b"]),
            Wv=inputs["L0_attn_Wv_w"], bv=vec(inputs["L0_attn_Wv_b"]),
            Wo=inputs["L0_attn_Wo_w"], bo=vec(inputs["L0_attn_Wo_b"]),
            mg=vec(inputs["L0_attn_mg"]), mr=vec(inputs["L0_attn_mr"]),
            mk=vec(inputs["L0_attn_mk"]), mv=vec(inputs["L0_attn_mv"]),
            wdec=vec(np.asarray(inputs["L0_attn_w"]).reshape(-1)),
            uvec=vec(np.asarray(inputs["L0_attn_u"]).reshape(-1)),
            gate=np.ascontiguousarray(
                np.asarray(inputs["L0_ffn_gate_w"], np.float32)
                .reshape(KT, 128, NEXP).transpose(1, 0, 2)),
            w1e=w1t,
            b1e=np.ascontiguousarray(
                np.asarray(inputs["L0_ffn_b1"])[c].reshape(32, 128).T),
            w2e=np.asarray(inputs["L0_ffn_w2"])[c],
            b2e=np.asarray(inputs["L0_ffn_b2"])[c],
        )
        maps.append({k: np.ascontiguousarray(np.asarray(v), dtype=np.float32)
                     for k, v in m.items()})
    return maps

def kernel(**inputs):
    nc = build_program(N=4096, dbg=False)
    in_maps = make_in_maps(inputs, 4096)
    res = run_bass_kernel_spmd(nc, in_maps, core_ids=list(range(8)))
    return np.stack([np.asarray(res.results[c]["out"]) for c in range(8)])


# version 12
# speedup vs baseline: 1.0347x; 1.0023x over previous
import contextlib
import numpy as np
import sys
for _p in ("/opt/trn_rl_repo",):
    if _p not in sys.path:
        sys.path.append(_p)
import concourse.bass as bass
import concourse.mybir as mybir
from concourse.bass_utils import run_bass_kernel_spmd
from concourse.aot_env import allow_ici_all_to_all

F32 = mybir.dt.float32
BF16 = mybir.dt.bfloat16
ALU = mybir.AluOpType
AF = mybir.ActivationFunctionType
AX = mybir.AxisListType.X
RG = [[0, 1, 2, 3, 4, 5, 6, 7]]

D = 1024
KT = 8
CT = 256
FT = 512
DFF = 4096
NEXP = 8
NB = 8


def build_program(N=4096, dbg=False):
    CH = N // CT
    NG = N // FT
    nc = bass.Bass("TRN2", target_bir_lowering=False)

    x_in = nc.dram_tensor("x", [CH * 128, 2 * D], F32, kind="ExternalInput").ap()
    xlast_in = nc.dram_tensor("xlast", [1, D], F32, kind="ExternalInput").ap()
    W_in, b_in, m_in = {}, {}, {}
    for wi, nm in enumerate("grkvo"):
        W_in[wi] = nc.dram_tensor(f"W{nm}", [D, D], F32, kind="ExternalInput").ap()
        b_in[wi] = nc.dram_tensor(f"b{nm}", [128, KT], F32, kind="ExternalInput").ap()
    for wi, nm in enumerate("grkv"):
        m_in[wi] = nc.dram_tensor(f"m{nm}", [128, KT], F32, kind="ExternalInput").ap()
    wdec_in = nc.dram_tensor("wdec", [128, KT], F32, kind="ExternalInput").ap()
    u_in = nc.dram_tensor("uvec", [128, KT], F32, kind="ExternalInput").ap()
    gate_in = nc.dram_tensor("gate", [128, KT, NEXP], F32, kind="ExternalInput").ap()
    w1_in = nc.dram_tensor("w1e", [32 * 128, D], F32, kind="ExternalInput").ap()
    b1_in = nc.dram_tensor("b1e", [128, 32], F32, kind="ExternalInput").ap()
    w2_in = nc.dram_tensor("w2e", [DFF, D], F32, kind="ExternalInput").ap()
    b2_in = nc.dram_tensor("b2e", [D], F32, kind="ExternalInput").ap()
    out_t = nc.dram_tensor("out", [N, D], F32, kind="ExternalOutput").ap()
    if dbg:
        dxmid = nc.dram_tensor("dxmid", [N, D], F32, kind="ExternalOutput").ap()
        dh2 = nc.dram_tensor("dh2", [D, N], F32, kind="ExternalOutput").ap()
        dscale = nc.dram_tensor("dscale", [NEXP, N], F32, kind="ExternalOutput").ap()
        dyloc = nc.dram_tensor("dyloc", [NB * N, D], F32, kind="ExternalOutput").ap()
        dyred = nc.dram_tensor("dyred", [N, D], F32, kind="ExternalOutput").ap()
        dw1b = nc.dram_tensor("dw1b", [128, KT * DFF], BF16, kind="ExternalOutput").ap()
        dw2b = nc.dram_tensor("dw2b", [128, 32 * D], BF16, kind="ExternalOutput").ap()
        db1 = nc.dram_tensor("db1", [128, 32], F32, kind="ExternalOutput").ap()
        db2 = nc.dram_tensor("db2", [1, D], BF16, kind="ExternalOutput").ap()
        dhsub = nc.dram_tensor("dhsub", [128, 32 * FT], BF16, kind="ExternalOutput").ap()

    x_mid = nc.dram_tensor("x_mid", [N, D], F32, kind="Internal").ap()
    h2_loc = nc.dram_tensor("h2_loc", [NG * D, FT], BF16, kind="Internal").ap()
    h2_all = nc.dram_tensor("h2_all", [NB * NG * D, FT], BF16, kind="Internal",
                            addr_space="Local").ap()
    sc_loc = nc.dram_tensor("sc_loc", [NEXP, N], F32, kind="Internal").ap()
    sc_c = nc.dram_tensor("sc_c", [NB, N], F32, kind="Internal",
                          addr_space="Local").ap()
    y_loc = nc.dram_tensor("y_loc", [NB * N, D], F32, kind="Internal").ap()
    y_red = nc.dram_tensor("y_red", [N, D], F32, kind="Internal",
                           addr_space="Local").ap()

    C = {}

    def mk(instr, sem, key, n=None):
        if n is None:
            n = 16 if key in ("qs", "qg", "so", "nr") else 1
        instr.then_inc(sem, n)
        C[key] = C.get(key, 0) + n
        return C[key]

    es = contextlib.ExitStack()
    with es:
        pb = [es.enter_context(nc.psum_tensor(f"pb{i}", [128, 512], F32)) for i in range(8)]
        epsv = es.enter_context(nc.sbuf_tensor("epsv", [128, 1], F32))
        ident = es.enter_context(nc.sbuf_tensor("ident", [128, 128], F32))
        sQS = es.enter_context(nc.semaphore("qs"))
        sQG = es.enter_context(nc.semaphore("qg"))
        sT = es.enter_context(nc.semaphore("t"))
        sV = es.enter_context(nc.semaphore("v"))
        sS = es.enter_context(nc.semaphore("s"))
        sG = es.enter_context(nc.semaphore("g"))
        sCC = es.enter_context(nc.semaphore("cc"))
        sSO = es.enter_context(nc.semaphore("so"))
        sNR = es.enter_context(nc.semaphore("nr"))
        SEM = {"v": sV, "s": sS, "t": sT, "g": sG, "qs": sQS, "qg": sQG,
               "cc": sCC, "so": sSO, "nr": sNR}

        aes = contextlib.ExitStack()
        TA = lambda name, shape, dt: aes.enter_context(nc.sbuf_tensor(name, shape, dt))
        WT = TA("WT", [128, 5, KT, KT, 128], BF16)
        bias_sb = TA("bias_sb", [128, 5, KT], F32)
        mvec = TA("mvec", [128, 4, KT], F32)
        uvec = TA("uvec_sb", [128, KT], F32)
        e1v = TA("e1v", [128, KT], F32)
        ngv = TA("ngv", [128, KT], F32)
        decv = TA("decv", [128, KT], F32)
        Gmat = TA("Gmat", [128, KT, 16], F32)
        Emat = TA("Emat", [16, KT, 128], F32)
        gate_sb = TA("gate_sb", [128, KT, NEXP], F32)
        h_bnd = TA("h_bnd", [128, KT, 1], F32)
        carry_s = TA("carry_s", [128, KT, 1], F32)
        ses = contextlib.ExitStack()
        wstg = ses.enter_context(nc.sbuf_tensor("wstg", [128, 2, D], F32))
        xrow = ses.enter_context(nc.sbuf_tensor("xrow", [1, D], F32))
        rsm = ses.enter_context(nc.sbuf_tensor("rsm", [1, 12], F32))
        Gp = ses.enter_context(nc.sbuf_tensor("Gp", [128, KT, 16], F32))
        rdmp = ses.enter_context(nc.sbuf_tensor("rdmp", [1, D], F32))

        block = es.enter_context(nc.Block())

        bank_mark = {i: ("v", 0) for i in range(8)}

        def bank_wait(eng, i):
            nm, val = bank_mark[i]
            if val > 0:
                eng.wait_ge(SEM[nm], val)

        @block.gpsimd
        def _(g):
            mk(g.memset(epsv[:], 1e-5), sG, "g")
            mk(g.memset(ident[:], 0.0), sG, "g")
            g.wait_ge(sG, C["g"])
            mk(g.affine_select(out=ident[:], in_=ident[:], compare_op=ALU.not_equal,
                               fill=1.0, base=0, pattern=[[-1, 128]],
                               channel_multiplier=1), sG, "g")
            mk(g.memset(Gmat[:], 0.0), sG, "g")
            mk(g.memset(Gp[:], 0.0), sG, "g")
            g.wait_ge(sG, C["g"])
            for kt in range(KT):
                mk(g.memset(Gmat[0:64, kt, 2 * kt:2 * kt + 1], 1.0 / 64), sG, "g")
                mk(g.memset(Gmat[64:128, kt, 2 * kt + 1:2 * kt + 2], 1.0 / 64),
                   sG, "g")
                mk(g.memset(Gp[0:64, kt, 2 * kt:2 * kt + 1], 1.0), sG, "g")
            for kt in range(KT):
                mk(g.memset(Gp[64:128, kt, 2 * kt + 1:2 * kt + 2], 1.0), sG, "g")
        m_setup_g = C["g"]

        @block.sync
        def _(q):
            for wi in range(5):
                mk(q.dma_start(out=bias_sb[:, wi, :], in_=b_in[wi]), sQS, "qs")
            for wi in range(4):
                mk(q.dma_start(out=mvec[:, wi, :], in_=m_in[wi]), sQS, "qs")
            mk(q.dma_start(out=uvec[:], in_=u_in), sQS, "qs")
            mk(q.dma_start(out=e1v[:], in_=wdec_in), sQS, "qs")
            mk(q.dma_start(out=gate_sb[:], in_=gate_in), sQS, "qs")
            mk(q.dma_start(out=xrow[:], in_=xlast_in), sNR, "nr")
        m_vecs = C["qs"]
        m_xrow = C["nr"]

        @block.scalar
        def _(sc):
            sc.wait_ge(sQS, m_vecs)
            mk(sc.activation(out=decv[:], in_=e1v[:], func=AF.Exp), sS, "s")

        @block.vector
        def _(v):
            v.wait_ge(sS, C["s"])
            mk(v.tensor_scalar(out=ngv[:], in0=decv[:], scalar1=-1.0, scalar2=None,
                               op0=ALU.mult), sV, "v")

        @block.scalar
        def _(sc):
            sc.wait_ge(sV, C["v"])
            mk(sc.activation(out=decv[:], in_=ngv[:], func=AF.Exp), sS, "s")
        m_dec = C["s"]

        m_decrep = m_dec

        for kt in range(KT):
            bk = kt % 2
            @block.tensor
            def _(t, kt=kt, bk=bk):
                t.wait_ge(sG, m_setup_g)
                bank_wait(t, bk)
                mk(t.transpose(out=pb[bk][0:16, 0:128], in_=Gp[:, kt, :],
                               identity=ident[:]), sT, "t")

            @block.vector
            def _(v, kt=kt, bk=bk):
                v.wait_ge(sT, C["t"])
                m = mk(v.tensor_copy(out=Emat[:, kt, :], in_=pb[bk][0:16, 0:128]),
                       sV, "v")
                bank_mark[bk] = ("v", m)

        wstg_guard = {}

        @block.sync
        def _(q):
            q.wait_ge(sQS, m_vecs)
            mk(q.dma_start(out=wstg[:, 0, :], in_=W_in[0][0:128, :]), sQS, "qs")
        m_wstg0 = C["qs"]

        for idx in range(5 * KT):
            wi, mt = divmod(idx, KT)
            @block.sync
            def _(q, idx=idx):
                if idx + 1 < 5 * KT:
                    if idx >= 1:
                        q.wait_ge(sT, wstg_guard[idx - 1])
                    else:
                        q.wait_ge(sQS, m_wstg0)
                    wi2, mt2 = divmod(idx + 1, KT)
                    mk(q.dma_start(out=wstg[:, (idx + 1) % 2, :],
                                   in_=W_in[wi2][mt2 * 128:(mt2 + 1) * 128, :]),
                       sQS, "qs")
                else:
                    q.wait_ge(sT, wstg_guard[idx - 1])
                    mk(q.dma_start(out=wstg[:, (idx + 1) % 2, :],
                                   in_=W_in[0][0:128, :]), sQS, "qs")
            m_wdma = C["qs"]
            for kt2 in range(KT):
                bk = 2 + kt2 % 2
                @block.tensor
                def _(t, wi=wi, mt=mt, idx=idx, kt2=kt2, bk=bk, m_wdma=m_wdma):
                    if kt2 == 0:
                        t.wait_ge(sQS, m_wdma)
                    bank_wait(t, bk)
                    m = mk(t.transpose(out=pb[bk][0:128, 0:128],
                                       in_=wstg[:, idx % 2, kt2 * 128:(kt2 + 1) * 128],
                                       identity=ident[:]), sT, "t")
                    if kt2 == KT - 1:
                        wstg_guard[idx] = m

                @block.vector
                def _(v, wi=wi, mt=mt, kt2=kt2, bk=bk):
                    v.wait_ge(sT, C["t"])
                    m = mk(v.tensor_copy(out=WT[:, wi, kt2, mt, :],
                                         in_=pb[bk][0:128, 0:128]), sV, "v")
                    bank_mark[bk] = ("v", m)
        m_wt = C["v"]

        @block.vector
        def _(v):
            v.wait_ge(sQS, m_vecs)
            v.wait_ge(sNR, m_xrow)
            mk(v.tensor_reduce(out=rsm[:, 0:1], in_=xrow[:], axis=AX, op=ALU.add),
               sV, "v")

        @block.scalar
        def _(sc):
            sc.wait_ge(sV, C["v"])
            mk(sc.mul(rsm[:, 1:2], rsm[:, 0:1], 1.0 / D), sS, "s")
            mk(sc.activation(out=rdmp[:], in_=xrow[:], func=AF.Square,
                             accum_out=rsm[:, 2:3]), sS, "s")

        @block.vector
        def _(v):
            v.wait_ge(sS, C["s"])
            v.tensor_tensor(out=rsm[:, 3:4], in0=rsm[:, 1:2], in1=rsm[:, 1:2], op=ALU.mult)
            mk(v.tensor_scalar(out=rsm[:, 4:5], in0=rsm[:, 2:3], scalar1=1.0 / D,
                               scalar2=None, op0=ALU.mult), sV, "v")

        @block.gpsimd
        def _(g):
            g.wait_ge(sV, C["v"])
            mk(g.tensor_tensor(out=rsm[:, 5:6], in0=rsm[:, 4:5], in1=rsm[:, 3:4],
                               op=ALU.subtract), sG, "g")

        @block.scalar
        def _(sc):
            sc.wait_ge(sG, C["g"])
            mk(sc.activation(out=rsm[:, 6:7], in_=rsm[:, 5:6], func=AF.Ln,
                             bias=epsv[0:1, 0:1]), sS, "s")

        @block.vector
        def _(v):
            v.wait_ge(sS, C["s"])
            mk(v.tensor_scalar(out=rsm[:, 7:8], in0=rsm[:, 6:7], scalar1=-0.5,
                               scalar2=None, op0=ALU.mult), sV, "v")

        @block.scalar
        def _(sc):
            sc.wait_ge(sV, C["v"])
            mk(sc.activation(out=rsm[:, 8:9], in_=rsm[:, 7:8], func=AF.Exp), sS, "s")

        @block.vector
        def _(v):
            v.wait_ge(sS, C["s"])
            mk(v.tensor_scalar(out=rdmp[:], in0=xrow[:], scalar1=rsm[:, 1:2],
                               scalar2=rsm[:, 8:9], op0=ALU.subtract, op1=ALU.mult),
               sV, "v")
        m_rowln = C["v"]

        for kt in range(KT):
            bk = 4 + kt % 2
            @block.tensor
            def _(t, kt=kt, bk=bk):
                t.wait_ge(sV, m_rowln)
                bank_wait(t, bk)
                mk(t.transpose(out=pb[bk][0:128, kt:kt + 1],
                               in_=rdmp[0:1, kt * 128:(kt + 1) * 128],
                               identity=ident[0:1, 0:1]), sT, "t")

            @block.vector
            def _(v, kt=kt, bk=bk):
                v.wait_ge(sT, C["t"])
                m = mk(v.tensor_copy(out=h_bnd[:, kt, :], in_=pb[bk][0:128, kt:kt + 1]),
                       sV, "v")
                bank_mark[bk] = ("v", m)
        m_hbnd_init = ("v", C["v"])
        m_setup_done_v = C["v"]

        ses.close()
        x_tm = [TA(f"x_tm{i}", [128, 2, D], F32) for i in range(2)]
        xn = TA("xn", [128, 2, D], F32)
        xm_tm = TA("xm_tm", [128, 2, D], F32)
        dmp = TA("dmp", [128, 2, D], F32)
        sm = TA("sm", [128, 2, 24], F32)
        h_ext = TA("h_ext", [128, KT, CT + 1], F32)
        diffb = TA("diffb", [128, KT, CT], BF16)
        mixb = [TA(f"mixb{i}", [128, KT, CT], BF16) for i in range(2)]
        sgb = TA("sgb", [128, KT, CT], BF16)
        rb = TA("rb", [128, KT, CT], F32)
        kb = TA("kb", [128, KT, CT], F32)
        kvb = TA("kvb", [128, KT, CT], F32)
        wkvb = TA("wkvb", [128, KT, CT], F32)
        sqb = TA("sqb", [128, KT, CT], F32)
        tmp1 = TA("tmp1", [128, CT], F32)
        h2f = TA("h2f", [128, KT, CT], F32)
        mean_sb = TA("mean_sb", [16, CT], F32)
        msq_sb = TA("msq_sb", [16, CT], F32)
        var16 = TA("var16", [16, CT], F32)
        l16 = TA("l16", [16, CT], F32)
        r16 = var16
        ltm_s = TA("ltm_s", [128, 2, 8], F32)
        r_sm = TA("r_sm", [128, 2, 6], F32)
        edmp = TA("edmp", [128, 2, 8], F32)
        scale_tm = TA("scale_tm", [128, 2, 8], F32)
        scaleT = TA("scaleT", [8, CT], F32)
        lgs = scaleT

        @block.gpsimd
        def _(g):
            g.wait_ge(sV, m_setup_done_v)
            mk(g.dma_start(out=x_tm[0][:].rearrange("p a b -> p (a b)"),
                           in_=x_in[0:128, :]), sQG, "qg")

        m_x = {0: C["qg"]}
        m_xlast = {}
        m_carry = {}
        m_hbnd = {-1: m_hbnd_init}
        m_h2dma = {}
        m_scdma = {}
        m_xmiddma = {}
        m_kvd = {}

        def emit_ln(src, dst, slot, c, extra_wait_v=None, extra_wait_qs=None):
            @block.vector
            def _(v):
                v.wait_ge(sQG, m_x[c])
                if extra_wait_v is not None:
                    v.wait_ge(sV, extra_wait_v)
                if extra_wait_qs is not None:
                    v.wait_ge(sQS, extra_wait_qs)
                for tt in range(2):
                    mk(v.tensor_reduce(out=sm[:, tt, slot:slot + 1],
                                       in_=src[:, tt, :], axis=AX, op=ALU.add), sV, "v")

            @block.scalar
            def _(sc):
                sc.wait_ge(sV, C["v"])
                sc.wait_ge(sQG, m_x[c])
                for tt in range(2):
                    mk(sc.mul(sm[:, tt, slot + 1:slot + 2],
                              sm[:, tt, slot:slot + 1], 1.0 / D), sS, "s")
                    mk(sc.activation(out=dmp[:, tt, :], in_=src[:, tt, :],
                                     func=AF.Square,
                                     accum_out=sm[:, tt, slot + 2:slot + 3]), sS, "s")

            @block.vector
            def _(v):
                v.wait_ge(sS, C["s"])
                for tt in range(2):
                    v.tensor_tensor(out=sm[:, tt, slot + 3:slot + 4],
                                    in0=sm[:, tt, slot + 1:slot + 2],
                                    in1=sm[:, tt, slot + 1:slot + 2], op=ALU.mult)
                    mk(v.tensor_scalar(out=sm[:, tt, slot + 4:slot + 5],
                                       in0=sm[:, tt, slot + 2:slot + 3],
                                       scalar1=1.0 / D, scalar2=None, op0=ALU.mult),
                       sV, "v")

            @block.gpsimd
            def _(g):
                g.wait_ge(sV, C["v"])
                for tt in range(2):
                    mk(g.tensor_tensor(out=sm[:, tt, slot + 5:slot + 6],
                                       in0=sm[:, tt, slot + 4:slot + 5],
                                       in1=sm[:, tt, slot + 3:slot + 4],
                                       op=ALU.subtract), sG, "g")

            @block.scalar
            def _(sc):
                sc.wait_ge(sG, C["g"])
                for tt in range(2):
                    mk(sc.activation(out=sm[:, tt, slot + 6:slot + 7],
                                     in_=sm[:, tt, slot + 5:slot + 6],
                                     func=AF.Ln, bias=epsv[:]), sS, "s")

            @block.vector
            def _(v):
                v.wait_ge(sS, C["s"])
                for tt in range(2):
                    mk(v.tensor_scalar(out=sm[:, tt, slot + 7:slot + 8],
                                       in0=sm[:, tt, slot + 6:slot + 7],
                                       scalar1=-0.5, scalar2=None, op0=ALU.mult), sV, "v")

            @block.scalar
            def _(sc):
                sc.wait_ge(sV, C["v"])
                for tt in range(2):
                    mk(sc.activation(out=sm[:, tt, slot + 8:slot + 9],
                                     in_=sm[:, tt, slot + 7:slot + 8],
                                     func=AF.Exp), sS, "s")

            @block.vector
            def _(v):
                v.wait_ge(sS, C["s"])
                for tt in range(2):
                    mk(v.tensor_scalar(out=dst[:, tt, :], in0=src[:, tt, :],
                                       scalar1=sm[:, tt, slot + 1:slot + 2],
                                       scalar2=sm[:, tt, slot + 8:slot + 9],
                                       op0=ALU.subtract, op1=ALU.mult), sV, "v")

        m_ln1d = {}
        emit_ln(x_tm[0], xn, 0, 0)
        m_ln1d[0] = C["v"]

        for c in range(CH):
            t0 = c * CT
            if c + 1 < CH:
                @block.gpsimd
                def _(g, c=c):
                    if c >= 1:
                        g.wait_ge(sV, m_xlast.get(c - 1, 0))
                    mk(g.dma_start(
                        out=x_tm[(c + 1) % 2][:].rearrange("p a b -> p (a b)"),
                        in_=x_in[(c + 1) * 128:(c + 2) * 128, :]), sQG, "qg")
                    m_x[c + 1] = C["qg"]

            xcur = x_tm[c % 2]
            m_ln1 = m_ln1d[c]

            @block.gpsimd
            def _(g, c=c):
                nm, val = m_hbnd[c - 1]
                g.wait_ge(SEM[nm], val)
                if c > 0:
                    g.wait_ge(sV, m_kvd[c - 1])
                mk(g.tensor_copy(out=h_ext[:, :, 0:1], in_=h_bnd[:]), sG, "g")
            m_bcol = C["g"]

            base_t = C["t"]
            base_v = C["v"]
            @block.tensor
            def _(t, base_v=base_v):
                t.wait_ge(sV, m_ln1)
                for i in range(16):
                    bk = 4 + i % 2
                    if i < 2:
                        bank_wait(t, bk)
                    else:
                        t.wait_ge(sV, base_v + (i - 2) + 1)
                    tt, kt = divmod(i, KT)
                    mk(t.transpose(out=pb[bk][0:128, 0:128],
                                   in_=xn[:, tt, kt * 128:(kt + 1) * 128],
                                   identity=ident[:]), sT, "t")

            @block.vector
            def _(v, c=c, base_t=base_t):
                if c > 0:
                    nm, val = m_hbnd[c - 1]
                    v.wait_ge(SEM[nm], val)
                for i in range(16):
                    bk = 4 + i % 2
                    tt, kt = divmod(i, KT)
                    v.wait_ge(sT, base_t + i + 1)
                    m = mk(v.tensor_copy(
                        out=h_ext[:, kt, 1 + tt * 128:1 + (tt + 1) * 128],
                        in_=pb[bk][0:128, 0:128]), sV, "v")
                    bank_mark[bk] = ("v", m)
            m_hx = C["v"]

            @block.scalar
            def _(sc, c=c, m_bcol=m_bcol):
                sc.wait_ge(sV, m_hx)
                sc.wait_ge(sG, m_bcol)
                m_hbnd[c] = ("s", mk(sc.copy(h_bnd[:], h_ext[:, :, CT:CT + 1]), sS, "s"))

            @block.vector
            def _(v, c=c, m_bcol=m_bcol, m_hx=m_hx):
                v.wait_ge(sG, m_bcol)
                v.wait_ge(sV, m_hx)
                if c == 0:
                    v.wait_ge(sV, m_wt)
                else:
                    v.wait_ge(sQS, m_h2dma[c - 1])
                mk(v.tensor_tensor(out=sqb[:], in0=h_ext[:, :, 1:CT + 1],
                                   in1=h_ext[:, :, 0:CT], op=ALU.subtract), sV, "v")
            m_diff = C["v"]

            m_mixmm = {}
            for wi in range(4):
                @block.vector
                def _(v, wi=wi):
                    v.wait_ge(sV, m_diff)
                    if wi >= 2:
                        v.wait_ge(sT, m_mixmm[wi - 2])
                    last = None
                    for kt in range(KT):
                        last = v.scalar_tensor_tensor(
                            out=mixb[wi % 2][:, kt, :], in0=sqb[:, kt, :],
                            scalar=mvec[:, wi, kt:kt + 1], in1=h_ext[:, kt, 0:CT],
                            op0=ALU.mult, op1=ALU.add)
                    mk(last, sV, "v")
                m_mix = C["v"]

                base_tm = C["t"]
                base_ev = C["s"] if wi < 3 else C["v"]
                @block.tensor
                def _(t, wi=wi, m_mix=m_mix, base_ev=base_ev):
                    t.wait_ge(sV, m_mix)
                    for mt in range(KT):
                        bk = mt % 4
                        if mt < 4:
                            bank_wait(t, bk)
                        else:
                            t.wait_ge(sS if wi < 3 else sV, base_ev + (mt - 4) + 1)
                        last = None
                        for kt in range(KT):
                            last = t.matmul(out=pb[bk][0:128, 0:CT],
                                            lhsT=WT[:, wi, kt, mt, :],
                                            rhs=mixb[wi % 2][:, kt, :],
                                            start=(kt == 0), stop=(kt == KT - 1))
                        mk(last, sT, "t")
                m_mixmm[wi] = C["t"]

                if wi < 3:
                    func = AF.Sigmoid if wi == 0 else AF.Identity
                    dstb = (sgb, rb, kb)[wi]
                    @block.scalar
                    def _(sc, wi=wi, func=func, dstb=dstb, base_tm=base_tm):
                        for mt in range(KT):
                            bk = mt % 4
                            sc.wait_ge(sT, base_tm + mt + 1)
                            m = mk(sc.activation(out=dstb[:, mt, :],
                                                 in_=pb[bk][0:128, 0:CT], func=func,
                                                 bias=bias_sb[:, wi, mt:mt + 1]), sS, "s")
                            bank_mark[bk] = ("s", m)
                else:
                    @block.vector
                    def _(v, base_tm=base_tm, kevm=C["s"]):
                        v.wait_ge(sS, kevm)
                        for mt in range(KT):
                            bk = mt % 4
                            v.wait_ge(sT, base_tm + mt + 1)
                            m = mk(v.scalar_tensor_tensor(
                                out=kvb[:, mt, :], in0=pb[bk][0:128, 0:CT],
                                scalar=bias_sb[:, 3, mt:mt + 1], in1=kb[:, mt, :],
                                op0=ALU.add, op1=ALU.mult), sV, "v")
                            bank_mark[bk] = ("v", m)
            m_kv = C["v"]
            m_kvd[c] = m_kv

            base_sc = C["v"]
            @block.vector
            def _(v, c=c):
                v.wait_ge(sV, m_kv)
                if c == 0:
                    v.wait_ge(sS, m_decrep)
                else:
                    nm, val = m_carry[c - 1]
                    v.wait_ge(SEM[nm], val)
                for kt in range(KT):
                    mk(v.tensor_tensor_scan(
                        out=wkvb[:, kt, :],
                        data0=decv[:, kt:kt + 1].broadcast_to([128, CT]),
                        data1=kvb[:, kt, :],
                        initial=(0.0 if c == 0 else carry_s[:, kt, 0:1]),
                        op0=ALU.mult, op1=ALU.add), sV, "v")
                    mk(v.scalar_tensor_tensor(
                        out=sqb[:, kt, :], in0=kvb[:, kt, :],
                        scalar=uvec[:, kt:kt + 1], in1=wkvb[:, kt, :],
                        op0=ALU.mult, op1=ALU.add), sV, "v")
                    mk(v.tensor_tensor(out=kvb[:, kt, :], in0=sqb[:, kt, :],
                                       in1=rb[:, kt, :], op=ALU.mult), sV, "v")
            m_rw = C["v"]

            base_sq = C["s"]
            @block.scalar
            def _(sc, base_sc=base_sc):
                for kt in range(KT):
                    sc.wait_ge(sV, base_sc + 3 * kt + 3)
                    mk(sc.activation(out=sqb[:, kt, :], in_=kvb[:, kt, :],
                                     func=AF.Square), sS, "s")
            m_sq = C["s"]

            @block.scalar
            def _(sc, c=c, m_rw=m_rw):
                sc.wait_ge(sV, m_rw)
                m_carry[c] = ("s", mk(sc.copy(carry_s[:], wkvb[:, :, CT - 1:CT]), sS, "s"))

            @block.tensor
            def _(t, base_sq=base_sq):
                bank_wait(t, 6)
                bank_wait(t, 7)
                last = None
                for kt in range(KT):
                    t.wait_ge(sS, base_sq + kt + 1)
                    t.matmul(out=pb[6][0:16, 0:CT], lhsT=Gmat[:, kt, :],
                             rhs=kvb[:, kt, :], start=(kt == 0), stop=(kt == KT - 1))
                    last = t.matmul(out=pb[7][0:16, 0:CT], lhsT=Gmat[:, kt, :],
                                    rhs=sqb[:, kt, :], start=(kt == 0), stop=(kt == KT - 1))
                mk(last, sT, "t")
            m_stmm = C["t"]

            @block.vector
            def _(v):
                v.wait_ge(sT, m_stmm)
                v.tensor_copy(out=mean_sb[:], in_=pb[6][0:16, 0:CT])
                m = mk(v.tensor_copy(out=msq_sb[:], in_=pb[7][0:16, 0:CT]), sV, "v")
                bank_mark[6] = ("v", m)
                bank_mark[7] = ("v", m)
                v.tensor_tensor(out=var16[:], in0=mean_sb[:], in1=mean_sb[:], op=ALU.mult)
                mk(v.tensor_tensor(out=var16[:], in0=msq_sb[:], in1=var16[:],
                                   op=ALU.subtract), sV, "v")

            @block.scalar
            def _(sc):
                sc.wait_ge(sV, C["v"])
                mk(sc.activation(out=l16[:], in_=var16[:], func=AF.Ln,
                                 bias=epsv[0:16, 0:1]), sS, "s")

            @block.vector
            def _(v):
                v.wait_ge(sS, C["s"])
                mk(v.tensor_scalar(out=l16[:], in0=l16[:], scalar1=-0.5, scalar2=None,
                                   op0=ALU.mult), sV, "v")

            @block.scalar
            def _(sc):
                sc.wait_ge(sV, C["v"])
                mk(sc.activation(out=r16[:], in_=l16[:], func=AF.Exp), sS, "s")
            m_r16 = C["s"]

            base_tb = C["t"]
            base_vb = C["v"]
            @block.tensor
            def _(t, base_vb=base_vb):
                t.wait_ge(sS, m_r16)
                for kt in range(KT):
                    bkA = 4 + (kt % 2)
                    bkB = 6 + (kt % 2)
                    if kt < 2:
                        bank_wait(t, bkA)
                        bank_wait(t, bkB)
                    else:
                        t.wait_ge(sV, base_vb + (kt - 2) + 1)
                    t.matmul(out=pb[bkA][0:128, 0:CT], lhsT=Emat[:, kt, :],
                             rhs=mean_sb[:], start=True, stop=True)
                    mk(t.matmul(out=pb[bkB][0:128, 0:CT], lhsT=Emat[:, kt, :],
                                rhs=r16[:], start=True, stop=True), sT, "t")

            @block.vector
            def _(v, base_tb=base_tb):
                for kt in range(KT):
                    bkA = 4 + (kt % 2)
                    bkB = 6 + (kt % 2)
                    v.wait_ge(sT, base_tb + kt + 1)
                    v.tensor_tensor(out=tmp1[:], in0=kvb[:, kt, :],
                                    in1=pb[bkA][0:128, 0:CT], op=ALU.subtract)
                    v.tensor_tensor(out=tmp1[:], in0=tmp1[:],
                                    in1=pb[bkB][0:128, 0:CT], op=ALU.mult)
                    m = mk(v.tensor_tensor(out=diffb[:, kt, :], in0=tmp1[:],
                                           in1=sgb[:, kt, :], op=ALU.mult), sV, "v")
                    bank_mark[bkA] = ("v", m)
                    bank_mark[bkB] = ("v", m)
            m_opre = C["v"]

            base_tw = C["t"]
            base_sw = C["s"]
            @block.tensor
            def _(t, base_sw=base_sw):
                t.wait_ge(sV, m_opre)
                for mt in range(KT):
                    bk = mt % 4
                    if mt < 4:
                        bank_wait(t, bk)
                    else:
                        t.wait_ge(sS, base_sw + (mt - 4) + 1)
                    last = None
                    for kt in range(KT):
                        last = t.matmul(out=pb[bk][0:128, 0:CT],
                                        lhsT=WT[:, 4, kt, mt, :],
                                        rhs=diffb[:, kt, :],
                                        start=(kt == 0), stop=(kt == KT - 1))
                    mk(last, sT, "t")

            @block.scalar
            def _(sc, base_tw=base_tw):
                for mt in range(KT):
                    bk = mt % 4
                    sc.wait_ge(sT, base_tw + mt + 1)
                    m = mk(sc.activation(out=sqb[:, mt, :], in_=pb[bk][0:128, 0:CT],
                                         func=AF.Identity,
                                         bias=bias_sb[:, 4, mt:mt + 1]), sS, "s")
                    bank_mark[bk] = ("s", m)
            m_ao = C["s"]

            if c + 1 < CH:
                emit_ln(x_tm[(c + 1) % 2], xn, 0, c + 1,
                        extra_wait_v=m_hx,
                        extra_wait_qs=m_xmiddma.get(c - 1))
                m_ln1d[c + 1] = C["v"]

            base_t10 = C["t"]
            base_v10 = C["v"]
            @block.tensor
            def _(t, base_v10=base_v10):
                t.wait_ge(sS, m_ao)
                for i in range(16):
                    bk = 4 + i % 2
                    tt, kt = divmod(i, KT)
                    if i < 2:
                        bank_wait(t, bk)
                    else:
                        t.wait_ge(sV, base_v10 + (i - 2) + 1)
                    mk(t.transpose(out=pb[bk][0:128, 0:128],
                                   in_=sqb[:, kt, tt * 128:(tt + 1) * 128],
                                   identity=ident[:]), sT, "t")

            @block.vector
            def _(v, base_t10=base_t10, c=c):
                if c > 0:
                    v.wait_ge(sQS, m_xmiddma[c - 1])
                for i in range(16):
                    bk = 4 + i % 2
                    tt, kt = divmod(i, KT)
                    v.wait_ge(sT, base_t10 + i + 1)
                    m = mk(v.tensor_tensor(
                        out=xm_tm[:, tt, kt * 128:(kt + 1) * 128],
                        in0=pb[bk][0:128, 0:128],
                        in1=xcur[:, tt, kt * 128:(kt + 1) * 128],
                        op=ALU.add), sV, "v")
                    bank_mark[bk] = ("v", m)
            m_xm = C["v"]

            @block.sync
            def _(q, c=c, t0=t0):
                q.wait_ge(sV, m_xm)
                m_xmiddma[c] = mk(
                    q.dma_start(out=x_mid[t0:t0 + CT, :].rearrange("(t p) n -> p t n", p=128),
                                in_=xm_tm[:]), sQS, "qs")
                if dbg:
                    mk(q.dma_start(out=dxmid[t0:t0 + CT, :].rearrange("(t p) n -> p t n", p=128),
                                   in_=xm_tm[:]), sQS, "qs")

            emit_ln(xm_tm, xcur, 12, c, extra_wait_v=m_xm)
            m_h2n = C["v"]

            base_t12 = C["t"]
            base_v12 = C["v"]
            @block.tensor
            def _(t, base_v12=base_v12):
                t.wait_ge(sV, m_h2n)
                for i in range(16):
                    bk = 4 + i % 2
                    tt, kt = divmod(i, KT)
                    if i < 2:
                        bank_wait(t, bk)
                    else:
                        t.wait_ge(sV, base_v12 + (i - 2) + 1)
                    mk(t.transpose(out=pb[bk][0:128, 0:128],
                                   in_=xcur[:, tt, kt * 128:(kt + 1) * 128],
                                   identity=ident[:]), sT, "t")

            @block.vector
            def _(v, base_t12=base_t12):
                for i in range(16):
                    bk = 4 + i % 2
                    tt, kt = divmod(i, KT)
                    v.wait_ge(sT, base_t12 + i + 1)
                    m = mk(v.tensor_copy(out=h2f[:, kt, tt * 128:(tt + 1) * 128],
                                         in_=pb[bk][0:128, 0:128]), sV, "v")
                    bank_mark[bk] = ("v", m)
            m_h2f = C["v"]

            @block.vector
            def _(v):
                mk(v.tensor_copy(out=diffb[:], in_=h2f[:]), sV, "v")
            m_h2b = C["v"]

            @block.sync
            def _(q, c=c, t0=t0):
                q.wait_ge(sV, m_h2b)
                g2, off = divmod(t0, FT)
                m_h2dma[c] = mk(q.dma_start(
                    out=h2_loc[g2 * D:(g2 + 1) * D, :].rearrange(
                        "(kt p) n -> p kt n", p=128)[:, :, off:off + CT],
                    in_=diffb[:]), sQS, "qs")
                if dbg:
                    mk(q.dma_start(
                        out=dh2.rearrange("(kt p) n -> p kt n", p=128)[:, :, t0:t0 + CT],
                        in_=h2f[:]), sQS, "qs")

            @block.tensor
            def _(t):
                t.wait_ge(sV, m_h2f)
                bank_wait(t, 6)
                last = None
                for kt in range(KT):
                    last = t.matmul(out=pb[6][0:8, 0:CT], lhsT=gate_sb[:, kt, :],
                                    rhs=h2f[:, kt, :], start=(kt == 0), stop=(kt == KT - 1))
                mk(last, sT, "t")

            @block.vector
            def _(v):
                v.wait_ge(sT, C["t"])
                m = mk(v.tensor_copy(out=lgs[:], in_=pb[6][0:8, 0:CT]), sV, "v")
                bank_mark[6] = ("v", m)
            m_lgs = C["v"]

            for tt in range(2):
                bk = 4 + tt
                @block.tensor
                def _(t, tt=tt, bk=bk):
                    t.wait_ge(sV, m_lgs)
                    bank_wait(t, bk)
                    mk(t.transpose(out=pb[bk][0:128, 0:8],
                                   in_=lgs[0:8, tt * 128:(tt + 1) * 128],
                                   identity=ident[0:8, 0:8]), sT, "t")
            m_ltr = C["t"]

            @block.vector
            def _(v, m_ltr=m_ltr):
                v.wait_ge(sT, m_ltr)
                for tt in range(2):
                    mk(v.tensor_reduce(out=r_sm[:, tt, 0:1], in_=pb[4 + tt][0:128, 0:8],
                                       axis=AX, op=ALU.max, negate=True), sV, "v")

            @block.scalar
            def _(sc, m_ltr=m_ltr):
                sc.wait_ge(sV, C["v"])
                sc.wait_ge(sT, m_ltr)
                for tt in range(2):
                    sc.copy(ltm_s[:, tt, :], pb[4 + tt][0:128, 0:8])
                    mk(sc.mul(r_sm[:, tt, 1:2], r_sm[:, tt, 0:1], -1.0), sS, "s")
                    m = mk(sc.activation(out=edmp[:, tt, :], in_=pb[4 + tt][0:128, 0:8],
                                         func=AF.Exp, bias=r_sm[:, tt, 0:1],
                                         accum_out=r_sm[:, tt, 2:3]), sS, "s")
                    bank_mark[4 + tt] = ("s", m)

            @block.vector
            def _(v):
                v.wait_ge(sS, C["s"])
                for tt in range(2):
                    mk(v.reciprocal(out=r_sm[:, tt, 3:4], in_=r_sm[:, tt, 2:3]), sV, "v")

            @block.scalar
            def _(sc):
                sc.wait_ge(sV, C["v"])
                for tt in range(2):
                    mk(sc.copy(r_sm[:, tt, 4:5], r_sm[:, tt, 3:4]), sS, "s")

            @block.vector
            def _(v):
                v.wait_ge(sS, C["s"])
                for tt in range(2):
                    mk(v.tensor_scalar(out=scale_tm[:, tt, :], in0=ltm_s[:, tt, :],
                                       scalar1=r_sm[:, tt, 1:2],
                                       scalar2=r_sm[:, tt, 4:5],
                                       op0=ALU.is_equal, op1=ALU.mult), sV, "v")
            m_sctm = C["v"]

            @block.tensor
            def _(t, m_sctm=m_sctm):
                t.wait_ge(sV, m_sctm)
                bank_wait(t, 6)
                for tt in range(2):
                    mk(t.transpose(out=pb[6][0:8, tt * 128:(tt + 1) * 128],
                                   in_=scale_tm[:, tt, :],
                                   identity=ident[:]), sT, "t")

            @block.vector
            def _(v, c=c):
                v.wait_ge(sT, C["t"])
                if c > 0:
                    v.wait_ge(sNR, m_scdma[c - 1])
                m = mk(v.tensor_copy(out=scaleT[:], in_=pb[6][0:8, 0:256]), sV, "v")
                bank_mark[6] = ("v", m)

            m_scT = C["v"]

            @block.sync
            def _(q, c=c, t0=t0):
                q.wait_ge(sV, m_scT)
                m_scdma[c] = mk(q.dma_start(out=sc_loc[:, t0:t0 + CT], in_=scaleT[:]),
                                sNR, "nr")
                if dbg:
                    mk(q.dma_start(out=dscale[:, t0:t0 + CT], in_=scaleT[:]), sQS, "qs")

            m_xlast[c] = C["v"]

            if (c % (FT // CT)) == (FT // CT) - 1:
                g2 = c // (FT // CT)
                @block.gpsimd
                def _(g, g2=g2, c=c):
                    g.wait_ge(sQS, m_h2dma[c])
                    with allow_ici_all_to_all():
                        mk(g.collective_compute(
                            "AllGather", ALU.bypass, replica_groups=RG,
                            ins=[h2_loc[g2 * D:(g2 + 1) * D, :].opt()],
                            outs=[h2_all[g2 * NB * D:(g2 + 1) * NB * D, :].opt()]),
                           sCC, "cc")

        @block.gpsimd
        def _(g):
            g.wait_ge(sQS, C["qs"])
            g.wait_ge(sNR, C["nr"])
            with allow_ici_all_to_all():
                mk(g.collective_compute("AllToAll", ALU.bypass, replica_groups=RG,
                                        ins=[sc_loc.opt()], outs=[sc_c.opt()]),
                   sCC, "cc")
        m_ag = C["cc"]

        attn_done_v, attn_done_s, attn_done_t = C["v"], C["s"], C["t"]
        attn_done_g = C["g"]
        aes.close()

        fes = contextlib.ExitStack()
        TF = lambda name, shape, dt: fes.enter_context(nc.sbuf_tensor(name, shape, dt))
        w1b = TF("w1b", [128, KT, DFF], BF16)
        w2b = TF("w2b", [128, 32, D], BF16)
        h2s = TF("h2s", [128, KT, FT], BF16)
        h_sub = TF("h_sub", [128, 32, FT], BF16)
        ych = TF("ych", [128, 2, D], F32)
        sct = TF("sct", [128, 2, 4], F32)
        b1sb = TF("b1sb", [128, 32], F32)
        b2st = TF("b2st", [1, D], F32)
        scrow = b2st[0:1, 0:FT]
        b2row = TF("b2row", [1, D], BF16)
        onesb = TF("onesb", [1, 128], BF16)
        xmt = TF("xmt", [128, D], F32)
        yrt = TF("yrt", [128, D], F32)
        ot = TF("ot", [128, D], F32)
        fstg = TF("fstg", [128, 3, D], F32)

        @block.sync
        def _(q):
            q.wait_ge(sV, attn_done_v)
            q.wait_ge(sS, attn_done_s)
            q.wait_ge(sT, attn_done_t)
            q.wait_ge(sG, attn_done_g)
            mk(q.dma_start(out=b1sb[:], in_=b1_in), sQS, "qs")
            mk(q.dma_start(out=b2st[:], in_=b2_in[None, :]), sNR, "nr")
        m_fb = C["qs"]
        m_b2d = C["nr"]

        @block.gpsimd
        def _(g):
            g.wait_ge(sQS, m_fb)
            mk(g.memset(onesb[:], 1.0), sG, "g")
        m_ones = C["g"]

        @block.scalar
        def _(sc):
            sc.wait_ge(sNR, m_b2d)
            mk(sc.copy(b2row[:], b2st[:]), sS, "s")
        m_b2 = C["s"]

        base_wv = C["v"]

        def wsrc(i):
            if i < 32:
                return w1_in[i * 128:(i + 1) * 128, :]
            return w2_in[(i - 32) * 128:(i - 31) * 128, :]

        def wdst(i):
            if i < 32:
                kt, qr = divmod(i, 4)
                return w1b[:, kt, qr * D:(qr + 1) * D]
            return w2b[:, i - 32, :]

        @block.sync
        def _(q):
            q.wait_ge(sCC, m_ag)
            for i in range(64):
                if i >= 3:
                    q.wait_ge(sV, base_wv + (i - 2))
                mk(q.dma_start(out=fstg[:, i % 3, :], in_=wsrc(i)), sQS, "qs")
            for e in range(2):
                q.wait_ge(sV, base_wv + 62 + e)
                mk(q.dma_start(out=fstg[:, (64 + e) % 3, :], in_=w1_in[0:128, :]),
                   sQS, "qs")
        base_wq = C["qs"] - 66 * 16

        @block.vector
        def _(v):
            for i in range(64):
                v.wait_ge(sQS, base_wq + (i + 3) * 16)
                mk(v.tensor_copy(out=wdst(i), in_=fstg[:, i % 3, :]), sV, "v")
        m_wv = C["v"]
        m_w1v = base_wv + 32

        m_h2s = {}
        m_sct = {}
        m_sctr = {-1: 0}
        m_A_last = {-1: 0}
        m_ydma = {}
        m_ytsd = {}
        m_rs = {}
        fbank = {i: ("v", 0) for i in range(8)}

        def fbank_wait(eng, i):
            nm, val = fbank[i]
            if val > 0:
                eng.wait_ge(SEM[nm], val)

        m_oadd = [0]
        m_odma = [0]

        def emit_final(gidx):
            for tm in range(FT // 128):
                row = gidx * FT + tm * 128
                @block.gpsimd
                def _(g, row=row, gidx=gidx):
                    g.wait_ge(sCC, m_rs[gidx])
                    if m_oadd[0] > 0:
                        g.wait_ge(sV, m_oadd[0])
                    mk(g.dma_start(out=yrt[:], in_=y_red[row:row + 128, :]), sQG, "qg")
                    mk(g.dma_start(out=xmt[:], in_=x_mid[row:row + 128, :]), sQG, "qg")

                @block.vector
                def _(v, row=row):
                    v.wait_ge(sQG, C["qg"])
                    if m_odma[0] > 0:
                        v.wait_ge(sSO, m_odma[0])
                    m_oadd[0] = mk(v.tensor_tensor(out=ot[:], in0=yrt[:], in1=xmt[:],
                                                   op=ALU.add), sV, "v")

                @block.gpsimd
                def _(g, row=row):
                    g.wait_ge(sV, m_oadd[0])
                    m_odma[0] = mk(g.dma_start(out=out_t[row:row + 128, :], in_=ot[:]),
                                   sSO, "so")

        def emit_rs(gidx):
            @block.gpsimd
            def _(g, gidx=gidx):
                for cc2 in range(gidx * NB, (gidx + 1) * NB):
                    g.wait_ge(sQS, m_ydma[(cc2, 1)])
                base = gidx * NB * FT
                m_rs[gidx] = mk(g.collective_compute(
                    "ReduceScatter", ALU.add, replica_groups=RG,
                    ins=[y_loc[base:base + NB * FT, :].opt()],
                    outs=[y_red[gidx * FT:(gidx + 1) * FT, :].opt()]), sCC, "cc")

        chunk = 0
        for gidx in range(NG):
            for b in range(NB):
                t0 = gidx * FT
                if b == 4 and gidx > 0:
                    emit_final(gidx - 1)
                @block.gpsimd
                def _(g, gidx=gidx, b=b, chunk=chunk, t0=t0):
                    g.wait_ge(sCC, m_ag)
                    g.wait_ge(sT, m_A_last[chunk - 1])
                    base = (gidx * NB + b) * D
                    m_h2s[chunk] = mk(g.dma_start(
                        out=h2s[:],
                        in_=h2_all[base:base + D, :].rearrange("(kt p) n -> p kt n", p=128)),
                        sQG, "qg")
                    g.wait_ge(sT, m_sctr[chunk - 1])
                    if chunk == 0:
                        g.wait_ge(sS, m_b2)
                    m_sct[chunk] = mk(g.dma_start(
                        out=scrow, in_=sc_c[b:b + 1, t0:t0 + FT]), sNR, "nr")
                if b == 0 and gidx > 0:
                    emit_rs(gidx - 1)

                base_t = C["t"]
                base_s = C["s"]
                @block.tensor
                def _(t, chunk=chunk, base_s=base_s):
                    t.wait_ge(sQG, m_h2s[chunk])
                    if chunk == 0:
                        t.wait_ge(sV, m_w1v)
                        t.wait_ge(sG, m_ones)
                    for j in range(32):
                        bk = j % 4
                        if j < 4:
                            fbank_wait(t, bk)
                        else:
                            t.wait_ge(sS, base_s + (j - 4) + 1)
                        last = None
                        for kt in range(KT):
                            last = t.matmul(out=pb[bk][0:128, 0:FT],
                                            lhsT=w1b[:, kt, j * 128:(j + 1) * 128],
                                            rhs=h2s[:, kt, :],
                                            start=(kt == 0), stop=(kt == KT - 1))
                        mk(last, sT, "t")

                @block.scalar
                def _(sc, base_t=base_t):
                    for j in range(32):
                        bk = j % 4
                        sc.wait_ge(sT, base_t + j + 1)
                        m = mk(sc.activation(out=h_sub[:, j, :], in_=pb[bk][0:128, 0:FT],
                                             func=AF.Gelu, bias=b1sb[:, j:j + 1]), sS, "s")
                        fbank[bk] = ("s", m)
                m_A_last[chunk] = C["t"]
                m_gelu_all = C["s"]

                @block.tensor
                def _(t, chunk=chunk):
                    t.wait_ge(sNR, m_sct[chunk])
                    fbank_wait(t, 0)
                    last = None
                    for tm in range(4):
                        last = t.transpose(out=pb[0][0:128, tm:tm + 1],
                                           in_=b2st[0:1, tm * 128:(tm + 1) * 128],
                                           identity=ident[0:1, 0:1])
                    m_sctr[chunk] = mk(last, sT, "t")

                @block.scalar
                def _(sc, chunk=chunk):
                    sc.wait_ge(sT, m_sctr[chunk])
                    if chunk >= 2:
                        sc.wait_ge(sV, m_ytsd[chunk - 2])
                    m = mk(sc.copy(sct[:, chunk % 2, :], pb[0][0:128, 0:4]), sS, "s")
                    fbank[0] = ("s", m)
                m_sctev = C["s"]

                base_t2 = C["t"]
                base_v2 = C["v"]
                @block.tensor
                def _(t, m_gelu_all=m_gelu_all, base_v2=base_v2, chunk=chunk):
                    t.wait_ge(sS, m_gelu_all)
                    t.wait_ge(sS, m_b2)
                    if chunk == 0:
                        t.wait_ge(sV, m_wv)
                    for q2 in range(8):
                        tm, dc = divmod(q2, 2)
                        bk = 4 + q2 % 4
                        if q2 < 4:
                            fbank_wait(t, bk)
                        else:
                            t.wait_ge(sV, base_v2 + (q2 - 4) + 1)
                        for j in range(32):
                            t.matmul(out=pb[bk][0:128, 0:FT],
                                     lhsT=h_sub[:, j, tm * 128:(tm + 1) * 128],
                                     rhs=w2b[:, j, dc * 512:(dc + 1) * 512],
                                     start=(j == 0), stop=False)
                        mk(t.matmul(out=pb[bk][0:128, 0:FT],
                                    lhsT=onesb[:],
                                    rhs=b2row[0:1, dc * 512:(dc + 1) * 512],
                                    start=False, stop=True), sT, "t")

                for half in range(2):
                    @block.vector
                    def _(v, half=half, chunk=chunk, m_sctev=m_sctev,
                          base_t2=base_t2):
                        for q2 in range(half * 4, half * 4 + 4):
                            tm, dc = divmod(q2, 2)
                            bk = 4 + q2 % 4
                            v.wait_ge(sT, base_t2 + q2 + 1)
                            if q2 == 0:
                                v.wait_ge(sS, m_sctev)
                                if chunk >= 1:
                                    v.wait_ge(sQS, m_ydma[(chunk - 1, 1)])
                            elif q2 == 4:
                                v.wait_ge(sQS, m_ydma[(chunk, 0)])
                            m = mk(v.tensor_scalar(
                                out=ych[:, tm % 2, dc * 512:(dc + 1) * 512],
                                in0=pb[bk][0:128, 0:FT],
                                scalar1=sct[:, chunk % 2, tm:tm + 1], scalar2=None,
                                op0=ALU.mult), sV, "v")
                            fbank[bk] = ("v", m)

                    @block.sync
                    def _(q, gidx=gidx, b=b, chunk=chunk, half=half):
                        q.wait_ge(sV, C["v"])
                        base = (gidx * NB + b) * FT + half * 256
                        m_ydma[(chunk, half)] = mk(q.dma_start(
                            out=y_loc[base:base + 256, :].rearrange(
                                "(t p) n -> p t n", p=128),
                            in_=ych[:]), sQS, "qs")
                m_ytsd[chunk] = C["v"]
                chunk += 1

        emit_rs(NG - 1)
        emit_final(NG - 1)

        if dbg:
            @block.sync
            def _(q):
                q.wait_ge(sSO, C["so"])
                mk(q.dma_start(out=dyloc, in_=y_loc), sQS, "qs")
                mk(q.dma_start(out=dyred, in_=y_red), sQS, "qs")
                mk(q.dma_start(out=dw1b, in_=w1b[:].rearrange("p a b -> p (a b)")), sQS, "qs")
                mk(q.dma_start(out=dw2b, in_=w2b[:].rearrange("p a b -> p (a b)")), sQS, "qs")
                mk(q.dma_start(out=db1, in_=b1sb[:]), sQS, "qs")
                mk(q.dma_start(out=db2, in_=b2row[:]), sQS, "qs")
                mk(q.dma_start(out=dhsub, in_=h_sub[:].rearrange("p a b -> p (a b)")), sQS, "qs")

        @block.gpsimd
        def _(g):
            g.wait_ge(sSO, C["so"])
            g.wait_ge(sQS, C["qs"])
            g.wait_ge(sQG, C["qg"])
            g.wait_ge(sCC, C["cc"])

        fes.close()
    return nc


def make_in_maps(inputs, N):
    CH = N // CT
    x = np.asarray(inputs["L0_attn_x"])
    vec = lambda v: np.ascontiguousarray(
        np.asarray(v, np.float32).reshape(KT, 128).T)
    maps = []
    for c in range(NB):
        xt = np.ascontiguousarray(
            x[c, :N].reshape(CH, 2, 128, D).transpose(0, 2, 1, 3).reshape(CH * 128, 2 * D))
        w1t = np.ascontiguousarray(
            np.asarray(inputs["L0_ffn_w1"])[c].reshape(KT, 128, 4, D)
            .transpose(0, 2, 1, 3).reshape(32 * 128, D))
        m = dict(
            x=xt,
            xlast=np.ascontiguousarray(x[c, N - 1:N]),
            Wg=inputs["L0_attn_Wg_w"], bg=vec(inputs["L0_attn_Wg_b"]),
            Wr=inputs["L0_attn_Wr_w"], br=vec(inputs["L0_attn_Wr_b"]),
            Wk=inputs["L0_attn_Wk_w"], bk=vec(inputs["L0_attn_Wk_b"]),
            Wv=inputs["L0_attn_Wv_w"], bv=vec(inputs["L0_attn_Wv_b"]),
            Wo=inputs["L0_attn_Wo_w"], bo=vec(inputs["L0_attn_Wo_b"]),
            mg=vec(inputs["L0_attn_mg"]), mr=vec(inputs["L0_attn_mr"]),
            mk=vec(inputs["L0_attn_mk"]), mv=vec(inputs["L0_attn_mv"]),
            wdec=vec(np.asarray(inputs["L0_attn_w"]).reshape(-1)),
            uvec=vec(np.asarray(inputs["L0_attn_u"]).reshape(-1)),
            gate=np.ascontiguousarray(
                np.asarray(inputs["L0_ffn_gate_w"], np.float32)
                .reshape(KT, 128, NEXP).transpose(1, 0, 2)),
            w1e=w1t,
            b1e=np.ascontiguousarray(
                np.asarray(inputs["L0_ffn_b1"])[c].reshape(32, 128).T),
            w2e=np.asarray(inputs["L0_ffn_w2"])[c],
            b2e=np.asarray(inputs["L0_ffn_b2"])[c],
        )
        maps.append({k: np.ascontiguousarray(np.asarray(v), dtype=np.float32)
                     for k, v in m.items()})
    return maps

def kernel(**inputs):
    nc = build_program(N=4096, dbg=False)
    in_maps = make_in_maps(inputs, 4096)
    res = run_bass_kernel_spmd(nc, in_maps, core_ids=list(range(8)))
    return np.stack([np.asarray(res.results[c]["out"]) for c in range(8)])
